# version 17
# baseline (speedup 1.0000x reference)
import contextlib
import numpy as np
import ml_dtypes
import concourse.bass as bass
import concourse.mybir as mybir
from concourse.alu_op_type import AluOpType
from concourse.bass_utils import run_bass_kernel_spmd

F32 = mybir.dt.float32
I32 = mybir.dt.int32
BF16 = mybir.dt.bfloat16
AF = mybir.ActivationFunctionType
AX = mybir.AxisListType
OP = AluOpType

BNEG = -30000.0
BBND = 50.0

D = 2048
ND = 16
H = 32
NPAIR = 16
NQ = 1024
NKV = 2048
INTER = 1408
NI = 11
NE = 16
SH_INTER = 2816

_BF = ml_dtypes.bfloat16


class Cnt:
    def __init__(self, nc, name):
        self.h = nc.alloc_semaphore(name)
        self.n = 0


class KB:

    def __init__(self, simcheck=False):
        self.nc = bass.Bass("TRN2", target_bir_lowering=False,
                            detect_race_conditions=simcheck)
        self.events = {k: [] for k in ("sync", "gpsimd", "tensor", "vector", "scalar")}
        self.simcheck = simcheck
        self.selfsem = {}

    def sem(self, name):
        return Cnt(self.nc, name)

    def ev(self, eng, fn, waits=(), inc=None, amt=1):
        waits = list(waits)
        if eng in self.selfsem and fn is not None:
            ss = self.selfsem[eng]
            waits.append((ss, ss.n))
            if inc is None:
                inc = ss
        w = tuple((s.h, int(v)) for (s, v) in waits if s is not None and v > 0)
        ih = None
        if inc is not None:
            inc.n += amt
            ih = (inc.h, amt)
        self.events[eng].append((w, fn, ih))

    def replay(self):
        nc = self.nc
        with nc.Block() as blk:
            for name, deco in (("sync", blk.sync), ("gpsimd", blk.gpsimd),
                               ("tensor", blk.tensor), ("vector", blk.vector),
                               ("scalar", blk.scalar)):
                evs = self.events[name]

                def make(evs):
                    def body(eng):
                        for w, fn, ih in evs:
                            for h, v in w:
                                eng.wait_ge(h, v)
                            inst = fn(eng) if fn is not None else None
                            if ih is not None:
                                inst.then_inc(*ih)
                    return body

                deco(make(evs))


def build_program(simcheck=False, dbg=False, gtap=0):
    kb = KB(simcheck)
    nc = kb.nc
    E = kb.ev

    x_d = nc.dram_tensor("x_all", [NKV, D], F32, kind="ExternalInput")
    wqkv_d = nc.dram_tensor("wqkv_p", [48, 128, 2048], F32, kind="ExternalInput")
    wo_d = nc.dram_tensor("woT", [D, D], F32, kind="ExternalInput")
    wf_d = nc.dram_tensor("wfT", [D, H], F32, kind="ExternalInput")
    bf_d = nc.dram_tensor("bf_col", [H, 1], F32, kind="ExternalInput")
    wg_d = nc.dram_tensor("wgT", [D, NE], F32, kind="ExternalInput")
    w13h_d = nc.dram_tensor("w13h", [(NE + 2) * NI, 128, 4096], F32, kind="ExternalInput")
    w2_d = nc.dram_tensor("w2T", [NE, INTER, D], F32, kind="ExternalInput")
    ws2_d = nc.dram_tensor("ws2T", [SH_INTER, D], F32, kind="ExternalInput")
    trit_d = nc.dram_tensor("trit", [16, 128, 2048], F32, kind="ExternalInput")
    odiag_d = nc.dram_tensor("odiag", [128, 128], BF16, kind="ExternalInput")
    trium_d = nc.dram_tensor("trium", [128, 128], BF16, kind="ExternalInput")
    identb_d = nc.dram_tensor("identb", [128, 128], BF16, kind="ExternalInput")
    identf_d = nc.dram_tensor("identf", [128, 128], F32, kind="ExternalInput")
    ones_d = nc.dram_tensor("ones_rows", [3, NKV], BF16, kind="ExternalInput")
    out_d = nc.dram_tensor("out", [NQ, D], F32, kind="ExternalOutput")
    if dbg:
        dxn_d = nc.dram_tensor("dxn", [128, ND * NKV], BF16, kind="ExternalOutput")
        dnegc_d = nc.dram_tensor("dnegc", [H, NKV], F32, kind="ExternalOutput")
        dcbf_d = nc.dram_tensor("dcbf", [3 * H, NKV], BF16, kind="ExternalOutput")
        dnegm_d = nc.dram_tensor("dnegm", [H, NQ], BF16, kind="ExternalOutput")
        dcomb_d = nc.dram_tensor("dcomb", [128, 128], F32, kind="ExternalOutput")
        dpt_d = nc.dram_tensor("dpt", [128, 1152], BF16, kind="ExternalOutput")
        datm_d = nc.dram_tensor("datm", [128, 64], F32, kind="ExternalOutput")
        dafm_d = nc.dram_tensor("dafm", [128, 8 * NQ], BF16, kind="ExternalOutput")
        dacc_d = nc.dram_tensor("dacc", [128, 8 * D], F32, kind="ExternalOutput")
        dvau_d = nc.dram_tensor("dvau", [128, 16 * 130], BF16, kind="ExternalOutput")
        s_dbg = kb.sem("s_dbg")
        s_dbg2 = kb.sem("s_dbg2")
        s_dbg3 = kb.sem("s_dbg3")

    SA = nc.alloc_psum_tensor("SA", [128, 2048], F32)
    SB = nc.alloc_psum_tensor("SB", [128, 2048], F32)
    rot512 = [SA[:, 1024:1536], SA[:, 1536:2048], SB[:, 1024:1536], SB[:, 1536:2048]]
    rotg = [(None, 0)] * 4

    acc = nc.alloc_sbuf_tensor("acc", [128, 8 * D], F32)
    xn = nc.alloc_sbuf_tensor("xn_fm", [128, ND * NKV], BF16)
    comb = nc.alloc_sbuf_tensor("comb", [128, 8 * 16], F32)
    trium = nc.alloc_sbuf_tensor("trium_s", [128, 128], BF16)
    identb = nc.alloc_sbuf_tensor("identb_s", [128, 128], BF16)
    identf = nc.alloc_sbuf_tensor("identf_s", [128, 128], F32)
    wg_sb = nc.alloc_sbuf_tensor("wg_sb", [128, ND * NE], BF16)
    wf_sb = nc.alloc_sbuf_tensor("wf_sb", [128, ND * H], BF16)
    bf_sb = nc.alloc_sbuf_tensor("bf_sb", [H, 1], F32)
    odiag = nc.alloc_sbuf_tensor("odiag_s", [128, 128], BF16)
    cols = nc.alloc_sbuf_tensor("cols", [128, 16], F32)
    colsq = nc.alloc_sbuf_tensor("colsq", [128, 16], F32)
    neg4 = nc.alloc_sbuf_tensor("neg4", [128, 16], F32)
    epst = nc.alloc_sbuf_tensor("epst", [128, 1], F32)
    nc.const_aps.register(F32, 1e-5, epst[:])
    sAB = contextlib.ExitStack()
    chi = sAB.enter_context(nc.sbuf_tensor("chi", [H, NKV], BF16))
    cmid = sAB.enter_context(nc.sbuf_tensor("cmid", [H, NKV], BF16))
    clo = sAB.enter_context(nc.sbuf_tensor("clo", [H, NKV], BF16))
    negm = sAB.enter_context(nc.sbuf_tensor("negm", [H, NQ], BF16))
    negc = sAB.enter_context(nc.sbuf_tensor("negc", [H, NKV], F32))

    s_pe = kb.sem("s_pe")
    s_dve = kb.sem("s_dve")
    s_act = kb.sem("s_act")
    kb.selfsem = {"vector": s_dve, "scalar": s_act}
    s_xa = kb.sem("s_xa")
    s_xt = [kb.sem(f"s_xt{i}") for i in range(2)]
    s_wc = kb.sem("s_wc")
    s_xc = kb.sem("s_xc")
    s_tr = [kb.sem(f"s_tr{i}") for i in range(2)]
    s_wp = kb.sem("s_wp")
    s_wo = [kb.sem(f"s_wo{i}") for i in range(12)]
    s_wou = kb.sem("s_wou")
    s_cr = kb.sem("s_cr")
    s_w13 = [kb.sem(f"s_w13_{i}") for i in range(2)]
    s_w2s = [[kb.sem(f"s_w2s_{i}_{pb}") for pb in range(2)] for i in range(NI)]
    s_w2u = kb.sem("s_w2u")
    s_out = kb.sem("s_out")

    def dma(dst, src):
        return lambda eng, d=dst, s=src: eng.dma_start(d, s)

    E("vector", lambda e: e.memset(neg4[:], -4.0))
    E("vector", lambda e: e.memset(epst[:], 1e-5), inc=s_dve)
    EPS_SET = s_dve.n
    E("gpsimd", dma(trium[:], trium_d[:]), inc=s_wc, amt=16)
    E("gpsimd", dma(odiag[:], odiag_d[:]), inc=s_wc, amt=16)
    E("gpsimd", dma(identb[:], identb_d[:]), inc=s_wc, amt=16)
    E("gpsimd", dma(wg_sb.rearrange("q (c g) -> q c g", g=NE),
                    wg_d.rearrange("(c q) g -> q c g", q=128)), inc=s_wc, amt=16)
    E("gpsimd", dma(wf_sb.rearrange("q (c h) -> q c h", h=H),
                    wf_d.rearrange("(c q) h -> q c h", q=128)), inc=s_wc, amt=16)
    WC_INIT = s_wc.n
    E("sync", dma(identf[:], identf_d[:]), inc=s_xc, amt=16)
    E("sync", dma(bf_sb[:], bf_d[:]), inc=s_xc, amt=16)
    XC_ALL = s_xc.n
    for tt in range(8):
        E("sync", dma(acc[:, tt * D:(tt + 1) * D], x_d[(8 + tt) * 128:(9 + tt) * 128, :]),
          inc=s_xa, amt=16)
    XA_ALL = s_xa.n

    with contextlib.ExitStack() as sA:
        tritp = [sA.enter_context(nc.sbuf_tensor(f"tritp{i}", [128, ND * 128], F32))
                 for i in range(2)]
        xn_tm = sA.enter_context(nc.sbuf_tensor("xn_tm", [128, D], F32))
        u_fm = sA.enter_context(nc.sbuf_tensor("u_fm", [H, NKV], F32))
        wa = sA.enter_context(nc.sbuf_tensor("wa", [H, NKV], F32))
        wb = sA.enter_context(nc.sbuf_tensor("wb", [H, NKV], F32))
        nlt = sA.enter_context(nc.sbuf_tensor("nlt", [128, ND * H], F32))
        sct = [sA.enter_context(nc.sbuf_tensor(f"sct{i}", [128, H], F32))
               for i in range(2)]
        rsc = sA.enter_context(nc.sbuf_tensor("rsc", [128, 120], F32))
        lt4 = sA.enter_context(nc.sbuf_tensor("lt4", [128, 4], I32))
        lt16 = sA.enter_context(nc.sbuf_tensor("lt16", [128, 16], I32))

        m_sq = [0] * 16
        m_xn = [0] * 16
        m_tp = [0] * 16
        m_var = [0] * 16
        m_rn = [0] * 16
        rot_i = 0
        xnsrc = [None] * 16
        rstd16 = [None] * 16
        mean16 = [None] * 16

        def ln_stats(tt):
            par = tt % 2
            if tt < 8:
                w = []
                if tt >= 2:
                    w = [(s_act, m_sq[tt - 2]), (s_dve, m_xn[tt - 2])]
                E("sync", dma(tritp[par][:], x_d[tt * 128:(tt + 1) * 128, :]),
                  waits=w, inc=s_xt[par], amt=16)
                src = tritp[par][:]
                ldw = [(s_xt[par], s_xt[par].n)]
            else:
                src = acc[:, (tt - 8) * D:(tt - 7) * D]
                ldw = [(s_xa, XA_ALL)]
            xnsrc[tt] = src
            sumc = cols[:, 8 * par + 0: 8 * par + 1]
            meanc = cols[:, 8 * par + 2: 8 * par + 3]
            msqc = cols[:, 8 * par + 3: 8 * par + 4]
            varc = cols[:, 8 * par + 4: 8 * par + 5]
            lnvc = cols[:, 8 * par + 5: 8 * par + 6]
            rstdc = cols[:, 8 * par + 6: 8 * par + 7]
            mean16[tt] = meanc
            rstd16[tt] = rstdc
            sq4 = [colsq[:, 8 * par + j: 8 * par + j + 1] for j in range(4)]
            s01 = colsq[:, 8 * par + 4: 8 * par + 5]
            s23 = colsq[:, 8 * par + 5: 8 * par + 6]
            sqs = colsq[:, 8 * par + 6: 8 * par + 7]

            aw = list(ldw)
            if tt >= 2:
                aw += [(s_dve, m_var[tt - 2])]
            for j in range(4):
                E("scalar", lambda e, s=src, j=j, a=sq4[j]: e.activation(
                    nlt[:, 0:512], s[:, j * 512:(j + 1) * 512], AF.Square, accum_out=a),
                  waits=(aw if j == 0 else ()), inc=(s_act if j == 3 else None))
            m_sq[tt] = s_act.n
            E("vector", lambda e, s=src, o=sumc: e.reduce_sum(o, s, axis=AX.X), waits=ldw)
            E("vector", lambda e, i=sumc, o=meanc: e.tensor_scalar(o, i, 1.0 / D, None, op0=OP.mult))
            E("vector", lambda e, i=meanc, o=msqc: e.tensor_tensor(o, i, i, op=OP.mult))
            E("vector", lambda e, a=sq4[0], b=sq4[1], o=s01: e.tensor_tensor(o, a, b, op=OP.add),
              waits=[(s_act, m_sq[tt])])
            E("vector", lambda e, a=sq4[2], b=sq4[3], o=s23: e.tensor_tensor(o, a, b, op=OP.add))
            E("vector", lambda e, a=s01, b=s23, o=sqs: e.tensor_tensor(o, a, b, op=OP.add))
            E("vector", lambda e, i=sqs, m=msqc, o=varc: e.scalar_tensor_tensor(
                o, i, 1.0 / D, m, op0=OP.mult, op1=OP.subtract), inc=s_dve)
            m_var[tt] = s_dve.n
            E("scalar", lambda e, i=varc, o=lnvc: e.activation(o, i, AF.Ln, bias=1e-5),
              waits=[(s_dve, m_var[tt]), (s_dve, EPS_SET)])
            E("scalar", lambda e, i=lnvc, o=rstdc: e.activation(o, i, AF.Exp, scale=-0.5),
              inc=s_act)
            m_rn[tt] = s_act.n

        def ln_xn(tt):
            xw = [(s_act, m_rn[tt])]
            if tt >= 1:
                xw += [(s_pe, m_tp[tt - 1])]
            E("vector", lambda e, s=xnsrc[tt], m=mean16[tt], r=rstd16[tt]: e.tensor_scalar(
                xn_tm[:], s, m, r, op0=OP.subtract, op1=OP.mult),
              waits=xw, inc=s_dve)
            m_xn[tt] = s_dve.n

        def ln_tp(tt):
            nonlocal rot_i
            xnr = xn.rearrange("p (c k) -> p c k", k=NKV)
            for gblk in range(4):
                r = rot_i % 4
                for j in range(4):
                    c = 4 * gblk + j
                    E("tensor", lambda e, c=c, r=r, j=j: e.transpose(
                        rot512[r][:, j * 128:(j + 1) * 128],
                        xn_tm[:, c * 128:(c + 1) * 128], identf[:]),
                      waits=([(s_dve, m_xn[tt]), rotg[r], (s_xc, XC_ALL)]
                             if j == 0 else ()),
                      inc=(s_pe if j == 3 else None))
                pn = s_pe.n
                E("scalar", lambda e, gblk=gblk, r=r, tt=tt: e.copy(
                    xnr[:, 4 * gblk:4 * gblk + 4, tt * 128:(tt + 1) * 128],
                    rot512[r][:, 0:512].rearrange("p (c k) -> p c k", k=128)),
                  waits=[(s_pe, pn)], inc=s_act)
                rotg[r] = (s_act, s_act.n)
                rot_i += 1
            m_tp[tt] = s_pe.n

        for tt in range(16):
            ln_stats(tt)
            if tt >= 1:
                ln_tp(tt - 1)
            ln_xn(tt)
        ln_tp(15)
        LN_ACT_DONE = s_act.n
        LN_PE_LAST = s_pe.n
        LN_XN_DVE = m_xn[15]

        for kc in range(4):
            r = rot_i % 4
            for c in range(ND):
                E("tensor", lambda e, c=c, kc=kc, r=r: e.matmul(
                    rot512[r][0:H, :], wf_sb[:, c * H:(c + 1) * H],
                    xn[:, c * NKV + kc * 512: c * NKV + (kc + 1) * 512],
                    start=(c == 0), stop=(c == ND - 1)),
                  waits=([(s_act, LN_ACT_DONE), (s_wc, WC_INIT), rotg[r]] if c == 0 else ()),
                  inc=(s_pe if c == ND - 1 else None))
            pn = s_pe.n
            E("vector", lambda e, r=r, kc=kc: e.tensor_scalar(
                u_fm[:, kc * 512:(kc + 1) * 512], rot512[r][0:H, :], bf_sb[:],
                None, op0=OP.add),
              waits=[(s_pe, pn), (s_xc, XC_ALL)], inc=s_dve)
            rotg[r] = (s_dve, s_dve.n)
            rot_i += 1
        un = s_dve.n
        E("scalar", lambda e: e.activation(wa[:], u_fm[:], AF.Abs), waits=[(s_dve, un)])
        E("scalar", lambda e: e.activation(wb[:], wa[:], AF.Exp, scale=-1.0))
        E("scalar", lambda e: e.activation(wa[:], wb[:], AF.Ln, bias=1.0), inc=s_act)
        an = s_act.n
        E("vector", lambda e: e.tensor_scalar(wb[:], u_fm[:], 0.0, None, op0=OP.min),
          waits=[(s_act, an)])
        E("vector", lambda e: e.tensor_tensor(u_fm[:], wa[:], wb[:], op=OP.subtract),
          inc=s_dve)
        nlogf_n = s_dve.n

        for t2 in range(ND):
            r = rot_i % 4
            E("tensor", lambda e, t2=t2, r=r: e.transpose(
                rot512[r][0:128, 0:H], u_fm[0:H, t2 * 128:(t2 + 1) * 128],
                identf[0:H, 0:H]),
              waits=[(s_dve, nlogf_n), rotg[r]], inc=s_pe)
            pn = s_pe.n
            E("scalar", lambda e, t2=t2, r=r: e.copy(nlt[:, t2 * H:(t2 + 1) * H],
                                                     rot512[r][0:128, 0:H]),
              waits=[(s_pe, pn)], inc=s_act)
            rotg[r] = (s_act, s_act.n)
            rot_i += 1
        NLT_DONE = s_act.n

        m_cum = [0] * ND
        for it in range(ND):
            par2 = it % 2
            if it == 0:
                gate = [(s_act, m_sq[14]), (s_dve, m_xn[14])]
            elif it == 1:
                gate = [(s_act, m_sq[15]), (s_dve, m_xn[15])]
            else:
                gate = [(s_pe, m_cum[it - 2])]
            E("sync", dma(tritp[par2][:, 0:1024], trit_d[it][:, 0:1024]), waits=gate,
              inc=s_tr[par2], amt=16)
            E("sync", dma(tritp[par2][:, 1024:2048], trit_d[it][:, 1024:2048]),
              inc=s_tr[par2], amt=16)
            trw = (s_tr[par2], s_tr[par2].n)
            r = rot_i % 4
            for jc in range(ND):
                E("tensor", lambda e, jc=jc, r=r, par2=par2: e.matmul(
                    rot512[r][:, 0:H], tritp[par2][:, jc * 128:(jc + 1) * 128],
                    nlt[:, jc * H:(jc + 1) * H],
                    start=(jc == 0), stop=(jc == ND - 1)),
                  waits=([trw, (s_act, NLT_DONE), rotg[r]] if jc == 0 else ()),
                  inc=(s_pe if jc == ND - 1 else None))
            m_cum[it] = s_pe.n
            par = it % 2
            E("vector", lambda e, r=r, par=par: e.tensor_copy(
                sct[par][:], rot512[r][:, 0:H]),
              waits=[(s_pe, m_cum[it])], inc=s_dve)
            rotg[r] = (s_dve, s_dve.n)
            dn = s_dve.n
            rot_i += 1
            r2 = rot_i % 4
            E("tensor", lambda e, par=par, r2=r2: e.transpose(
                rot512[r2][0:H, 0:128], sct[par][:], identf[:]),
              waits=[(s_dve, dn), rotg[r2]], inc=s_pe)
            pn = s_pe.n
            E("scalar", lambda e, r2=r2, it=it: e.copy(negc[:, it * 128:(it + 1) * 128],
                                                       rot512[r2][0:H, 0:128]),
              waits=[(s_pe, pn)], inc=s_act)
            rotg[r2] = (s_act, s_act.n)
            rot_i += 1
        NEGC_DONE = s_act.n
        PHASE_A_CUM = s_pe.n

        E("vector", lambda e: e.tensor_copy(chi[:], negc[:]), waits=[(s_act, NEGC_DONE)])
        E("vector", lambda e: e.tensor_tensor(wa[:], negc[:], chi[:], op=OP.subtract))
        E("vector", lambda e: e.tensor_copy(cmid[:], wa[:]))
        E("vector", lambda e: e.tensor_tensor(wb[:], wa[:], cmid[:], op=OP.subtract))
        E("vector", lambda e: e.tensor_copy(clo[:], wb[:]))
        E("vector", lambda e: e.tensor_scalar(negm[:], negc[:, NQ:NKV], -1.0, -BBND,
                                              op0=OP.mult, op1=OP.add), inc=s_dve)
        C_DONE = s_dve.n

        scp = [rsc[:, 0:16], rsc[:, 96:112]]
        c1p = [rsc[:, 57:58], rsc[:, 112:113]]
        c2p = [rsc[:, 58:59], rsc[:, 113:114]]
        gs = rsc[:, 16:20]
        pen = rsc[:, 20:24]
        msk = rsc[:, 24:40]
        cur = rsc[:, 40:56]
        c0 = rsc[:, 56:57]
        cc1 = rsc[:, 59:60]
        gs2 = rsc[:, 60:64]
        ltm = rsc[:, 64:80]
        cur2 = rsc[:, 80:96]
        m_comb = [0] * 8
        for tq in range(8):
            sc = scp[tq % 2]
            c1 = c1p[tq % 2]
            c2 = c2p[tq % 2]
            r = rot_i % 4
            ps = rot512[r][:, 0:16]
            for c in range(ND):
                E("tensor", lambda e, c=c, tq=tq, ps=ps: e.matmul(
                    ps, xn[:, c * NKV + NQ + tq * 128: c * NKV + NQ + (tq + 1) * 128],
                    wg_sb[:, c * NE:(c + 1) * NE],
                    start=(c == 0), stop=(c == ND - 1)),
                  waits=([(s_act, LN_ACT_DONE), (s_wc, WC_INIT), rotg[r]] if c == 0 else ()),
                  inc=(s_pe if c == ND - 1 else None))
            rn = s_pe.n
            ew = [(s_pe, rn)]
            if tq >= 2:
                ew += [(s_dve, m_comb[tq - 2])]
            E("scalar", lambda e, ps=ps, sc=sc, c1=c1: e.activation(
                sc, ps, AF.Exp, accum_out=c1), waits=ew, inc=s_act)
            en = s_act.n
            rotg[r] = (s_act, en)
            rot_i += 1
            E("vector", lambda e, c1=c1, c2=c2: e.reciprocal(c2, c1), waits=[(s_act, en)])
            E("vector", lambda e, sc=sc, c2=c2: e.tensor_scalar(sc, sc, c2, None, op0=OP.mult))
            E("vector", lambda e, sc=sc: e.tensor_reduce(
                gs, sc.rearrange("p (g e) -> p g e", e=4), axis=AX.X, op=OP.max))
            E("vector", lambda e: e.reduce_max(c0, gs, axis=AX.X))
            E("vector", lambda e: e.tensor_scalar(lt4[:], gs, c0, None, op0=OP.is_lt))
            E("vector", lambda e: e.select(gs2, lt4[:], gs, neg4[:, 0:4], add_drain=True))
            E("vector", lambda e: e.reduce_max(cc1, gs2, axis=AX.X))
            E("vector", lambda e: e.tensor_scalar(pen, gs, cc1, None, op0=OP.subtract))
            E("vector", lambda e: e.tensor_scalar(pen, pen, 0.0, 30000.0,
                                                  op0=OP.min, op1=OP.mult))
            for g in range(4):
                E("vector", lambda e, g=g, sc=sc: e.tensor_scalar(
                    msk[:, 4 * g:4 * g + 4], sc[:, 4 * g:4 * g + 4], pen[:, g:g + 1],
                    None, op0=OP.add))
            E("vector", lambda e: e.tensor_copy(cur, msk))
            bufs = [cur, cur2]
            for rnd in range(3):
                a, b = bufs[rnd % 2], bufs[(rnd + 1) % 2]
                E("vector", lambda e, a=a: e.reduce_max(c0, a, axis=AX.X))
                E("vector", lambda e, a=a: e.tensor_scalar(lt16[:], a, c0, None, op0=OP.is_lt))
                E("vector", lambda e, a=a, b=b: e.select(b, lt16[:], a, neg4[:], add_drain=True))
            E("vector", lambda e: e.reduce_max(c0, cur2, axis=AX.X))
            E("vector", lambda e: e.tensor_scalar(cur, msk, c0, None, op0=OP.is_ge))
            E("vector", lambda e, tq=tq, sc=sc: e.tensor_tensor(
                comb[:, tq * 16:(tq + 1) * 16], sc, cur, op=OP.mult), inc=s_dve)
            m_comb[tq] = s_dve.n
        ROUTER_DONE = s_dve.n
        ROUTER_ACT = s_act.n
        PHASE_A_PE = s_pe.n
        if dbg:
            E("gpsimd", dma(dxn_d[:], xn[:]), waits=[(s_act, LN_ACT_DONE)], inc=s_dbg, amt=16)
            E("gpsimd", dma(dnegc_d[:], negc[:]), waits=[(s_act, NEGC_DONE)], inc=s_dbg, amt=16)
            E("gpsimd", dma(dcbf_d[0:H, :], chi[:]), waits=[(s_dve, C_DONE)], inc=s_dbg, amt=16)
            E("gpsimd", dma(dcbf_d[H:2 * H, :], cmid[:]), inc=s_dbg, amt=16)
            E("gpsimd", dma(dcbf_d[2 * H:3 * H, :], clo[:]), inc=s_dbg, amt=16)
            E("gpsimd", dma(dnegm_d[:], negm[:]), waits=[(s_dve, C_DONE)], inc=s_dbg, amt=16)
            E("gpsimd", dma(dcomb_d[:], comb[:]), waits=[(s_dve, ROUTER_DONE)], inc=s_dbg, amt=16)

    A_FREE = [(s_pe, PHASE_A_PE), (s_dve, ROUTER_DONE), (s_act, ROUTER_ACT)]

    with contextlib.ExitStack() as sB:
        pnl = sB.enter_context(nc.sbuf_tensor("pnl", [128, 3 * 2048], BF16))
        kaug = [sB.enter_context(nc.sbuf_tensor(f"kaug{i}", [68, NKV], BF16))
                for i in range(2)]
        vaug = sB.enter_context(nc.sbuf_tensor("vaug", [128, 16 * 130], BF16))
        qaug = [sB.enter_context(nc.sbuf_tensor(f"qaug{i}", [68, NQ], BF16))
                for i in range(2)]
        pt = [sB.enter_context(nc.sbuf_tensor(f"pt{i}", [128, 2048], BF16))
              for i in range(2)]
        atm = [sB.enter_context(nc.sbuf_tensor(f"atm{i}", [128, 64], F32))
               for i in range(2)]
        afm = sB.enter_context(nc.sbuf_tensor("afm", [128, 8 * NQ], BF16))
        linv = sB.enter_context(nc.sbuf_tensor("linv", [128, 2], F32))

        T2 = [(s_pe, PHASE_A_CUM), (s_dve, C_DONE), (s_act, NEGC_DONE)]
        E("gpsimd", dma(kaug[0][67:68, :], ones_d[0:1, :]), waits=T2, inc=s_wc, amt=16)
        E("gpsimd", dma(kaug[1][67:68, :], ones_d[0:1, :]), inc=s_wc, amt=16)
        E("gpsimd", dma(qaug[0][64:67, :], ones_d[0:3, 0:NQ]), inc=s_wc, amt=16)
        E("gpsimd", dma(qaug[1][64:67, :], ones_d[0:3, 0:NQ]), inc=s_wc, amt=16)
        ONES_ALL = s_wc.n
        E("vector", lambda e: e.memset(
            vaug.rearrange("p (c k) -> p c k", k=65)[:, :, 64:65], 1.0),
          waits=[(s_act, ROUTER_ACT)], inc=s_dve)
        VONES = s_dve.n

        NIT = NPAIR * 16
        m_qk = [0] * NIT
        m_exp = [0] * NIT
        m_av = [0] * NIT
        m_avsc = [0] * NIT
        m_attnT = [0] * NIT
        m_afe = [0] * NIT
        m_projdone = [0] * NPAIR
        m_qkdone = [0] * NPAIR
        m_projevac = [0] * NPAIR

        def emit_panels(p, gate):
            for j in range(3):
                for s2 in range(2):
                    E("gpsimd", dma(
                        pnl[:, j * 2048 + s2 * 1024: j * 2048 + (s2 + 1) * 1024],
                        wqkv_d[j * 16 + p][:, s2 * 1024:(s2 + 1) * 1024]),
                      waits=(gate if (j == 0 and s2 == 0) else ()), inc=s_wp, amt=16)

        def emit_crows(p, gate):
            h0, h1 = 2 * p, 2 * p + 1
            for kb_, h in ((kaug[0], h0), (kaug[1], h1)):
                E("gpsimd", dma(kb_[64:65, :], chi[h:h + 1, :]),
                  waits=gate, inc=s_cr, amt=16)
                gate = ()
                E("gpsimd", dma(kb_[65:66, :], cmid[h:h + 1, :]), inc=s_cr, amt=16)
                E("gpsimd", dma(kb_[66:67, :], clo[h:h + 1, :]), inc=s_cr, amt=16)
            E("gpsimd", dma(qaug[0][67:68, :], negm[h0:h0 + 1, :]), inc=s_cr, amt=16)
            E("gpsimd", dma(qaug[1][67:68, :], negm[h1:h1 + 1, :]), inc=s_cr, amt=16)

        emit_panels(0, [(s_pe, LN_PE_LAST), (s_dve, LN_XN_DVE),
                        (s_act, LN_ACT_DONE)])
        emit_crows(0, T2)

        for p in range(NPAIR):
            first_w = [(s_wp, 96 * (p + 1)), (s_wc, ONES_ALL)]
            if p > 0:
                first_w += [(s_act, m_afe[p * 16 - 1])]
            for kc in range(4):
                r = rot_i % 4
                for c in range(ND):
                    E("tensor", lambda e, c=c, kc=kc, r=r: e.matmul(
                        rot512[r], pnl[:, 2048 + c * 128: 2048 + (c + 1) * 128],
                        xn[:, c * NKV + kc * 512: c * NKV + (kc + 1) * 512],
                        start=(c == 0), stop=(c == ND - 1)),
                      waits=((first_w + [rotg[r]]) if c == 0 else ()),
                      inc=(s_pe if c == ND - 1 else None))
                first_w = []
                pn = s_pe.n
                E("vector", lambda e, r=r, kc=kc: e.tensor_copy(
                    kaug[0][0:64, kc * 512:(kc + 1) * 512], rot512[r][0:64, :]),
                  waits=[(s_pe, pn)])
                E("vector", lambda e, r=r, kc=kc: e.tensor_copy(
                    kaug[1][0:64, kc * 512:(kc + 1) * 512], rot512[r][64:128, :]),
                  inc=s_dve)
                rotg[r] = (s_dve, s_dve.n)
                rot_i += 1
            for kt in range(16):
                r = rot_i % 4
                ps = rot512[r][:, 0:128]
                for c in range(ND):
                    E("tensor", lambda e, c=c, kt=kt, ps=ps: e.matmul(
                        ps, xn[:, c * NKV + kt * 128: c * NKV + (kt + 1) * 128],
                        pnl[:, 4096 + c * 128: 4096 + (c + 1) * 128],
                        start=(c == 0), stop=(c == ND - 1)),
                      waits=([rotg[r], (s_dve, VONES)] if c == 0 else ()),
                      inc=(s_pe if c == ND - 1 else None))
                pn = s_pe.n
                E("vector", lambda e, ps=ps, kt=kt: e.tensor_copy(
                    vaug.rearrange("p (c k) -> p c k", k=65)[:, 2 * kt:2 * kt + 2, 0:64],
                    ps.rearrange("p (c k) -> p c k", k=64)),
                  waits=[(s_pe, pn)], inc=s_dve)
                rotg[r] = (s_dve, s_dve.n)
                rot_i += 1
            for qc in range(2):
                r = rot_i % 4
                for c in range(ND):
                    E("tensor", lambda e, c=c, qc=qc, r=r: e.matmul(
                        rot512[r], pnl[:, c * 128:(c + 1) * 128],
                        xn[:, c * NKV + NQ + qc * 512: c * NKV + NQ + (qc + 1) * 512],
                        start=(c == 0), stop=(c == ND - 1)),
                      waits=([rotg[r]] if c == 0 else ()),
                      inc=(s_pe if c == ND - 1 else None))
                pn = s_pe.n
                E("vector", lambda e, r=r, qc=qc: e.tensor_scalar(
                    qaug[0][0:64, qc * 512:(qc + 1) * 512], rot512[r][0:64, :],
                    0.125, None, op0=OP.mult),
                  waits=[(s_pe, pn)])
                E("vector", lambda e, r=r, qc=qc: e.tensor_scalar(
                    qaug[1][0:64, qc * 512:(qc + 1) * 512], rot512[r][64:128, :],
                    0.125, None, op0=OP.mult),
                  inc=s_dve)
                rotg[r] = (s_dve, s_dve.n)
                rot_i += 1
            m_projdone[p] = s_pe.n
            m_projevac[p] = s_dve.n
            if p + 1 < NPAIR and p != 7:
                emit_panels(p + 1, [(s_pe, m_projdone[p])])

            def emit_qk(g):
                h, qt = divmod(g % 16, 8)
                ka, qa = kaug[h], qaug[h]
                par = g % 2
                T = SA if par == 0 else SB
                live = 2 * (qt + 1) * 128
                w0 = [(s_cr, 128 * (g // 16 + 1)), (s_dve, m_projevac[g // 16])]
                if g >= 3:
                    w0 += [(s_act, m_afe[g - 3])]
                if g >= 2:
                    w0 += [(s_dve, m_avsc[g - 2])]
                for kt in range(qt + 1):
                    E("tensor", lambda e, kt=kt, ka=ka, qa=qa, qt=qt, T=T: e.matmul(
                        T[:, kt * 128:(kt + 1) * 128], ka[:, kt * 128:(kt + 1) * 128],
                        qa[:, qt * 128:(qt + 1) * 128], start=True, stop=(kt != qt)),
                      waits=(w0 if kt == 0 else ()))
                    w0 = []
                E("tensor", lambda e, qt=qt, T=T: e.matmul(
                    T[:, qt * 128:(qt + 1) * 128], identb[:], odiag[:],
                    start=False, stop=True))
                for j in range(qt + 1):
                    E("tensor", lambda e, j=j, ka=ka, qa=qa, qt=qt, T=T: e.matmul(
                        T[:, (qt + 1 + j) * 128: (qt + 2 + j) * 128],
                        ka[:, NQ + j * 128: NQ + (j + 1) * 128],
                        qa[:, qt * 128:(qt + 1) * 128], start=True, stop=(j != qt)))
                E("tensor", lambda e, qt=qt, T=T: e.matmul(
                    T[:, (2 * qt + 1) * 128: (2 * qt + 2) * 128], identb[:], trium[:],
                    start=False, stop=True), inc=s_pe)
                m_qk[g] = s_pe.n
                par2 = g % 2
                ew = [(s_pe, m_qk[g])]
                if g >= 2:
                    ew += [(s_pe, m_av[g - 2])]
                if dbg and g == gtap + 2:
                    ew += [(s_dbg2, 16)]
                E("scalar", lambda e, par=par2, live=live, T=T: e.activation(
                    pt[par][:, 0:live], T[:, 0:live], AF.Exp),
                  waits=ew, inc=s_act)
                m_exp[g] = s_act.n
                if dbg and g == gtap:
                    E("gpsimd", dma(dpt_d[:], pt[par2][:, 0:1152]),
                      waits=[(s_act, m_exp[g])], inc=s_dbg2, amt=16)
                    E("gpsimd", dma(dvau_d[:], vaug[:]),
                      waits=[(s_pe, m_qk[g])], inc=s_dbg2, amt=16)

            def emit_av(g):
                h, qt = divmod(g % 16, 8)
                par = g % 2
                T = SA if par == 0 else SB
                pav = T[:, 0:65]
                aw = [(s_act, m_exp[g])]
                if g >= 2:
                    aw += [(s_dve, m_avsc[g - 2])]
                nch = 2 * (qt + 1)
                for jj in range(nch):
                    vidx = jj if jj <= qt else 8 + (jj - (qt + 1))
                    E("tensor", lambda e, jj=jj, par=par, h=h, pav=pav, nch=nch,
                      vidx=vidx: e.matmul(
                        pav, pt[par][:, jj * 128:(jj + 1) * 128],
                        vaug[:, vidx * 130 + h * 65: vidx * 130 + h * 65 + 65],
                        start=(jj == 0), stop=(jj == nch - 1)),
                      waits=(aw if jj == 0 else ()),
                      inc=(s_pe if jj == nch - 1 else None))
                m_av[g] = s_pe.n
                lv = linv[:, par:par + 1]
                E("vector", lambda e, pav=pav, lv=lv: e.reciprocal(lv, pav[:, 64:65]),
                  waits=[(s_pe, m_av[g])])
                E("vector", lambda e, pav=pav, lv=lv, par=par: e.tensor_scalar(
                    atm[par][:], pav[:, 0:64], lv, None, op0=OP.mult),
                  waits=([(s_dbg3, 16)] if (dbg and g == gtap + 2) else ()), inc=s_dve)
                m_avsc[g] = s_dve.n
                if dbg and g == gtap:
                    E("gpsimd", dma(datm_d[:], atm[par][:]),
                      waits=[(s_dve, m_avsc[g])], inc=s_dbg3, amt=16)

            def emit_tp(g):
                h, qt = divmod(g % 16, 8)
                par = g % 2
                Tother = SB if par == 0 else SA
                ptp = Tother[:, 512:640]
                tw = [(s_dve, m_avsc[g])]
                if g >= 2:
                    tw += [(s_act, m_afe[g - 2])]
                E("tensor", lambda e, ptp=ptp, par=par: e.transpose(
                    ptp[0:64, :], atm[par][:], identf[:]), waits=tw, inc=s_pe)
                m_attnT[g] = s_pe.n
                p_, qt_ = g // 16, qt
                E("scalar", lambda e, ptp=ptp, h=h, qt=qt_, p=p_: e.copy(
                    afm[h * 64:(h + 1) * 64,
                        (p % 8) * NQ + qt * 128:(p % 8) * NQ + (qt + 1) * 128],
                    ptp[0:64, :]),
                  waits=[(s_pe, m_attnT[g])], inc=s_act)
                m_afe[g] = s_act.n

            for i in range(16):
                g = p * 16 + i
                emit_qk(g)
                if i >= 1:
                    emit_av(g - 1)
                if i >= 2:
                    emit_tp(g - 2)
            emit_av(p * 16 + 15)
            emit_tp(p * 16 + 14)
            emit_tp(p * 16 + 15)
            m_qkdone[p] = s_pe.n
            if p + 1 < NPAIR:
                emit_crows(p + 1, [(s_pe, m_qkdone[p])])

            if p in (7, 15):
                half = p // 8
                wou_base = s_wou.n
                m_woev = [[0] * 8 for _ in range(4)]
                wo_tgt = [0] * 32
                for dq in range(4):
                    for pc2 in range(8):
                        n = dq * 8 + pc2
                        slot = n % 12
                        dchunk = half * 8 + pc2
                        if n == 0:
                            gate = [(s_pe, m_projdone[p])]
                        elif n < 12:
                            gate = ()
                        else:
                            gate = [(s_wou, wou_base + n - 12 + 1)]
                        E("gpsimd", dma(pnl[:, slot * 512:(slot + 1) * 512],
                                        wo_d[dchunk * 128:(dchunk + 1) * 128,
                                             dq * 512:(dq + 1) * 512]),
                          waits=gate, inc=s_wo[slot], amt=16)
                        wo_tgt[n] = s_wo[slot].n
                    for qt in range(8):
                        r = rot_i % 4
                        ps = rot512[r]
                        for pc2 in range(8):
                            slot = (dq * 8 + pc2) % 12
                            w = []
                            if pc2 == 0:
                                w = [rotg[r]]
                                if dq == 0 and qt == 0:
                                    w += [(s_act, m_afe[p * 16 + 15]),
                                          (s_dve, m_avsc[p * 16 + 15])]
                            if qt == 0:
                                w = list(w) + [(s_wo[slot], wo_tgt[dq * 8 + pc2])]
                            lastmm = (pc2 == 7)
                            if qt == 7:
                                inc_s = s_wou
                            else:
                                inc_s = s_pe if lastmm else None
                            E("tensor", lambda e, pc2=pc2, qt=qt, dq=dq, ps=ps, slot=slot: e.matmul(
                                ps, afm[:, pc2 * NQ + qt * 128: pc2 * NQ + (qt + 1) * 128],
                                pnl[:, slot * 512:(slot + 1) * 512],
                                start=(pc2 == 0), stop=(pc2 == 7)),
                              waits=w,
                              inc=(inc_s if (qt == 7 or lastmm) else None))
                        if qt == 7:
                            sem, val = s_wou, s_wou.n
                        else:
                            sem, val = s_pe, s_pe.n
                        E("vector", lambda e, qt=qt, dq=dq, ps=ps: e.tensor_tensor(
                            acc[:, qt * D + dq * 512: qt * D + (dq + 1) * 512], ps,
                            acc[:, qt * D + dq * 512: qt * D + (dq + 1) * 512],
                            op=OP.add),
                          waits=[(sem, val)], inc=s_dve)
                        m_woev[dq][qt] = s_dve.n
                        rotg[r] = (s_dve, s_dve.n)
                        rot_i += 1
                WO_PE = s_pe.n
                WO_WOU = s_wou.n
                if p == 7:
                    emit_panels(8, [(s_wou, WO_WOU)])
        ATT_PE_DONE = s_pe.n
        ATT_DVE_DONE = s_dve.n
        ATT_ACT_DONE = s_act.n
        if dbg:
            E("gpsimd", dma(dafm_d[:], afm[:]), waits=[(s_act, m_afe[NIT - 1])],
              inc=s_dbg, amt=16)

    B_FREE = [(s_pe, ATT_PE_DONE), (s_dve, ATT_DVE_DONE), (s_act, ATT_ACT_DONE),
              (s_wou, s_wou.n)]
    if dbg:
        B_FREE = B_FREE + [(s_dbg, s_dbg.n)]

    sAB.close()
    with contextlib.ExitStack() as sC:
        w13 = [sC.enter_context(nc.sbuf_tensor(f"w13_{i}", [128, 2 * 2048], BF16))
               for i in range(2)]
        w2s = [[sC.enter_context(nc.sbuf_tensor(f"w2s_{i}_{pb}", [128, 512], BF16))
                for pb in range(2)] for i in range(NI)]
        hb = sC.enter_context(nc.sbuf_tensor("hb", [128, NI * NQ], BF16))
        silu = [sC.enter_context(nc.sbuf_tensor("silu0", [128, NQ], F32))] * 2

        NEXP = NE + 2
        m_h1 = [[0] * NI for _ in range(NEXP)]
        m_h3 = [[0] * NI for _ in range(NEXP)]
        m_silu = [[0] * NI for _ in range(NEXP)]
        m_mult = [[0] * NI for _ in range(NEXP)]

        def w2src(e_, i, q):
            if e_ < NE:
                return w2_d[e_, i * 128:(i + 1) * 128, q * 512:(q + 1) * 512]
            o = (e_ - NE) * INTER
            return ws2_d[o + i * 128: o + (i + 1) * 128, q * 512:(q + 1) * 512]

        def emit_w13(n, gate):
            par = n % 2
            for s2 in range(4):
                E("gpsimd", dma(w13[par][:, s2 * 1024:(s2 + 1) * 1024],
                                w13h_d[n][:, s2 * 1024:(s2 + 1) * 1024]),
                  waits=(gate if s2 == 0 else ()), inc=s_w13[par], amt=16)

        emit_w13(0, list(B_FREE))
        emit_w13(1, [])
        n13 = 2

        for e_ in range(NEXP):
            for i in range(NI):
                n = e_ * NI + i
                par = n % 2
                hw = [(s_w13[par], (n // 2 + 1) * 64)]
                if n > 0:
                    pe_, pi = divmod(n - 1, NI)
                    hw += [(s_act, m_silu[pe_][pi])]
                else:
                    hw += [(s_act, ATT_ACT_DONE)]
                for tc in range(2):
                    for c in range(ND):
                        E("tensor", lambda e, c=c, tc=tc, par=par: e.matmul(
                            SA[:, tc * 512:(tc + 1) * 512],
                            w13[par][:, c * 128:(c + 1) * 128],
                            xn[:, c * NKV + NQ + tc * 512: c * NKV + NQ + (tc + 1) * 512],
                            start=(c == 0), stop=(c == ND - 1)),
                          waits=(hw if (tc == 0 and c == 0) else ()),
                          inc=(s_pe if (tc == 1 and c == ND - 1) else None))
                m_h1[e_][i] = s_pe.n
                hw3 = []
                if n > 0:
                    pe_, pi = divmod(n - 1, NI)
                    hw3 = [(s_dve, m_mult[pe_][pi])]
                for tc in range(2):
                    for c in range(ND):
                        E("tensor", lambda e, c=c, tc=tc, par=par: e.matmul(
                            SB[:, tc * 512:(tc + 1) * 512],
                            w13[par][:, 2048 + c * 128: 2048 + (c + 1) * 128],
                            xn[:, c * NKV + NQ + tc * 512: c * NKV + NQ + (tc + 1) * 512],
                            start=(c == 0), stop=(c == ND - 1)),
                          waits=(hw3 if (tc == 0 and c == 0) else ()),
                          inc=(s_pe if (tc == 1 and c == ND - 1) else None))
                m_h3[e_][i] = s_pe.n
                while n13 < NEXP * NI and n13 % NI >= 2 and (n13 - 2) <= e_ * NI + i:
                    ge_, gi_ = divmod(n13 - 2, NI)
                    emit_w13(n13, [(s_pe, m_h3[ge_][gi_])])
                    n13 += 1
                sw = [(s_pe, m_h1[e_][i])]
                if n >= 1:
                    pe2, pi2 = divmod(n - 1, NI)
                    sw += [(s_dve, m_mult[pe2][pi2])]
                E("scalar", lambda e, par=par: e.activation(silu[par][:], SA[:, 0:1024],
                                                            AF.Silu),
                  waits=sw, inc=s_act)
                m_silu[e_][i] = s_act.n
                E("vector", lambda e, i=i, par=par: e.tensor_tensor(
                    hb[:, i * NQ:(i + 1) * NQ], SB[:, 0:1024], silu[par][:],
                    op=OP.mult),
                  waits=[(s_act, m_silu[e_][i]), (s_pe, m_h3[e_][i])], inc=s_dve)
                m_mult[e_][i] = s_dve.n

            w2_tgt = [0] * NI
            for q in range(4):
                seq = e_ * 4 + q
                pb = seq % 2
                for i in range(NI):
                    if seq == 0:
                        gate = B_FREE if i == 0 else ()
                    elif seq == 1:
                        gate = ()
                    else:
                        gate = [(s_w2u, (seq - 2) * NI + i + 1)]
                    E("gpsimd", dma(w2s[i][pb][:], w2src(e_, i, q)), waits=gate,
                      inc=s_w2s[i][pb], amt=16)
                    w2_tgt[i] = s_w2s[i][pb].n
                for tt in range(8):
                    r = rot_i % 4
                    ps = rot512[r]
                    for i in range(NI):
                        w = []
                        if i == 0:
                            w = [rotg[r], (s_dve, m_mult[e_][NI - 1])]
                        if tt == 0:
                            w = list(w) + [(s_w2s[i][pb], w2_tgt[i])]
                        if qt_inc := (tt == 7):
                            inc_s = s_w2u
                        else:
                            inc_s = s_pe if i == NI - 1 else None
                        E("tensor", lambda e, i=i, tt=tt, ps=ps, pb=pb: e.matmul(
                            ps, hb[:, i * NQ + tt * 128: i * NQ + (tt + 1) * 128],
                            w2s[i][pb][:], start=(i == 0), stop=(i == NI - 1)),
                          waits=w,
                          inc=(inc_s if (tt == 7 or i == NI - 1) else None))
                    if tt == 7:
                        sem, val = s_w2u, s_w2u.n
                    else:
                        sem, val = s_pe, s_pe.n
                    scal = comb[:, tt * 16 + e_: tt * 16 + e_ + 1] if e_ < NE else 1.0
                    E("vector", lambda e, tt=tt, q=q, ps=ps, scal=scal: e.scalar_tensor_tensor(
                        acc[:, tt * D + q * 512: tt * D + (q + 1) * 512], ps, scal,
                        acc[:, tt * D + q * 512: tt * D + (q + 1) * 512],
                        op0=OP.mult, op1=OP.add),
                      waits=[(sem, val)], inc=s_dve)
                    rotg[r] = (s_dve, s_dve.n)
                    rot_i += 1
            while n13 < NEXP * NI and n13 % NI < 2:
                ge_, gi_ = divmod(n13 - 2, NI)
                emit_w13(n13, [(s_pe, m_h3[ge_][gi_])])
                n13 += 1
        MOE_DVE_DONE = s_dve.n

    if dbg:
        E("gpsimd", dma(dacc_d[:], acc[:]), waits=[(s_dve, MOE_DVE_DONE)],
          inc=s_dbg, amt=16)
        E("gpsimd", None, waits=[(s_dbg, s_dbg.n), (s_dbg2, s_dbg2.n), (s_dbg3, s_dbg3.n)])

    for tt in range(8):
        E("sync", dma(out_d[tt * 128:(tt + 1) * 128, :], acc[:, tt * D:(tt + 1) * D]),
          waits=([(s_dve, MOE_DVE_DONE)] if tt == 0 else ()), inc=s_out, amt=16)
    E("sync", None, waits=[(s_out, s_out.n)])

    kb.replay()
    nc.finalize()
    return nc


_CACHE = {}


def _host_prep(inputs):
    f = np.float32

    def panelize(w):
        wT = np.asarray(w, f).T
        return wT.reshape(16, 128, 16, 128).transpose(2, 1, 0, 3).reshape(16, 128, 2048)

    def blk13(w, ws):
        a = np.asarray(w, f).transpose(0, 2, 1).reshape(NE, 16, 128, NI, 128) \
            .transpose(0, 3, 2, 1, 4).reshape(NE * NI, 128, 2048)
        b = np.asarray(ws, f).T.reshape(16, 128, 2 * NI, 128) \
            .transpose(2, 1, 0, 3).reshape(2 * NI, 128, 2048)
        return np.concatenate([a, b], axis=0)

    wqkv = np.ascontiguousarray(np.concatenate(
        [panelize(inputs["L0_attn_Wq"]), panelize(inputs["L0_attn_Wk"]),
         panelize(inputs["L0_attn_Wv"])], axis=0))
    w13h = np.ascontiguousarray(np.concatenate(
        [blk13(inputs["L0_ffn_W1"], inputs["L0_ffn_Ws1"]),
         blk13(inputs["L0_ffn_W3"], inputs["L0_ffn_Ws3"])], axis=2))
    shared = dict(
        wqkv_p=wqkv,
        w13h=w13h,
        woT=np.ascontiguousarray(inputs["L0_attn_Wo"].T, dtype=f),
        wfT=np.ascontiguousarray(inputs["L0_attn_Wf"].T, dtype=f),
        bf_col=np.ascontiguousarray(np.asarray(inputs["L0_attn_bf"]).reshape(H, 1), dtype=f),
        wgT=np.ascontiguousarray(inputs["L0_ffn_Wg"].T, dtype=f),
        w2T=np.ascontiguousarray(np.asarray(inputs["L0_ffn_W2"]).transpose(0, 2, 1), dtype=f),
        ws2T=np.ascontiguousarray(inputs["L0_ffn_Ws2"].T, dtype=f),
        trium=np.ascontiguousarray(np.tril(np.full((128, 128), BNEG, f), -1).astype(_BF)),
        identb=np.ascontiguousarray(np.eye(128, dtype=_BF)),
        identf=np.ascontiguousarray(np.eye(128, dtype=f)),
        ones_rows=np.ones((3, NKV), dtype=_BF),
    )
    idx = np.arange(2048).reshape(16, 128)
    tp = {}
    for s in (0, 1):
        other = idx[(1 - s)::2].reshape(-1)
        own = idx[s::2].reshape(-1)
        tp[s] = np.concatenate([other, own])
    trit = {s: np.ascontiguousarray(
                (tp[s][:, None] <= tp[s][None, :]).astype(f)
                .reshape(16, 128, 16, 128).transpose(2, 1, 0, 3).reshape(16, 128, 2048))
            for s in (0, 1)}
    odiag = {0: np.full((128, 128), BNEG, dtype=_BF),
             1: np.zeros((128, 128), dtype=_BF)}

    x = np.asarray(inputs["L0_attn_x"])
    in_maps = []
    for c in range(8):
        b, s = divmod(c, 2)
        m = dict(shared)
        m["x_all"] = np.ascontiguousarray(x[b][tp[s]], dtype=f)
        m["trit"] = trit[s]
        m["odiag"] = odiag[s]
        in_maps.append(m)
    return in_maps, tp


def kernel(**inputs):
    if "nc" not in _CACHE:
        _CACHE["nc"] = build_program()
    nc = _CACHE["nc"]
    in_maps, tp = _host_prep(inputs)
    res = run_bass_kernel_spmd(nc, in_maps, core_ids=list(range(8)))
    out = np.empty((4, 2048, 2048), np.float32)
    for c in range(8):
        b, s = divmod(c, 2)
        out[b][tp[s][1024:]] = res.results[c]["out"]
    return out


# version 19
# speedup vs baseline: 1.1202x; 1.1202x over previous
import contextlib
import numpy as np
import ml_dtypes
import concourse.bass as bass
import concourse.mybir as mybir
from concourse.alu_op_type import AluOpType
from concourse.bass_utils import run_bass_kernel_spmd

F32 = mybir.dt.float32
I32 = mybir.dt.int32
BF16 = mybir.dt.bfloat16
AF = mybir.ActivationFunctionType
AX = mybir.AxisListType
OP = AluOpType

BNEG = -30000.0
BBND = 50.0

D = 2048
ND = 16
H = 32
NPAIR = 16
NQ = 1024
NKV = 2048
INTER = 1408
NI = 11
NE = 16
SH_INTER = 2816

_BF = ml_dtypes.bfloat16


class Cnt:
    def __init__(self, nc, name):
        self.h = nc.alloc_semaphore(name)
        self.n = 0


class KB:

    def __init__(self, simcheck=False):
        self.nc = bass.Bass("TRN2", target_bir_lowering=False,
                            detect_race_conditions=simcheck)
        self.events = {k: [] for k in ("sync", "gpsimd", "tensor", "vector", "scalar")}
        self.simcheck = simcheck
        self.selfsem = {}

    def sem(self, name):
        return Cnt(self.nc, name)

    def ev(self, eng, fn, waits=(), inc=None, amt=1):
        waits = list(waits)
        if eng in self.selfsem and fn is not None:
            ss = self.selfsem[eng]
            waits.append((ss, ss.n))
            if inc is None:
                inc = ss
        w = tuple((s.h, int(v)) for (s, v) in waits if s is not None and v > 0)
        ih = None
        if inc is not None:
            inc.n += amt
            ih = (inc.h, amt)
        self.events[eng].append((w, fn, ih))

    def replay(self):
        nc = self.nc
        with nc.Block() as blk:
            for name, deco in (("sync", blk.sync), ("gpsimd", blk.gpsimd),
                               ("tensor", blk.tensor), ("vector", blk.vector),
                               ("scalar", blk.scalar)):
                evs = self.events[name]

                def make(evs):
                    def body(eng):
                        for w, fn, ih in evs:
                            for h, v in w:
                                eng.wait_ge(h, v)
                            inst = fn(eng) if fn is not None else None
                            if ih is not None:
                                inst.then_inc(*ih)
                    return body

                deco(make(evs))


def build_program(simcheck=False, dbg=False, gtap=0):
    kb = KB(simcheck)
    nc = kb.nc
    E = kb.ev

    x_d = nc.dram_tensor("x_all", [NKV, D], F32, kind="ExternalInput")
    wqkv_d = nc.dram_tensor("wqkv_p", [48, 128, 2048], F32, kind="ExternalInput")
    wo_d = nc.dram_tensor("woT", [D, D], F32, kind="ExternalInput")
    wf_d = nc.dram_tensor("wfT", [D, H], F32, kind="ExternalInput")
    bf_d = nc.dram_tensor("bf_col", [H, 1], F32, kind="ExternalInput")
    wg_d = nc.dram_tensor("wgT", [D, NE], F32, kind="ExternalInput")
    w13h_d = nc.dram_tensor("w13h", [(NE + 2) * NI, 128, 4096], F32, kind="ExternalInput")
    w2_d = nc.dram_tensor("w2T", [NE, INTER, D], F32, kind="ExternalInput")
    ws2_d = nc.dram_tensor("ws2T", [SH_INTER, D], F32, kind="ExternalInput")
    trit_d = nc.dram_tensor("trit", [16, 128, 2048], F32, kind="ExternalInput")
    odiag_d = nc.dram_tensor("odiag", [128, 128], BF16, kind="ExternalInput")
    trium_d = nc.dram_tensor("trium", [128, 128], BF16, kind="ExternalInput")
    identb_d = nc.dram_tensor("identb", [128, 128], BF16, kind="ExternalInput")
    identf_d = nc.dram_tensor("identf", [128, 128], F32, kind="ExternalInput")
    ones_d = nc.dram_tensor("ones_rows", [3, NKV], BF16, kind="ExternalInput")
    out_d = nc.dram_tensor("out", [NQ, D], F32, kind="ExternalOutput")
    if dbg:
        dxn_d = nc.dram_tensor("dxn", [128, ND * NKV], BF16, kind="ExternalOutput")
        dnegc_d = nc.dram_tensor("dnegc", [H, NKV], F32, kind="ExternalOutput")
        dcbf_d = nc.dram_tensor("dcbf", [3 * H, NKV], BF16, kind="ExternalOutput")
        dnegm_d = nc.dram_tensor("dnegm", [H, NQ], BF16, kind="ExternalOutput")
        dcomb_d = nc.dram_tensor("dcomb", [128, 128], F32, kind="ExternalOutput")
        dpt_d = nc.dram_tensor("dpt", [128, 1152], BF16, kind="ExternalOutput")
        datm_d = nc.dram_tensor("datm", [128, 64], F32, kind="ExternalOutput")
        dafm_d = nc.dram_tensor("dafm", [128, 8 * NQ], BF16, kind="ExternalOutput")
        dacc_d = nc.dram_tensor("dacc", [128, 8 * D], F32, kind="ExternalOutput")
        dvau_d = nc.dram_tensor("dvau", [128, 16 * 130], BF16, kind="ExternalOutput")
        s_dbg = kb.sem("s_dbg")
        s_dbg2 = kb.sem("s_dbg2")
        s_dbg3 = kb.sem("s_dbg3")

    SA = nc.alloc_psum_tensor("SA", [128, 2048], F32)
    SB = nc.alloc_psum_tensor("SB", [128, 2048], F32)
    rot512 = [SA[:, 1024:1536], SA[:, 1536:2048], SB[:, 1024:1536], SB[:, 1536:2048]]
    rotg = [(None, 0)] * 4

    acc = nc.alloc_sbuf_tensor("acc", [128, 8 * D], F32)
    xn = nc.alloc_sbuf_tensor("xn_fm", [128, ND * NKV], BF16)
    comb = nc.alloc_sbuf_tensor("comb", [128, 8 * 16], F32)
    trium = nc.alloc_sbuf_tensor("trium_s", [128, 128], BF16)
    identb = nc.alloc_sbuf_tensor("identb_s", [128, 128], BF16)
    identf = nc.alloc_sbuf_tensor("identf_s", [128, 128], F32)
    wg_sb = nc.alloc_sbuf_tensor("wg_sb", [128, ND * NE], BF16)
    wf_sb = nc.alloc_sbuf_tensor("wf_sb", [128, ND * H], BF16)
    bf_sb = nc.alloc_sbuf_tensor("bf_sb", [H, 1], F32)
    odiag = nc.alloc_sbuf_tensor("odiag_s", [128, 128], BF16)
    cols = nc.alloc_sbuf_tensor("cols", [128, 16], F32)
    colsq = nc.alloc_sbuf_tensor("colsq", [128, 16], F32)
    neg4 = nc.alloc_sbuf_tensor("neg4", [128, 16], F32)
    epst = nc.alloc_sbuf_tensor("epst", [128, 1], F32)
    nc.const_aps.register(F32, 1e-5, epst[:])
    sAB = contextlib.ExitStack()
    chi = sAB.enter_context(nc.sbuf_tensor("chi", [H, NKV], BF16))
    cmid = sAB.enter_context(nc.sbuf_tensor("cmid", [H, NKV], BF16))
    clo = sAB.enter_context(nc.sbuf_tensor("clo", [H, NKV], BF16))
    negm = sAB.enter_context(nc.sbuf_tensor("negm", [H, NQ], BF16))
    negc = sAB.enter_context(nc.sbuf_tensor("negc", [H, NKV], F32))

    s_pe = kb.sem("s_pe")
    s_dve = kb.sem("s_dve")
    s_act = kb.sem("s_act")
    kb.selfsem = {"vector": s_dve, "scalar": s_act}
    s_xa = kb.sem("s_xa")
    s_xt = [kb.sem(f"s_xt{i}") for i in range(2)]
    s_wc = kb.sem("s_wc")
    s_xc = kb.sem("s_xc")
    s_tr = [kb.sem(f"s_tr{i}") for i in range(2)]
    s_wp = kb.sem("s_wp")
    s_wo = [kb.sem(f"s_wo{i}") for i in range(12)]
    s_wou = kb.sem("s_wou")
    s_cr = kb.sem("s_cr")
    s_w13 = [kb.sem(f"s_w13_{i}") for i in range(2)]
    s_w2s = [[kb.sem(f"s_w2s_{i}_{pb}") for pb in range(2)] for i in range(NI)]
    s_w2u = kb.sem("s_w2u")
    s_out = kb.sem("s_out")

    def dma(dst, src):
        return lambda eng, d=dst, s=src: eng.dma_start(d, s)

    E("vector", lambda e: e.memset(neg4[:], -4.0))
    E("vector", lambda e: e.memset(epst[:], 1e-5), inc=s_dve)
    EPS_SET = s_dve.n
    E("gpsimd", dma(trium[:], trium_d[:]), inc=s_wc, amt=16)
    E("gpsimd", dma(odiag[:], odiag_d[:]), inc=s_wc, amt=16)
    E("gpsimd", dma(identb[:], identb_d[:]), inc=s_wc, amt=16)
    E("gpsimd", dma(wg_sb.rearrange("q (c g) -> q c g", g=NE),
                    wg_d.rearrange("(c q) g -> q c g", q=128)), inc=s_wc, amt=16)
    E("gpsimd", dma(wf_sb.rearrange("q (c h) -> q c h", h=H),
                    wf_d.rearrange("(c q) h -> q c h", q=128)), inc=s_wc, amt=16)
    WC_INIT = s_wc.n
    E("sync", dma(identf[:], identf_d[:]), inc=s_xc, amt=16)
    E("sync", dma(bf_sb[:], bf_d[:]), inc=s_xc, amt=16)
    XC_ALL = s_xc.n
    for tt in range(8):
        E("sync", dma(acc[:, tt * D:(tt + 1) * D], x_d[(8 + tt) * 128:(9 + tt) * 128, :]),
          inc=s_xa, amt=16)
    XA_ALL = s_xa.n

    with contextlib.ExitStack() as sA:
        tritp = [sA.enter_context(nc.sbuf_tensor(f"tritp{i}", [128, ND * 128], F32))
                 for i in range(2)]
        xn_tm = sA.enter_context(nc.sbuf_tensor("xn_tm", [128, D], F32))
        u_fm = sA.enter_context(nc.sbuf_tensor("u_fm", [H, NKV], F32))
        wa = sA.enter_context(nc.sbuf_tensor("wa", [H, NKV], F32))
        wb = sA.enter_context(nc.sbuf_tensor("wb", [H, NKV], F32))
        nlt = sA.enter_context(nc.sbuf_tensor("nlt", [128, ND * H], F32))
        sct = [sA.enter_context(nc.sbuf_tensor(f"sct{i}", [128, H], F32))
               for i in range(2)]
        rsc = sA.enter_context(nc.sbuf_tensor("rsc", [128, 176], F32))
        lt4 = sA.enter_context(nc.sbuf_tensor("lt4", [128, 4], I32))
        lt16 = sA.enter_context(nc.sbuf_tensor("lt16", [128, 16], I32))

        m_sq = [0] * 16
        m_xn = [0] * 16
        m_tp = [0] * 16
        m_var = [0] * 16
        m_rn = [0] * 16
        rot_i = 0
        xnsrc = [None] * 16
        rstd16 = [None] * 16
        mean16 = [None] * 16

        def ln_stats(tt):
            par = tt % 2
            if tt < 8:
                w = []
                if tt >= 2:
                    w = [(s_act, m_sq[tt - 2]), (s_dve, m_xn[tt - 2])]
                E("sync", dma(tritp[par][:], x_d[tt * 128:(tt + 1) * 128, :]),
                  waits=w, inc=s_xt[par], amt=16)
                src = tritp[par][:]
                ldw = [(s_xt[par], s_xt[par].n)]
            else:
                src = acc[:, (tt - 8) * D:(tt - 7) * D]
                ldw = [(s_xa, XA_ALL)]
            xnsrc[tt] = src
            sumc = cols[:, 8 * par + 0: 8 * par + 1]
            meanc = cols[:, 8 * par + 2: 8 * par + 3]
            msqc = cols[:, 8 * par + 3: 8 * par + 4]
            varc = cols[:, 8 * par + 4: 8 * par + 5]
            lnvc = cols[:, 8 * par + 5: 8 * par + 6]
            rstdc = cols[:, 8 * par + 6: 8 * par + 7]
            mean16[tt] = meanc
            rstd16[tt] = rstdc
            sq4 = [colsq[:, 8 * par + j: 8 * par + j + 1] for j in range(4)]
            s01 = colsq[:, 8 * par + 4: 8 * par + 5]
            s23 = colsq[:, 8 * par + 5: 8 * par + 6]
            sqs = colsq[:, 8 * par + 6: 8 * par + 7]

            aw = list(ldw)
            if tt >= 2:
                aw += [(s_dve, m_var[tt - 2])]
            for j in range(4):
                E("scalar", lambda e, s=src, j=j, a=sq4[j]: e.activation(
                    nlt[:, 0:512], s[:, j * 512:(j + 1) * 512], AF.Square, accum_out=a),
                  waits=(aw if j == 0 else ()), inc=(s_act if j == 3 else None))
            m_sq[tt] = s_act.n
            E("vector", lambda e, s=src, o=sumc: e.reduce_sum(o, s, axis=AX.X), waits=ldw)
            E("vector", lambda e, i=sumc, o=meanc: e.tensor_scalar(o, i, 1.0 / D, None, op0=OP.mult))
            E("vector", lambda e, i=meanc, o=msqc: e.tensor_tensor(o, i, i, op=OP.mult))
            E("vector", lambda e, a=sq4[0], b=sq4[1], o=s01: e.tensor_tensor(o, a, b, op=OP.add),
              waits=[(s_act, m_sq[tt])])
            E("vector", lambda e, a=sq4[2], b=sq4[3], o=s23: e.tensor_tensor(o, a, b, op=OP.add))
            E("vector", lambda e, a=s01, b=s23, o=sqs: e.tensor_tensor(o, a, b, op=OP.add))
            E("vector", lambda e, i=sqs, m=msqc, o=varc: e.scalar_tensor_tensor(
                o, i, 1.0 / D, m, op0=OP.mult, op1=OP.subtract), inc=s_dve)
            m_var[tt] = s_dve.n
            E("scalar", lambda e, i=varc, o=lnvc: e.activation(o, i, AF.Ln, bias=1e-5),
              waits=[(s_dve, m_var[tt]), (s_dve, EPS_SET)])
            E("scalar", lambda e, i=lnvc, o=rstdc: e.activation(o, i, AF.Exp, scale=-0.5),
              inc=s_act)
            m_rn[tt] = s_act.n

        def ln_xn(tt):
            xw = [(s_act, m_rn[tt])]
            if tt >= 1:
                xw += [(s_pe, m_tp[tt - 1])]
            E("vector", lambda e, s=xnsrc[tt], m=mean16[tt], r=rstd16[tt]: e.tensor_scalar(
                xn_tm[:], s, m, r, op0=OP.subtract, op1=OP.mult),
              waits=xw, inc=s_dve)
            m_xn[tt] = s_dve.n

        def ln_tp(tt):
            nonlocal rot_i
            xnr = xn.rearrange("p (c k) -> p c k", k=NKV)
            for gblk in range(4):
                r = rot_i % 4
                for j in range(4):
                    c = 4 * gblk + j
                    E("tensor", lambda e, c=c, r=r, j=j: e.transpose(
                        rot512[r][:, j * 128:(j + 1) * 128],
                        xn_tm[:, c * 128:(c + 1) * 128], identf[:]),
                      waits=([(s_dve, m_xn[tt]), rotg[r], (s_xc, XC_ALL)]
                             if j == 0 else ()),
                      inc=(s_pe if j == 3 else None))
                pn = s_pe.n
                E("scalar", lambda e, gblk=gblk, r=r, tt=tt: e.copy(
                    xnr[:, 4 * gblk:4 * gblk + 4, tt * 128:(tt + 1) * 128],
                    rot512[r][:, 0:512].rearrange("p (c k) -> p c k", k=128)),
                  waits=[(s_pe, pn)], inc=s_act)
                rotg[r] = (s_act, s_act.n)
                rot_i += 1
            m_tp[tt] = s_pe.n

        for tt in range(16):
            ln_stats(tt)
            if tt >= 1:
                ln_tp(tt - 1)
            ln_xn(tt)
        ln_tp(15)
        LN_ACT_DONE = s_act.n
        LN_PE_LAST = s_pe.n
        LN_XN_DVE = m_xn[15]

        for kc in range(4):
            r = rot_i % 4
            for c in range(ND):
                E("tensor", lambda e, c=c, kc=kc, r=r: e.matmul(
                    rot512[r][0:H, :], wf_sb[:, c * H:(c + 1) * H],
                    xn[:, c * NKV + kc * 512: c * NKV + (kc + 1) * 512],
                    start=(c == 0), stop=(c == ND - 1)),
                  waits=([(s_act, LN_ACT_DONE), (s_wc, WC_INIT), rotg[r]] if c == 0 else ()),
                  inc=(s_pe if c == ND - 1 else None))
            pn = s_pe.n
            E("vector", lambda e, r=r, kc=kc: e.tensor_scalar(
                u_fm[:, kc * 512:(kc + 1) * 512], rot512[r][0:H, :], bf_sb[:],
                None, op0=OP.add),
              waits=[(s_pe, pn), (s_xc, XC_ALL)], inc=s_dve)
            rotg[r] = (s_dve, s_dve.n)
            rot_i += 1
        un = s_dve.n
        E("scalar", lambda e: e.activation(wa[:], u_fm[:], AF.Abs), waits=[(s_dve, un)])
        E("scalar", lambda e: e.activation(wb[:], wa[:], AF.Exp, scale=-1.0))
        E("scalar", lambda e: e.activation(wa[:], wb[:], AF.Ln, bias=1.0), inc=s_act)
        an = s_act.n
        E("vector", lambda e: e.tensor_scalar(wb[:], u_fm[:], 0.0, None, op0=OP.min),
          waits=[(s_act, an)])
        E("vector", lambda e: e.tensor_tensor(u_fm[:], wa[:], wb[:], op=OP.subtract),
          inc=s_dve)
        nlogf_n = s_dve.n

        for t2 in range(ND):
            r = rot_i % 4
            E("tensor", lambda e, t2=t2, r=r: e.transpose(
                rot512[r][0:128, 0:H], u_fm[0:H, t2 * 128:(t2 + 1) * 128],
                identf[0:H, 0:H]),
              waits=[(s_dve, nlogf_n), rotg[r]], inc=s_pe)
            pn = s_pe.n
            E("scalar", lambda e, t2=t2, r=r: e.copy(nlt[:, t2 * H:(t2 + 1) * H],
                                                     rot512[r][0:128, 0:H]),
              waits=[(s_pe, pn)], inc=s_act)
            rotg[r] = (s_act, s_act.n)
            rot_i += 1
        NLT_DONE = s_act.n

        m_cum = [0] * ND
        for it in range(ND):
            par2 = it % 2
            if it == 0:
                gate = [(s_act, m_sq[14]), (s_dve, m_xn[14])]
            elif it == 1:
                gate = [(s_act, m_sq[15]), (s_dve, m_xn[15])]
            else:
                gate = [(s_pe, m_cum[it - 2])]
            E("sync", dma(tritp[par2][:, 0:1024], trit_d[it][:, 0:1024]), waits=gate,
              inc=s_tr[par2], amt=16)
            E("sync", dma(tritp[par2][:, 1024:2048], trit_d[it][:, 1024:2048]),
              inc=s_tr[par2], amt=16)
            trw = (s_tr[par2], s_tr[par2].n)
            r = rot_i % 4
            for jc in range(ND):
                E("tensor", lambda e, jc=jc, r=r, par2=par2: e.matmul(
                    rot512[r][:, 0:H], tritp[par2][:, jc * 128:(jc + 1) * 128],
                    nlt[:, jc * H:(jc + 1) * H],
                    start=(jc == 0), stop=(jc == ND - 1)),
                  waits=([trw, (s_act, NLT_DONE), rotg[r]] if jc == 0 else ()),
                  inc=(s_pe if jc == ND - 1 else None))
            m_cum[it] = s_pe.n
            par = it % 2
            E("vector", lambda e, r=r, par=par: e.tensor_copy(
                sct[par][:], rot512[r][:, 0:H]),
              waits=[(s_pe, m_cum[it])], inc=s_dve)
            rotg[r] = (s_dve, s_dve.n)
            dn = s_dve.n
            rot_i += 1
            r2 = rot_i % 4
            E("tensor", lambda e, par=par, r2=r2: e.transpose(
                rot512[r2][0:H, 0:128], sct[par][:], identf[:]),
              waits=[(s_dve, dn), rotg[r2]], inc=s_pe)
            pn = s_pe.n
            E("scalar", lambda e, r2=r2, it=it: e.copy(negc[:, it * 128:(it + 1) * 128],
                                                       rot512[r2][0:H, 0:128]),
              waits=[(s_pe, pn)], inc=s_act)
            rotg[r2] = (s_act, s_act.n)
            rot_i += 1
        NEGC_DONE = s_act.n
        PHASE_A_CUM = s_pe.n

        E("vector", lambda e: e.tensor_copy(chi[:], negc[:]), waits=[(s_act, NEGC_DONE)])
        E("vector", lambda e: e.tensor_tensor(wa[:], negc[:], chi[:], op=OP.subtract))
        E("vector", lambda e: e.tensor_copy(cmid[:], wa[:]))
        E("vector", lambda e: e.tensor_tensor(wb[:], wa[:], cmid[:], op=OP.subtract))
        E("vector", lambda e: e.tensor_copy(clo[:], wb[:]))
        E("vector", lambda e: e.tensor_scalar(negm[:], negc[:, NQ:NKV], -1.0, -BBND,
                                              op0=OP.mult, op1=OP.add), inc=s_dve)
        C_DONE = s_dve.n

        scp = [rsc[:, 0:16], rsc[:, 96:112], rsc[:, 112:128], rsc[:, 128:144]]
        c1p = [rsc[:, 144 + i:145 + i] for i in range(4)]
        c2p = [rsc[:, 148 + i:149 + i] for i in range(4)]
        gs = rsc[:, 16:20]
        pen = rsc[:, 20:24]
        msk = rsc[:, 24:40]
        cur = rsc[:, 40:56]
        c0 = rsc[:, 56:57]
        cc1 = rsc[:, 59:60]
        gs2 = rsc[:, 60:64]
        ltm = rsc[:, 64:80]
        cur2 = rsc[:, 80:96]
        m_comb = [0] * 8
        for tq in range(8):
            sc = scp[tq % 4]
            c1 = c1p[tq % 4]
            c2 = c2p[tq % 4]
            r = rot_i % 4
            ps = rot512[r][:, 0:16]
            for c in range(ND):
                E("tensor", lambda e, c=c, tq=tq, ps=ps: e.matmul(
                    ps, xn[:, c * NKV + NQ + tq * 128: c * NKV + NQ + (tq + 1) * 128],
                    wg_sb[:, c * NE:(c + 1) * NE],
                    start=(c == 0), stop=(c == ND - 1)),
                  waits=([(s_act, LN_ACT_DONE), (s_wc, WC_INIT), rotg[r]] if c == 0 else ()),
                  inc=(s_pe if c == ND - 1 else None))
            rn = s_pe.n
            ew = [(s_pe, rn)]
            if tq >= 4:
                ew += [(s_dve, m_comb[tq - 4])]
            E("scalar", lambda e, ps=ps, sc=sc, c1=c1: e.activation(
                sc, ps, AF.Exp, accum_out=c1), waits=ew, inc=s_act)
            en = s_act.n
            rotg[r] = (s_act, en)
            rot_i += 1
            E("vector", lambda e, c1=c1, c2=c2: e.reciprocal(c2, c1), waits=[(s_act, en)])
            E("vector", lambda e, sc=sc, c2=c2: e.tensor_scalar(sc, sc, c2, None, op0=OP.mult))
            E("vector", lambda e, sc=sc: e.tensor_reduce(
                gs, sc.rearrange("p (g e) -> p g e", e=4), axis=AX.X, op=OP.max))
            E("vector", lambda e: e.reduce_max(c0, gs, axis=AX.X))
            E("vector", lambda e: e.tensor_scalar(lt4[:], gs, c0, None, op0=OP.is_lt))
            E("vector", lambda e: e.select(gs2, lt4[:], gs, neg4[:, 0:4], add_drain=True))
            E("vector", lambda e: e.reduce_max(cc1, gs2, axis=AX.X))
            E("vector", lambda e: e.tensor_scalar(pen, gs, cc1, None, op0=OP.subtract))
            E("vector", lambda e: e.tensor_scalar(pen, pen, 0.0, 30000.0,
                                                  op0=OP.min, op1=OP.mult))
            for g in range(4):
                E("vector", lambda e, g=g, sc=sc: e.tensor_scalar(
                    msk[:, 4 * g:4 * g + 4], sc[:, 4 * g:4 * g + 4], pen[:, g:g + 1],
                    None, op0=OP.add))
            E("vector", lambda e: e.tensor_copy(cur, msk))
            bufs = [cur, cur2]
            for rnd in range(3):
                a, b = bufs[rnd % 2], bufs[(rnd + 1) % 2]
                E("vector", lambda e, a=a: e.reduce_max(c0, a, axis=AX.X))
                E("vector", lambda e, a=a: e.tensor_scalar(lt16[:], a, c0, None, op0=OP.is_lt))
                E("vector", lambda e, a=a, b=b: e.select(b, lt16[:], a, neg4[:], add_drain=True))
            E("vector", lambda e: e.reduce_max(c0, cur2, axis=AX.X))
            E("vector", lambda e: e.tensor_scalar(cur, msk, c0, None, op0=OP.is_ge))
            E("vector", lambda e, tq=tq, sc=sc: e.tensor_tensor(
                comb[:, tq * 16:(tq + 1) * 16], sc, cur, op=OP.mult), inc=s_dve)
            m_comb[tq] = s_dve.n
        ROUTER_DONE = s_dve.n
        ROUTER_ACT = s_act.n
        PHASE_A_PE = s_pe.n
        if dbg:
            E("gpsimd", dma(dxn_d[:], xn[:]), waits=[(s_act, LN_ACT_DONE)], inc=s_dbg, amt=16)
            E("gpsimd", dma(dnegc_d[:], negc[:]), waits=[(s_act, NEGC_DONE)], inc=s_dbg, amt=16)
            E("gpsimd", dma(dcbf_d[0:H, :], chi[:]), waits=[(s_dve, C_DONE)], inc=s_dbg, amt=16)
            E("gpsimd", dma(dcbf_d[H:2 * H, :], cmid[:]), inc=s_dbg, amt=16)
            E("gpsimd", dma(dcbf_d[2 * H:3 * H, :], clo[:]), inc=s_dbg, amt=16)
            E("gpsimd", dma(dnegm_d[:], negm[:]), waits=[(s_dve, C_DONE)], inc=s_dbg, amt=16)
            E("gpsimd", dma(dcomb_d[:], comb[:]), waits=[(s_dve, ROUTER_DONE)], inc=s_dbg, amt=16)

    A_FREE = [(s_pe, PHASE_A_PE), (s_dve, ROUTER_DONE), (s_act, ROUTER_ACT)]

    with contextlib.ExitStack() as sB:
        pnl = sB.enter_context(nc.sbuf_tensor("pnl", [128, 3 * 2048], BF16))
        kaug = [sB.enter_context(nc.sbuf_tensor(f"kaug{i}", [68, NKV], BF16))
                for i in range(2)]
        vaug = sB.enter_context(nc.sbuf_tensor("vaug", [128, 16 * 130], BF16))
        qaug = [sB.enter_context(nc.sbuf_tensor(f"qaug{i}", [68, NQ], BF16))
                for i in range(2)]
        pt = [sB.enter_context(nc.sbuf_tensor(f"pt{i}", [128, 2048], BF16))
              for i in range(2)]
        atm = [sB.enter_context(nc.sbuf_tensor(f"atm{i}", [128, 64], F32))
               for i in range(2)]
        afm = sB.enter_context(nc.sbuf_tensor("afm", [128, 8 * NQ], BF16))
        linv = sB.enter_context(nc.sbuf_tensor("linv", [128, 2], F32))

        T2 = [(s_pe, PHASE_A_CUM), (s_dve, C_DONE), (s_act, NEGC_DONE)]
        E("gpsimd", dma(kaug[0][67:68, :], ones_d[0:1, :]), waits=T2, inc=s_wc, amt=16)
        E("gpsimd", dma(kaug[1][67:68, :], ones_d[0:1, :]), inc=s_wc, amt=16)
        E("gpsimd", dma(qaug[0][64:67, :], ones_d[0:3, 0:NQ]), inc=s_wc, amt=16)
        E("gpsimd", dma(qaug[1][64:67, :], ones_d[0:3, 0:NQ]), inc=s_wc, amt=16)
        ONES_ALL = s_wc.n
        E("vector", lambda e: e.memset(
            vaug.rearrange("p (c k) -> p c k", k=65)[:, :, 64:65], 1.0),
          waits=[(s_act, ROUTER_ACT)], inc=s_dve)
        VONES = s_dve.n

        NIT = NPAIR * 16
        m_qk = [0] * NIT
        m_exp = [0] * NIT
        m_av = [0] * NIT
        m_avsc = [0] * NIT
        m_attnT = [0] * NIT
        m_afe = [0] * NIT
        m_projdone = [0] * NPAIR
        m_qkdone = [0] * NPAIR
        m_projevac = [0] * NPAIR

        def emit_panels(p, gate):
            for j in range(3):
                for s2 in range(2):
                    E("gpsimd", dma(
                        pnl[:, j * 2048 + s2 * 1024: j * 2048 + (s2 + 1) * 1024],
                        wqkv_d[j * 16 + p][:, s2 * 1024:(s2 + 1) * 1024]),
                      waits=(gate if (j == 0 and s2 == 0) else ()), inc=s_wp, amt=16)

        def emit_crows(p, gate):
            h0, h1 = 2 * p, 2 * p + 1
            for kb_, h in ((kaug[0], h0), (kaug[1], h1)):
                E("gpsimd", dma(kb_[64:65, :], chi[h:h + 1, :]),
                  waits=gate, inc=s_cr, amt=16)
                gate = ()
                E("gpsimd", dma(kb_[65:66, :], cmid[h:h + 1, :]), inc=s_cr, amt=16)
                E("gpsimd", dma(kb_[66:67, :], clo[h:h + 1, :]), inc=s_cr, amt=16)
            E("gpsimd", dma(qaug[0][67:68, :], negm[h0:h0 + 1, :]), inc=s_cr, amt=16)
            E("gpsimd", dma(qaug[1][67:68, :], negm[h1:h1 + 1, :]), inc=s_cr, amt=16)

        emit_panels(0, [(s_pe, LN_PE_LAST), (s_dve, LN_XN_DVE),
                        (s_act, LN_ACT_DONE)])
        emit_crows(0, T2)

        for p in range(NPAIR):
            first_w = [(s_wp, 96 * (p + 1)), (s_wc, ONES_ALL)]
            if p > 0:
                first_w += [(s_act, m_afe[p * 16 - 1])]
            for kc in range(4):
                r = rot_i % 4
                for c in range(ND):
                    E("tensor", lambda e, c=c, kc=kc, r=r: e.matmul(
                        rot512[r], pnl[:, 2048 + c * 128: 2048 + (c + 1) * 128],
                        xn[:, c * NKV + kc * 512: c * NKV + (kc + 1) * 512],
                        start=(c == 0), stop=(c == ND - 1)),
                      waits=((first_w + [rotg[r]]) if c == 0 else ()),
                      inc=(s_pe if c == ND - 1 else None))
                first_w = []
                pn = s_pe.n
                E("vector", lambda e, r=r, kc=kc: e.tensor_copy(
                    kaug[0][0:64, kc * 512:(kc + 1) * 512], rot512[r][0:64, :]),
                  waits=[(s_pe, pn)])
                E("vector", lambda e, r=r, kc=kc: e.tensor_copy(
                    kaug[1][0:64, kc * 512:(kc + 1) * 512], rot512[r][64:128, :]),
                  inc=s_dve)
                rotg[r] = (s_dve, s_dve.n)
                rot_i += 1
            for kt in range(16):
                r = rot_i % 4
                ps = rot512[r][:, 0:128]
                for c in range(ND):
                    E("tensor", lambda e, c=c, kt=kt, ps=ps: e.matmul(
                        ps, xn[:, c * NKV + kt * 128: c * NKV + (kt + 1) * 128],
                        pnl[:, 4096 + c * 128: 4096 + (c + 1) * 128],
                        start=(c == 0), stop=(c == ND - 1)),
                      waits=([rotg[r], (s_dve, VONES)] if c == 0 else ()),
                      inc=(s_pe if c == ND - 1 else None))
                pn = s_pe.n
                E("vector", lambda e, ps=ps, kt=kt: e.tensor_copy(
                    vaug.rearrange("p (c k) -> p c k", k=65)[:, 2 * kt:2 * kt + 2, 0:64],
                    ps.rearrange("p (c k) -> p c k", k=64)),
                  waits=[(s_pe, pn)], inc=s_dve)
                rotg[r] = (s_dve, s_dve.n)
                rot_i += 1
            for qc in range(2):
                r = rot_i % 4
                for c in range(ND):
                    E("tensor", lambda e, c=c, qc=qc, r=r: e.matmul(
                        rot512[r], pnl[:, c * 128:(c + 1) * 128],
                        xn[:, c * NKV + NQ + qc * 512: c * NKV + NQ + (qc + 1) * 512],
                        start=(c == 0), stop=(c == ND - 1)),
                      waits=([rotg[r]] if c == 0 else ()),
                      inc=(s_pe if c == ND - 1 else None))
                pn = s_pe.n
                E("vector", lambda e, r=r, qc=qc: e.tensor_scalar(
                    qaug[0][0:64, qc * 512:(qc + 1) * 512], rot512[r][0:64, :],
                    0.125, None, op0=OP.mult),
                  waits=[(s_pe, pn)])
                E("vector", lambda e, r=r, qc=qc: e.tensor_scalar(
                    qaug[1][0:64, qc * 512:(qc + 1) * 512], rot512[r][64:128, :],
                    0.125, None, op0=OP.mult),
                  inc=s_dve)
                rotg[r] = (s_dve, s_dve.n)
                rot_i += 1
            m_projdone[p] = s_pe.n
            m_projevac[p] = s_dve.n
            if p + 1 < NPAIR and p != 7:
                emit_panels(p + 1, [(s_pe, m_projdone[p])])

            QTORD = [(h, qt) for h in range(2) for qt in (7, 0, 6, 1, 5, 2, 4, 3)]

            def emit_qk(g):
                h, qt = QTORD[g % 16]
                ka, qa = kaug[h], qaug[h]
                par = g % 2
                T = SA if par == 0 else SB
                live = 2 * (qt + 1) * 128
                w0 = [(s_cr, 128 * (g // 16 + 1)), (s_dve, m_projevac[g // 16])]
                if g >= 3:
                    w0 += [(s_act, m_afe[g - 3])]
                if g >= 2:
                    w0 += [(s_dve, m_avsc[g - 2])]
                for kt in range(qt + 1):
                    E("tensor", lambda e, kt=kt, ka=ka, qa=qa, qt=qt, T=T: e.matmul(
                        T[:, kt * 128:(kt + 1) * 128], ka[:, kt * 128:(kt + 1) * 128],
                        qa[:, qt * 128:(qt + 1) * 128], start=True, stop=(kt != qt)),
                      waits=(w0 if kt == 0 else ()))
                    w0 = []
                E("tensor", lambda e, qt=qt, T=T: e.matmul(
                    T[:, qt * 128:(qt + 1) * 128], identb[:], odiag[:],
                    start=False, stop=True))
                for j in range(qt + 1):
                    E("tensor", lambda e, j=j, ka=ka, qa=qa, qt=qt, T=T: e.matmul(
                        T[:, (qt + 1 + j) * 128: (qt + 2 + j) * 128],
                        ka[:, NQ + j * 128: NQ + (j + 1) * 128],
                        qa[:, qt * 128:(qt + 1) * 128], start=True, stop=(j != qt)))
                E("tensor", lambda e, qt=qt, T=T: e.matmul(
                    T[:, (2 * qt + 1) * 128: (2 * qt + 2) * 128], identb[:], trium[:],
                    start=False, stop=True), inc=s_pe)
                m_qk[g] = s_pe.n
                par2 = g % 2
                ew = [(s_pe, m_qk[g])]
                if g >= 2:
                    ew += [(s_pe, m_av[g - 2])]
                if dbg and g == gtap + 2:
                    ew += [(s_dbg2, 16)]
                E("scalar", lambda e, par=par2, live=live, T=T: e.activation(
                    pt[par][:, 0:live], T[:, 0:live], AF.Exp),
                  waits=ew, inc=s_act)
                m_exp[g] = s_act.n
                if dbg and g == gtap:
                    E("gpsimd", dma(dpt_d[:], pt[par2][:, 0:1152]),
                      waits=[(s_act, m_exp[g])], inc=s_dbg2, amt=16)
                    E("gpsimd", dma(dvau_d[:], vaug[:]),
                      waits=[(s_pe, m_qk[g])], inc=s_dbg2, amt=16)

            def emit_av(g):
                h, qt = QTORD[g % 16]
                par = g % 2
                T = SA if par == 0 else SB
                pav = T[:, 0:65]
                aw = [(s_act, m_exp[g])]
                if g >= 2:
                    aw += [(s_dve, m_avsc[g - 2])]
                nch = 2 * (qt + 1)
                for jj in range(nch):
                    vidx = jj if jj <= qt else 8 + (jj - (qt + 1))
                    E("tensor", lambda e, jj=jj, par=par, h=h, pav=pav, nch=nch,
                      vidx=vidx: e.matmul(
                        pav, pt[par][:, jj * 128:(jj + 1) * 128],
                        vaug[:, vidx * 130 + h * 65: vidx * 130 + h * 65 + 65],
                        start=(jj == 0), stop=(jj == nch - 1)),
                      waits=(aw if jj == 0 else ()),
                      inc=(s_pe if jj == nch - 1 else None))
                m_av[g] = s_pe.n
                lv = linv[:, par:par + 1]
                E("vector", lambda e, pav=pav, lv=lv: e.reciprocal(lv, pav[:, 64:65]),
                  waits=[(s_pe, m_av[g])])
                E("vector", lambda e, pav=pav, lv=lv, par=par: e.tensor_scalar(
                    atm[par][:], pav[:, 0:64], lv, None, op0=OP.mult),
                  waits=([(s_dbg3, 16)] if (dbg and g == gtap + 2) else []), inc=s_dve)
                m_avsc[g] = s_dve.n
                if dbg and g == gtap:
                    E("gpsimd", dma(datm_d[:], atm[par][:]),
                      waits=[(s_dve, m_avsc[g])], inc=s_dbg3, amt=16)

            def emit_tp(g):
                h, qt = QTORD[g % 16]
                par = g % 2
                Tother = SB if par == 0 else SA
                ptp = Tother[:, 512:640]
                tw = [(s_dve, m_avsc[g])]
                if g >= 2:
                    tw += [(s_act, m_afe[g - 2])]
                E("tensor", lambda e, ptp=ptp, par=par: e.transpose(
                    ptp[0:64, :], atm[par][:], identf[:]), waits=tw, inc=s_pe)
                m_attnT[g] = s_pe.n
                p_, qt_ = g // 16, qt
                E("scalar", lambda e, ptp=ptp, h=h, qt=qt_, p=p_: e.copy(
                    afm[h * 64:(h + 1) * 64,
                        (p % 8) * NQ + qt * 128:(p % 8) * NQ + (qt + 1) * 128],
                    ptp[0:64, :]),
                  waits=[(s_pe, m_attnT[g])], inc=s_act)
                m_afe[g] = s_act.n

            for i in range(16):
                g = p * 16 + i
                emit_qk(g)
                if i >= 1:
                    emit_av(g - 1)
                if i >= 2:
                    emit_tp(g - 2)
            emit_av(p * 16 + 15)
            emit_tp(p * 16 + 14)
            emit_tp(p * 16 + 15)
            m_qkdone[p] = s_pe.n
            if p + 1 < NPAIR:
                emit_crows(p + 1, [(s_pe, m_qkdone[p])])

            if p in (7, 15):
                half = p // 8
                wou_base = s_wou.n
                m_woev = [[0] * 8 for _ in range(4)]
                wo_tgt = [0] * 32
                for dq in range(4):
                    for pc2 in range(8):
                        n = dq * 8 + pc2
                        slot = n % 12
                        dchunk = half * 8 + pc2
                        if n == 0:
                            gate = [(s_pe, m_projdone[p])]
                        elif n < 12:
                            gate = ()
                        else:
                            gate = [(s_wou, wou_base + n - 12 + 1)]
                        E("gpsimd", dma(pnl[:, slot * 512:(slot + 1) * 512],
                                        wo_d[dchunk * 128:(dchunk + 1) * 128,
                                             dq * 512:(dq + 1) * 512]),
                          waits=gate, inc=s_wo[slot], amt=16)
                        wo_tgt[n] = s_wo[slot].n
                    for qt in range(8):
                        r = rot_i % 4
                        ps = rot512[r]
                        for pc2 in range(8):
                            slot = (dq * 8 + pc2) % 12
                            w = []
                            if pc2 == 0:
                                w = [rotg[r]]
                                if dq == 0 and qt == 0:
                                    w += [(s_act, m_afe[p * 16 + 15]),
                                          (s_dve, m_avsc[p * 16 + 15])]
                            if qt == 0:
                                w = list(w) + [(s_wo[slot], wo_tgt[dq * 8 + pc2])]
                            lastmm = (pc2 == 7)
                            if qt == 7:
                                inc_s = s_wou
                            else:
                                inc_s = s_pe if lastmm else None
                            E("tensor", lambda e, pc2=pc2, qt=qt, dq=dq, ps=ps, slot=slot: e.matmul(
                                ps, afm[:, pc2 * NQ + qt * 128: pc2 * NQ + (qt + 1) * 128],
                                pnl[:, slot * 512:(slot + 1) * 512],
                                start=(pc2 == 0), stop=(pc2 == 7)),
                              waits=w,
                              inc=(inc_s if (qt == 7 or lastmm) else None))
                        if qt == 7:
                            sem, val = s_wou, s_wou.n
                        else:
                            sem, val = s_pe, s_pe.n
                        E("vector", lambda e, qt=qt, dq=dq, ps=ps: e.tensor_tensor(
                            acc[:, qt * D + dq * 512: qt * D + (dq + 1) * 512], ps,
                            acc[:, qt * D + dq * 512: qt * D + (dq + 1) * 512],
                            op=OP.add),
                          waits=[(sem, val)], inc=s_dve)
                        m_woev[dq][qt] = s_dve.n
                        rotg[r] = (s_dve, s_dve.n)
                        rot_i += 1
                WO_PE = s_pe.n
                WO_WOU = s_wou.n
                if p == 7:
                    emit_panels(8, [(s_wou, WO_WOU)])
        ATT_PE_DONE = s_pe.n
        ATT_DVE_DONE = s_dve.n
        ATT_ACT_DONE = s_act.n
        if dbg:
            E("gpsimd", dma(dafm_d[:], afm[:]), waits=[(s_act, m_afe[NIT - 1])],
              inc=s_dbg, amt=16)

    B_FREE = [(s_pe, ATT_PE_DONE), (s_dve, ATT_DVE_DONE), (s_act, ATT_ACT_DONE),
              (s_wou, s_wou.n)]
    if dbg:
        B_FREE = B_FREE + [(s_dbg, s_dbg.n)]

    sAB.close()
    with contextlib.ExitStack() as sC:
        w13 = [sC.enter_context(nc.sbuf_tensor(f"w13_{i}", [128, 2 * 2048], BF16))
               for i in range(2)]
        w2s = [[sC.enter_context(nc.sbuf_tensor(f"w2s_{i}_{pb}", [128, 512], BF16))
                for pb in range(2)] for i in range(NI)]
        hb = sC.enter_context(nc.sbuf_tensor("hb", [128, NI * NQ], BF16))
        silu = [sC.enter_context(nc.sbuf_tensor("silu0", [128, NQ], F32))] * 2

        NEXP = NE + 2
        m_h1 = [[0] * NI for _ in range(NEXP)]
        m_h3 = [[0] * NI for _ in range(NEXP)]
        m_silu = [[0] * NI for _ in range(NEXP)]
        m_mult = [[0] * NI for _ in range(NEXP)]

        def w2src(e_, i, q):
            if e_ < NE:
                return w2_d[e_, i * 128:(i + 1) * 128, q * 512:(q + 1) * 512]
            o = (e_ - NE) * INTER
            return ws2_d[o + i * 128: o + (i + 1) * 128, q * 512:(q + 1) * 512]

        def emit_w13(n, gate):
            par = n % 2
            for s2 in range(4):
                E("gpsimd", dma(w13[par][:, s2 * 1024:(s2 + 1) * 1024],
                                w13h_d[n][:, s2 * 1024:(s2 + 1) * 1024]),
                  waits=(gate if s2 == 0 else ()), inc=s_w13[par], amt=16)

        emit_w13(0, list(B_FREE))
        emit_w13(1, [])
        n13 = 2

        for e_ in range(NEXP):
            for i in range(NI):
                n = e_ * NI + i
                par = n % 2
                hw = [(s_w13[par], (n // 2 + 1) * 64)]
                if n > 0:
                    pe_, pi = divmod(n - 1, NI)
                    hw += [(s_act, m_silu[pe_][pi])]
                else:
                    hw += [(s_act, ATT_ACT_DONE)]
                for tc in range(2):
                    for c in range(ND):
                        E("tensor", lambda e, c=c, tc=tc, par=par: e.matmul(
                            SA[:, tc * 512:(tc + 1) * 512],
                            w13[par][:, c * 128:(c + 1) * 128],
                            xn[:, c * NKV + NQ + tc * 512: c * NKV + NQ + (tc + 1) * 512],
                            start=(c == 0), stop=(c == ND - 1)),
                          waits=(hw if (tc == 0 and c == 0) else ()),
                          inc=(s_pe if (tc == 1 and c == ND - 1) else None))
                m_h1[e_][i] = s_pe.n
                hw3 = []
                if n > 0:
                    pe_, pi = divmod(n - 1, NI)
                    hw3 = [(s_dve, m_mult[pe_][pi])]
                for tc in range(2):
                    for c in range(ND):
                        E("tensor", lambda e, c=c, tc=tc, par=par: e.matmul(
                            SB[:, tc * 512:(tc + 1) * 512],
                            w13[par][:, 2048 + c * 128: 2048 + (c + 1) * 128],
                            xn[:, c * NKV + NQ + tc * 512: c * NKV + NQ + (tc + 1) * 512],
                            start=(c == 0), stop=(c == ND - 1)),
                          waits=(hw3 if (tc == 0 and c == 0) else ()),
                          inc=(s_pe if (tc == 1 and c == ND - 1) else None))
                m_h3[e_][i] = s_pe.n
                while n13 < NEXP * NI and n13 % NI >= 2 and (n13 - 2) <= e_ * NI + i:
                    ge_, gi_ = divmod(n13 - 2, NI)
                    emit_w13(n13, [(s_pe, m_h3[ge_][gi_])])
                    n13 += 1
                sw = [(s_pe, m_h1[e_][i])]
                if n >= 1:
                    pe2, pi2 = divmod(n - 1, NI)
                    sw += [(s_dve, m_mult[pe2][pi2])]
                E("scalar", lambda e, par=par: e.activation(silu[par][:], SA[:, 0:1024],
                                                            AF.Silu),
                  waits=sw, inc=s_act)
                m_silu[e_][i] = s_act.n
                E("vector", lambda e, i=i, par=par: e.tensor_tensor(
                    hb[:, i * NQ:(i + 1) * NQ], SB[:, 0:1024], silu[par][:],
                    op=OP.mult),
                  waits=[(s_act, m_silu[e_][i]), (s_pe, m_h3[e_][i])], inc=s_dve)
                m_mult[e_][i] = s_dve.n

            w2_tgt = [0] * NI
            for q in range(4):
                seq = e_ * 4 + q
                pb = seq % 2
                for i in range(NI):
                    if seq == 0:
                        gate = B_FREE if i == 0 else ()
                    elif seq == 1:
                        gate = ()
                    else:
                        gate = [(s_w2u, (seq - 2) * NI + i + 1)]
                    E("gpsimd", dma(w2s[i][pb][:], w2src(e_, i, q)), waits=gate,
                      inc=s_w2s[i][pb], amt=16)
                    w2_tgt[i] = s_w2s[i][pb].n
                for tt in range(8):
                    r = rot_i % 4
                    ps = rot512[r]
                    for i in range(NI):
                        w = []
                        if i == 0:
                            w = [rotg[r], (s_dve, m_mult[e_][NI - 1])]
                        if tt == 0:
                            w = list(w) + [(s_w2s[i][pb], w2_tgt[i])]
                        if qt_inc := (tt == 7):
                            inc_s = s_w2u
                        else:
                            inc_s = s_pe if i == NI - 1 else None
                        E("tensor", lambda e, i=i, tt=tt, ps=ps, pb=pb: e.matmul(
                            ps, hb[:, i * NQ + tt * 128: i * NQ + (tt + 1) * 128],
                            w2s[i][pb][:], start=(i == 0), stop=(i == NI - 1)),
                          waits=w,
                          inc=(inc_s if (tt == 7 or i == NI - 1) else None))
                    if tt == 7:
                        sem, val = s_w2u, s_w2u.n
                    else:
                        sem, val = s_pe, s_pe.n
                    scal = comb[:, tt * 16 + e_: tt * 16 + e_ + 1] if e_ < NE else 1.0
                    E("vector", lambda e, tt=tt, q=q, ps=ps, scal=scal: e.scalar_tensor_tensor(
                        acc[:, tt * D + q * 512: tt * D + (q + 1) * 512], ps, scal,
                        acc[:, tt * D + q * 512: tt * D + (q + 1) * 512],
                        op0=OP.mult, op1=OP.add),
                      waits=[(sem, val)], inc=s_dve)
                    rotg[r] = (s_dve, s_dve.n)
                    rot_i += 1
            while n13 < NEXP * NI and n13 % NI < 2:
                ge_, gi_ = divmod(n13 - 2, NI)
                emit_w13(n13, [(s_pe, m_h3[ge_][gi_])])
                n13 += 1
        MOE_DVE_DONE = s_dve.n

    if dbg:
        E("gpsimd", dma(dacc_d[:], acc[:]), waits=[(s_dve, MOE_DVE_DONE)],
          inc=s_dbg, amt=16)
        E("gpsimd", None, waits=[(s_dbg, s_dbg.n), (s_dbg2, s_dbg2.n), (s_dbg3, s_dbg3.n)])

    for tt in range(8):
        E("sync", dma(out_d[tt * 128:(tt + 1) * 128, :], acc[:, tt * D:(tt + 1) * D]),
          waits=([(s_dve, MOE_DVE_DONE)] if tt == 0 else ()), inc=s_out, amt=16)
    E("sync", None, waits=[(s_out, s_out.n)])

    kb.replay()
    nc.finalize()
    return nc


_CACHE = {}


def _host_prep(inputs):
    f = np.float32

    def panelize(w):
        wT = np.asarray(w, f).T
        return wT.reshape(16, 128, 16, 128).transpose(2, 1, 0, 3).reshape(16, 128, 2048)

    def blk13(w, ws):
        a = np.asarray(w, f).transpose(0, 2, 1).reshape(NE, 16, 128, NI, 128) \
            .transpose(0, 3, 2, 1, 4).reshape(NE * NI, 128, 2048)
        b = np.asarray(ws, f).T.reshape(16, 128, 2 * NI, 128) \
            .transpose(2, 1, 0, 3).reshape(2 * NI, 128, 2048)
        return np.concatenate([a, b], axis=0)

    wqkv = np.ascontiguousarray(np.concatenate(
        [panelize(inputs["L0_attn_Wq"]), panelize(inputs["L0_attn_Wk"]),
         panelize(inputs["L0_attn_Wv"])], axis=0))
    w13h = np.ascontiguousarray(np.concatenate(
        [blk13(inputs["L0_ffn_W1"], inputs["L0_ffn_Ws1"]),
         blk13(inputs["L0_ffn_W3"], inputs["L0_ffn_Ws3"])], axis=2))
    shared = dict(
        wqkv_p=wqkv,
        w13h=w13h,
        woT=np.ascontiguousarray(inputs["L0_attn_Wo"].T, dtype=f),
        wfT=np.ascontiguousarray(inputs["L0_attn_Wf"].T, dtype=f),
        bf_col=np.ascontiguousarray(np.asarray(inputs["L0_attn_bf"]).reshape(H, 1), dtype=f),
        wgT=np.ascontiguousarray(inputs["L0_ffn_Wg"].T, dtype=f),
        w2T=np.ascontiguousarray(np.asarray(inputs["L0_ffn_W2"]).transpose(0, 2, 1), dtype=f),
        ws2T=np.ascontiguousarray(inputs["L0_ffn_Ws2"].T, dtype=f),
        trium=np.ascontiguousarray(np.tril(np.full((128, 128), BNEG, f), -1).astype(_BF)),
        identb=np.ascontiguousarray(np.eye(128, dtype=_BF)),
        identf=np.ascontiguousarray(np.eye(128, dtype=f)),
        ones_rows=np.ones((3, NKV), dtype=_BF),
    )
    idx = np.arange(2048).reshape(16, 128)
    tp = {}
    for s in (0, 1):
        other = idx[(1 - s)::2].reshape(-1)
        own = idx[s::2].reshape(-1)
        tp[s] = np.concatenate([other, own])
    trit = {s: np.ascontiguousarray(
                (tp[s][:, None] <= tp[s][None, :]).astype(f)
                .reshape(16, 128, 16, 128).transpose(2, 1, 0, 3).reshape(16, 128, 2048))
            for s in (0, 1)}
    odiag = {0: np.full((128, 128), BNEG, dtype=_BF),
             1: np.zeros((128, 128), dtype=_BF)}

    x = np.asarray(inputs["L0_attn_x"])
    in_maps = []
    for c in range(8):
        b, s = divmod(c, 2)
        m = dict(shared)
        m["x_all"] = np.ascontiguousarray(x[b][tp[s]], dtype=f)
        m["trit"] = trit[s]
        m["odiag"] = odiag[s]
        in_maps.append(m)
    return in_maps, tp


def kernel(**inputs):
    if "nc" not in _CACHE:
        _CACHE["nc"] = build_program()
    nc = _CACHE["nc"]
    in_maps, tp = _host_prep(inputs)
    res = run_bass_kernel_spmd(nc, in_maps, core_ids=list(range(8)))
    out = np.empty((4, 2048, 2048), np.float32)
    for c in range(8):
        b, s = divmod(c, 2)
        out[b][tp[s][1024:]] = res.results[c]["out"]
    return out


# version 21
# speedup vs baseline: 1.1521x; 1.0285x over previous
import contextlib
import numpy as np
import ml_dtypes
import concourse.bass as bass
import concourse.mybir as mybir
from concourse.alu_op_type import AluOpType
from concourse.bass_utils import run_bass_kernel_spmd

F32 = mybir.dt.float32
I32 = mybir.dt.int32
BF16 = mybir.dt.bfloat16
AF = mybir.ActivationFunctionType
AX = mybir.AxisListType
OP = AluOpType

BNEG = -30000.0
BBND = 50.0

D = 2048
ND = 16
H = 32
NPAIR = 16
NQ = 1024
NKV = 2048
INTER = 1408
NI = 11
NE = 16
SH_INTER = 2816

_BF = ml_dtypes.bfloat16


class Cnt:
    def __init__(self, nc, name):
        self.h = nc.alloc_semaphore(name)
        self.n = 0


class KB:

    def __init__(self, simcheck=False):
        self.nc = bass.Bass("TRN2", target_bir_lowering=False,
                            detect_race_conditions=simcheck)
        self.events = {k: [] for k in ("sync", "gpsimd", "tensor", "vector", "scalar")}
        self.simcheck = simcheck
        self.selfsem = {}

    def sem(self, name):
        return Cnt(self.nc, name)

    def ev(self, eng, fn, waits=(), inc=None, amt=1):
        waits = list(waits)
        if eng in self.selfsem and fn is not None:
            ss = self.selfsem[eng]
            waits.append((ss, ss.n))
            if inc is None:
                inc = ss
        w = tuple((s.h, int(v)) for (s, v) in waits if s is not None and v > 0)
        ih = None
        if inc is not None:
            inc.n += amt
            ih = (inc.h, amt)
        self.events[eng].append((w, fn, ih))

    def replay(self):
        nc = self.nc
        with nc.Block() as blk:
            for name, deco in (("sync", blk.sync), ("gpsimd", blk.gpsimd),
                               ("tensor", blk.tensor), ("vector", blk.vector),
                               ("scalar", blk.scalar)):
                evs = self.events[name]

                def make(evs):
                    def body(eng):
                        for w, fn, ih in evs:
                            for h, v in w:
                                eng.wait_ge(h, v)
                            inst = fn(eng) if fn is not None else None
                            if ih is not None:
                                inst.then_inc(*ih)
                    return body

                deco(make(evs))


def build_program(simcheck=False, dbg=False, gtap=0):
    kb = KB(simcheck)
    nc = kb.nc
    E = kb.ev

    x_d = nc.dram_tensor("x_all", [NKV, D], F32, kind="ExternalInput")
    wqkv_d = nc.dram_tensor("wqkv_p", [48, 128, 2048], F32, kind="ExternalInput")
    wo_d = nc.dram_tensor("woT", [D, D], F32, kind="ExternalInput")
    wf_d = nc.dram_tensor("wfT", [D, H], F32, kind="ExternalInput")
    bf_d = nc.dram_tensor("bf_col", [H, 1], F32, kind="ExternalInput")
    wg_d = nc.dram_tensor("wgT", [D, NE], F32, kind="ExternalInput")
    w13h_d = nc.dram_tensor("w13h", [(NE + 2) * NI, 128, 4096], F32, kind="ExternalInput")
    w2_d = nc.dram_tensor("w2T", [NE, INTER, D], F32, kind="ExternalInput")
    ws2_d = nc.dram_tensor("ws2T", [SH_INTER, D], F32, kind="ExternalInput")
    trit_d = nc.dram_tensor("trit", [16, 128, 2048], F32, kind="ExternalInput")
    odiag_d = nc.dram_tensor("odiag", [128, 128], BF16, kind="ExternalInput")
    trium_d = nc.dram_tensor("trium", [128, 128], BF16, kind="ExternalInput")
    identb_d = nc.dram_tensor("identb", [128, 128], BF16, kind="ExternalInput")
    identf_d = nc.dram_tensor("identf", [128, 128], F32, kind="ExternalInput")
    ones_d = nc.dram_tensor("ones_rows", [3, NKV], BF16, kind="ExternalInput")
    out_d = nc.dram_tensor("out", [NQ, D], F32, kind="ExternalOutput")
    if dbg:
        dxn_d = nc.dram_tensor("dxn", [128, ND * NKV], BF16, kind="ExternalOutput")
        dnegc_d = nc.dram_tensor("dnegc", [H, NKV], F32, kind="ExternalOutput")
        dcbf_d = nc.dram_tensor("dcbf", [3 * H, NKV], BF16, kind="ExternalOutput")
        dnegm_d = nc.dram_tensor("dnegm", [H, NQ], BF16, kind="ExternalOutput")
        dcomb_d = nc.dram_tensor("dcomb", [128, 128], F32, kind="ExternalOutput")
        dpt_d = nc.dram_tensor("dpt", [128, 1152], BF16, kind="ExternalOutput")
        datm_d = nc.dram_tensor("datm", [128, 64], F32, kind="ExternalOutput")
        dafm_d = nc.dram_tensor("dafm", [128, 8 * NQ], BF16, kind="ExternalOutput")
        dacc_d = nc.dram_tensor("dacc", [128, 8 * D], F32, kind="ExternalOutput")
        dvau_d = nc.dram_tensor("dvau", [128, 16 * 130], BF16, kind="ExternalOutput")
        s_dbg = kb.sem("s_dbg")
        s_dbg2 = kb.sem("s_dbg2")
        s_dbg3 = kb.sem("s_dbg3")

    SA = nc.alloc_psum_tensor("SA", [128, 2048], F32)
    SB = nc.alloc_psum_tensor("SB", [128, 2048], F32)
    rot512 = [SA[:, 1024:1536], SA[:, 1536:2048], SB[:, 1024:1536], SB[:, 1536:2048]]
    rotg = [(None, 0)] * 4

    acc = nc.alloc_sbuf_tensor("acc", [128, 8 * D], F32)
    xn = nc.alloc_sbuf_tensor("xn_fm", [128, ND * NKV], BF16)
    comb = nc.alloc_sbuf_tensor("comb", [128, 8 * 16], F32)
    trium = nc.alloc_sbuf_tensor("trium_s", [128, 128], BF16)
    identb = nc.alloc_sbuf_tensor("identb_s", [128, 128], BF16)
    identf = nc.alloc_sbuf_tensor("identf_s", [128, 128], F32)
    wg_sb = nc.alloc_sbuf_tensor("wg_sb", [128, ND * NE], BF16)
    wf_sb = nc.alloc_sbuf_tensor("wf_sb", [128, ND * H], BF16)
    bf_sb = nc.alloc_sbuf_tensor("bf_sb", [H, 1], F32)
    odiag = nc.alloc_sbuf_tensor("odiag_s", [128, 128], BF16)
    cols = nc.alloc_sbuf_tensor("cols", [128, 16], F32)
    colsq = nc.alloc_sbuf_tensor("colsq", [128, 16], F32)
    neg4 = nc.alloc_sbuf_tensor("neg4", [128, 16], F32)
    epst = nc.alloc_sbuf_tensor("epst", [128, 1], F32)
    nc.const_aps.register(F32, 1e-5, epst[:])
    sAB = contextlib.ExitStack()
    chi = sAB.enter_context(nc.sbuf_tensor("chi", [H, NKV], BF16))
    cmid = sAB.enter_context(nc.sbuf_tensor("cmid", [H, NKV], BF16))
    clo = sAB.enter_context(nc.sbuf_tensor("clo", [H, NKV], BF16))
    negm = sAB.enter_context(nc.sbuf_tensor("negm", [H, NQ], BF16))
    negc = sAB.enter_context(nc.sbuf_tensor("negc", [H, NKV], F32))

    s_pe = kb.sem("s_pe")
    s_dve = kb.sem("s_dve")
    s_act = kb.sem("s_act")
    kb.selfsem = {"vector": s_dve, "scalar": s_act}
    s_xa = kb.sem("s_xa")
    s_xt = [kb.sem(f"s_xt{i}") for i in range(2)]
    s_wc = kb.sem("s_wc")
    s_xc = kb.sem("s_xc")
    s_tr = [kb.sem(f"s_tr{i}") for i in range(2)]
    s_wp = kb.sem("s_wp")
    s_wo = [kb.sem(f"s_wo{i}") for i in range(12)]
    s_wou = kb.sem("s_wou")
    s_cr = kb.sem("s_cr")
    s_w13 = [kb.sem(f"s_w13_{i}") for i in range(2)]
    s_w2s = [[kb.sem(f"s_w2s_{i}_{pb}") for pb in range(2)] for i in range(NI)]
    s_w2u = kb.sem("s_w2u")
    s_out = kb.sem("s_out")

    def dma(dst, src):
        return lambda eng, d=dst, s=src: eng.dma_start(d, s)

    E("vector", lambda e: e.memset(neg4[:], -4.0))
    E("vector", lambda e: e.memset(epst[:], 1e-5), inc=s_dve)
    EPS_SET = s_dve.n
    E("gpsimd", dma(trium[:], trium_d[:]), inc=s_wc, amt=16)
    E("gpsimd", dma(odiag[:], odiag_d[:]), inc=s_wc, amt=16)
    E("gpsimd", dma(identb[:], identb_d[:]), inc=s_wc, amt=16)
    E("gpsimd", dma(wg_sb.rearrange("q (c g) -> q c g", g=NE),
                    wg_d.rearrange("(c q) g -> q c g", q=128)), inc=s_wc, amt=16)
    E("gpsimd", dma(wf_sb.rearrange("q (c h) -> q c h", h=H),
                    wf_d.rearrange("(c q) h -> q c h", q=128)), inc=s_wc, amt=16)
    WC_INIT = s_wc.n
    E("sync", dma(identf[:], identf_d[:]), inc=s_xc, amt=16)
    E("sync", dma(bf_sb[:], bf_d[:]), inc=s_xc, amt=16)
    XC_ALL = s_xc.n
    for tt in range(8):
        E("sync", dma(acc[:, tt * D:(tt + 1) * D], x_d[(8 + tt) * 128:(9 + tt) * 128, :]),
          inc=s_xa, amt=16)
    XA_ALL = s_xa.n

    with contextlib.ExitStack() as sA:
        tritp = [sA.enter_context(nc.sbuf_tensor(f"tritp{i}", [128, ND * 128], F32))
                 for i in range(2)]
        xn_tm = sA.enter_context(nc.sbuf_tensor("xn_tm", [128, D], F32))
        u_fm = sA.enter_context(nc.sbuf_tensor("u_fm", [H, NKV], F32))
        wa = sA.enter_context(nc.sbuf_tensor("wa", [H, NKV], F32))
        wb = sA.enter_context(nc.sbuf_tensor("wb", [H, NKV], F32))
        nlt = sA.enter_context(nc.sbuf_tensor("nlt", [128, ND * H], F32))
        sct = [sA.enter_context(nc.sbuf_tensor(f"sct{i}", [128, H], F32))
               for i in range(2)]
        rsc = sA.enter_context(nc.sbuf_tensor("rsc", [128, 176], F32))
        lt4 = sA.enter_context(nc.sbuf_tensor("lt4", [128, 4], I32))
        lt16 = sA.enter_context(nc.sbuf_tensor("lt16", [128, 16], I32))

        m_sq = [0] * 16
        m_xn = [0] * 16
        m_tp = [0] * 16
        m_var = [0] * 16
        m_rn = [0] * 16
        rot_i = 0
        xnsrc = [None] * 16
        rstd16 = [None] * 16
        mean16 = [None] * 16

        def ln_stats(tt):
            par = tt % 2
            if tt < 8:
                w = []
                if tt >= 2:
                    w = [(s_act, m_sq[tt - 2]), (s_dve, m_xn[tt - 2])]
                E("sync", dma(tritp[par][:], x_d[tt * 128:(tt + 1) * 128, :]),
                  waits=w, inc=s_xt[par], amt=16)
                src = tritp[par][:]
                ldw = [(s_xt[par], s_xt[par].n)]
            else:
                src = acc[:, (tt - 8) * D:(tt - 7) * D]
                ldw = [(s_xa, XA_ALL)]
            xnsrc[tt] = src
            sumc = cols[:, 8 * par + 0: 8 * par + 1]
            meanc = cols[:, 8 * par + 2: 8 * par + 3]
            msqc = cols[:, 8 * par + 3: 8 * par + 4]
            varc = cols[:, 8 * par + 4: 8 * par + 5]
            lnvc = cols[:, 8 * par + 5: 8 * par + 6]
            rstdc = cols[:, 8 * par + 6: 8 * par + 7]
            mean16[tt] = meanc
            rstd16[tt] = rstdc
            sq4 = [colsq[:, 8 * par + j: 8 * par + j + 1] for j in range(4)]
            s01 = colsq[:, 8 * par + 4: 8 * par + 5]
            s23 = colsq[:, 8 * par + 5: 8 * par + 6]
            sqs = colsq[:, 8 * par + 6: 8 * par + 7]

            aw = list(ldw)
            if tt >= 2:
                aw += [(s_dve, m_var[tt - 2])]
            for j in range(4):
                E("scalar", lambda e, s=src, j=j, a=sq4[j]: e.activation(
                    nlt[:, 0:512], s[:, j * 512:(j + 1) * 512], AF.Square, accum_out=a),
                  waits=(aw if j == 0 else ()), inc=(s_act if j == 3 else None))
            m_sq[tt] = s_act.n
            E("vector", lambda e, s=src, o=sumc: e.reduce_sum(o, s, axis=AX.X), waits=ldw)
            E("vector", lambda e, i=sumc, o=meanc: e.tensor_scalar(o, i, 1.0 / D, None, op0=OP.mult))
            E("vector", lambda e, i=meanc, o=msqc: e.tensor_tensor(o, i, i, op=OP.mult))
            E("vector", lambda e, a=sq4[0], b=sq4[1], o=s01: e.tensor_tensor(o, a, b, op=OP.add),
              waits=[(s_act, m_sq[tt])])
            E("vector", lambda e, a=sq4[2], b=sq4[3], o=s23: e.tensor_tensor(o, a, b, op=OP.add))
            E("vector", lambda e, a=s01, b=s23, o=sqs: e.tensor_tensor(o, a, b, op=OP.add))
            E("vector", lambda e, i=sqs, m=msqc, o=varc: e.scalar_tensor_tensor(
                o, i, 1.0 / D, m, op0=OP.mult, op1=OP.subtract), inc=s_dve)
            m_var[tt] = s_dve.n
            E("scalar", lambda e, i=varc, o=lnvc: e.activation(o, i, AF.Ln, bias=1e-5),
              waits=[(s_dve, m_var[tt]), (s_dve, EPS_SET)])
            E("scalar", lambda e, i=lnvc, o=rstdc: e.activation(o, i, AF.Exp, scale=-0.5),
              inc=s_act)
            m_rn[tt] = s_act.n

        def ln_xn(tt):
            xw = [(s_act, m_rn[tt])]
            if tt >= 1:
                xw += [(s_pe, m_tp[tt - 1])]
            E("vector", lambda e, s=xnsrc[tt], m=mean16[tt], r=rstd16[tt]: e.tensor_scalar(
                xn_tm[:], s, m, r, op0=OP.subtract, op1=OP.mult),
              waits=xw, inc=s_dve)
            m_xn[tt] = s_dve.n

        def ln_tp(tt):
            nonlocal rot_i
            xnr = xn.rearrange("p (c k) -> p c k", k=NKV)
            for gblk in range(4):
                r = rot_i % 4
                for j in range(4):
                    c = 4 * gblk + j
                    E("tensor", lambda e, c=c, r=r, j=j: e.transpose(
                        rot512[r][:, j * 128:(j + 1) * 128],
                        xn_tm[:, c * 128:(c + 1) * 128], identf[:]),
                      waits=([(s_dve, m_xn[tt]), rotg[r], (s_xc, XC_ALL)]
                             if j == 0 else ()),
                      inc=(s_pe if j == 3 else None))
                pn = s_pe.n
                E("scalar", lambda e, gblk=gblk, r=r, tt=tt: e.copy(
                    xnr[:, 4 * gblk:4 * gblk + 4, tt * 128:(tt + 1) * 128],
                    rot512[r][:, 0:512].rearrange("p (c k) -> p c k", k=128)),
                  waits=[(s_pe, pn)], inc=s_act)
                rotg[r] = (s_act, s_act.n)
                rot_i += 1
            m_tp[tt] = s_pe.n

        for tt in range(16):
            ln_stats(tt)
            if tt >= 1:
                ln_tp(tt - 1)
            ln_xn(tt)
        ln_tp(15)
        LN_ACT_DONE = s_act.n
        LN_PE_LAST = s_pe.n
        LN_XN_DVE = m_xn[15]

        for kc in range(4):
            r = rot_i % 4
            for c in range(ND):
                E("tensor", lambda e, c=c, kc=kc, r=r: e.matmul(
                    rot512[r][0:H, :], wf_sb[:, c * H:(c + 1) * H],
                    xn[:, c * NKV + kc * 512: c * NKV + (kc + 1) * 512],
                    start=(c == 0), stop=(c == ND - 1)),
                  waits=([(s_act, LN_ACT_DONE), (s_wc, WC_INIT), rotg[r]] if c == 0 else ()),
                  inc=(s_pe if c == ND - 1 else None))
            pn = s_pe.n
            E("vector", lambda e, r=r, kc=kc: e.tensor_scalar(
                u_fm[:, kc * 512:(kc + 1) * 512], rot512[r][0:H, :], bf_sb[:],
                None, op0=OP.add),
              waits=[(s_pe, pn), (s_xc, XC_ALL)], inc=s_dve)
            rotg[r] = (s_dve, s_dve.n)
            rot_i += 1
        un = s_dve.n
        E("scalar", lambda e: e.activation(wa[:], u_fm[:], AF.Abs), waits=[(s_dve, un)])
        E("scalar", lambda e: e.activation(wb[:], wa[:], AF.Exp, scale=-1.0))
        E("scalar", lambda e: e.activation(wa[:], wb[:], AF.Ln, bias=1.0), inc=s_act)
        an = s_act.n
        E("vector", lambda e: e.tensor_scalar(wb[:], u_fm[:], 0.0, None, op0=OP.min),
          waits=[(s_act, an)])
        E("vector", lambda e: e.tensor_tensor(u_fm[:], wa[:], wb[:], op=OP.subtract),
          inc=s_dve)
        nlogf_n = s_dve.n

        for t2 in range(ND):
            r = rot_i % 4
            E("tensor", lambda e, t2=t2, r=r: e.transpose(
                rot512[r][0:128, 0:H], u_fm[0:H, t2 * 128:(t2 + 1) * 128],
                identf[0:H, 0:H]),
              waits=[(s_dve, nlogf_n), rotg[r]], inc=s_pe)
            pn = s_pe.n
            E("scalar", lambda e, t2=t2, r=r: e.copy(nlt[:, t2 * H:(t2 + 1) * H],
                                                     rot512[r][0:128, 0:H]),
              waits=[(s_pe, pn)], inc=s_act)
            rotg[r] = (s_act, s_act.n)
            rot_i += 1
        NLT_DONE = s_act.n

        m_cum = [0] * ND
        for it in range(ND):
            par2 = it % 2
            if it == 0:
                gate = [(s_act, m_sq[14]), (s_dve, m_xn[14])]
            elif it == 1:
                gate = [(s_act, m_sq[15]), (s_dve, m_xn[15])]
            else:
                gate = [(s_pe, m_cum[it - 2])]
            E("sync", dma(tritp[par2][:, 0:1024], trit_d[it][:, 0:1024]), waits=gate,
              inc=s_tr[par2], amt=16)
            E("sync", dma(tritp[par2][:, 1024:2048], trit_d[it][:, 1024:2048]),
              inc=s_tr[par2], amt=16)
            trw = (s_tr[par2], s_tr[par2].n)
            r = rot_i % 4
            for jc in range(ND):
                E("tensor", lambda e, jc=jc, r=r, par2=par2: e.matmul(
                    rot512[r][:, 0:H], tritp[par2][:, jc * 128:(jc + 1) * 128],
                    nlt[:, jc * H:(jc + 1) * H],
                    start=(jc == 0), stop=(jc == ND - 1)),
                  waits=([trw, (s_act, NLT_DONE), rotg[r]] if jc == 0 else ()),
                  inc=(s_pe if jc == ND - 1 else None))
            m_cum[it] = s_pe.n
            par = it % 2
            E("vector", lambda e, r=r, par=par: e.tensor_copy(
                sct[par][:], rot512[r][:, 0:H]),
              waits=[(s_pe, m_cum[it])], inc=s_dve)
            rotg[r] = (s_dve, s_dve.n)
            dn = s_dve.n
            rot_i += 1
            r2 = rot_i % 4
            E("tensor", lambda e, par=par, r2=r2: e.transpose(
                rot512[r2][0:H, 0:128], sct[par][:], identf[:]),
              waits=[(s_dve, dn), rotg[r2]], inc=s_pe)
            pn = s_pe.n
            E("scalar", lambda e, r2=r2, it=it: e.copy(negc[:, it * 128:(it + 1) * 128],
                                                       rot512[r2][0:H, 0:128]),
              waits=[(s_pe, pn)], inc=s_act)
            rotg[r2] = (s_act, s_act.n)
            rot_i += 1
        NEGC_DONE = s_act.n
        PHASE_A_CUM = s_pe.n

        E("vector", lambda e: e.tensor_copy(chi[:], negc[:]), waits=[(s_act, NEGC_DONE)])
        E("vector", lambda e: e.tensor_tensor(wa[:], negc[:], chi[:], op=OP.subtract))
        E("vector", lambda e: e.tensor_copy(cmid[:], wa[:]))
        E("vector", lambda e: e.tensor_tensor(wb[:], wa[:], cmid[:], op=OP.subtract))
        E("vector", lambda e: e.tensor_copy(clo[:], wb[:]))
        E("vector", lambda e: e.tensor_scalar(negm[:], negc[:, NQ:NKV], -1.0, -BBND,
                                              op0=OP.mult, op1=OP.add), inc=s_dve)
        C_DONE = s_dve.n

        scp = [rsc[:, 0:16], rsc[:, 96:112], rsc[:, 112:128], rsc[:, 128:144]]
        c1p = [rsc[:, 144 + i:145 + i] for i in range(4)]
        c2p = [rsc[:, 148 + i:149 + i] for i in range(4)]
        gs = rsc[:, 16:20]
        pen = rsc[:, 20:24]
        msk = rsc[:, 24:40]
        cur = rsc[:, 40:56]
        c0 = rsc[:, 56:57]
        cc1 = rsc[:, 59:60]
        gs2 = rsc[:, 60:64]
        ltm = rsc[:, 64:80]
        cur2 = rsc[:, 80:96]
        m_comb = [0] * 8
        for tq in range(8):
            sc = scp[tq % 4]
            c1 = c1p[tq % 4]
            c2 = c2p[tq % 4]
            r = rot_i % 4
            ps = rot512[r][:, 0:16]
            for c in range(ND):
                E("tensor", lambda e, c=c, tq=tq, ps=ps: e.matmul(
                    ps, xn[:, c * NKV + NQ + tq * 128: c * NKV + NQ + (tq + 1) * 128],
                    wg_sb[:, c * NE:(c + 1) * NE],
                    start=(c == 0), stop=(c == ND - 1)),
                  waits=([(s_act, LN_ACT_DONE), (s_wc, WC_INIT), rotg[r]] if c == 0 else ()),
                  inc=(s_pe if c == ND - 1 else None))
            rn = s_pe.n
            ew = [(s_pe, rn)]
            if tq >= 4:
                ew += [(s_dve, m_comb[tq - 4])]
            E("scalar", lambda e, ps=ps, sc=sc, c1=c1: e.activation(
                sc, ps, AF.Exp, accum_out=c1), waits=ew, inc=s_act)
            en = s_act.n
            rotg[r] = (s_act, en)
            rot_i += 1
            E("vector", lambda e, c1=c1, c2=c2: e.reciprocal(c2, c1), waits=[(s_act, en)])
            E("vector", lambda e, sc=sc, c2=c2: e.tensor_scalar(sc, sc, c2, None, op0=OP.mult))
            E("vector", lambda e, sc=sc: e.tensor_reduce(
                gs, sc.rearrange("p (g e) -> p g e", e=4), axis=AX.X, op=OP.max))
            E("vector", lambda e: e.reduce_max(c0, gs, axis=AX.X))
            E("vector", lambda e: e.tensor_scalar(lt4[:], gs, c0, None, op0=OP.is_lt))
            E("vector", lambda e: e.select(gs2, lt4[:], gs, neg4[:, 0:4], add_drain=True))
            E("vector", lambda e: e.reduce_max(cc1, gs2, axis=AX.X))
            E("vector", lambda e: e.tensor_scalar(pen, gs, cc1, None, op0=OP.subtract))
            E("vector", lambda e: e.tensor_scalar(pen, pen, 0.0, 30000.0,
                                                  op0=OP.min, op1=OP.mult))
            for g in range(4):
                E("vector", lambda e, g=g, sc=sc: e.tensor_scalar(
                    msk[:, 4 * g:4 * g + 4], sc[:, 4 * g:4 * g + 4], pen[:, g:g + 1],
                    None, op0=OP.add))
            E("vector", lambda e: e.tensor_copy(cur, msk))
            bufs = [cur, cur2]
            for rnd in range(3):
                a, b = bufs[rnd % 2], bufs[(rnd + 1) % 2]
                E("vector", lambda e, a=a: e.reduce_max(c0, a, axis=AX.X))
                E("vector", lambda e, a=a: e.tensor_scalar(lt16[:], a, c0, None, op0=OP.is_lt))
                E("vector", lambda e, a=a, b=b: e.select(b, lt16[:], a, neg4[:], add_drain=True))
            E("vector", lambda e: e.reduce_max(c0, cur2, axis=AX.X))
            E("vector", lambda e: e.tensor_scalar(cur, msk, c0, None, op0=OP.is_ge))
            E("vector", lambda e, tq=tq, sc=sc: e.tensor_tensor(
                comb[:, tq * 16:(tq + 1) * 16], sc, cur, op=OP.mult), inc=s_dve)
            m_comb[tq] = s_dve.n
        ROUTER_DONE = s_dve.n
        ROUTER_ACT = s_act.n
        PHASE_A_PE = s_pe.n
        if dbg:
            E("gpsimd", dma(dxn_d[:], xn[:]), waits=[(s_act, LN_ACT_DONE)], inc=s_dbg, amt=16)
            E("gpsimd", dma(dnegc_d[:], negc[:]), waits=[(s_act, NEGC_DONE)], inc=s_dbg, amt=16)
            E("gpsimd", dma(dcbf_d[0:H, :], chi[:]), waits=[(s_dve, C_DONE)], inc=s_dbg, amt=16)
            E("gpsimd", dma(dcbf_d[H:2 * H, :], cmid[:]), inc=s_dbg, amt=16)
            E("gpsimd", dma(dcbf_d[2 * H:3 * H, :], clo[:]), inc=s_dbg, amt=16)
            E("gpsimd", dma(dnegm_d[:], negm[:]), waits=[(s_dve, C_DONE)], inc=s_dbg, amt=16)
            E("gpsimd", dma(dcomb_d[:], comb[:]), waits=[(s_dve, ROUTER_DONE)], inc=s_dbg, amt=16)

    A_FREE = [(s_pe, PHASE_A_PE), (s_dve, ROUTER_DONE), (s_act, ROUTER_ACT)]

    with contextlib.ExitStack() as sB:
        pnl = sB.enter_context(nc.sbuf_tensor("pnl", [128, 3 * 2048], BF16))
        kaug = [sB.enter_context(nc.sbuf_tensor(f"kaug{i}", [68, NKV], BF16))
                for i in range(2)]
        vaug = sB.enter_context(nc.sbuf_tensor("vaug", [128, 16 * 130], BF16))
        qaug = [sB.enter_context(nc.sbuf_tensor(f"qaug{i}", [68, NQ], BF16))
                for i in range(2)]
        pt = [sB.enter_context(nc.sbuf_tensor(f"pt{i}", [128, 2048], BF16))
              for i in range(2)]
        atm = [sB.enter_context(nc.sbuf_tensor(f"atm{i}", [128, 64], F32))
               for i in range(2)]
        afm = sB.enter_context(nc.sbuf_tensor("afm", [128, 8 * NQ], BF16))
        linv = sB.enter_context(nc.sbuf_tensor("linv", [128, 2], F32))

        T2 = [(s_pe, PHASE_A_CUM), (s_dve, C_DONE), (s_act, NEGC_DONE)]
        E("gpsimd", dma(kaug[0][67:68, :], ones_d[0:1, :]), waits=T2, inc=s_wc, amt=16)
        E("gpsimd", dma(kaug[1][67:68, :], ones_d[0:1, :]), inc=s_wc, amt=16)
        E("gpsimd", dma(qaug[0][64:67, :], ones_d[0:3, 0:NQ]), inc=s_wc, amt=16)
        E("gpsimd", dma(qaug[1][64:67, :], ones_d[0:3, 0:NQ]), inc=s_wc, amt=16)
        ONES_ALL = s_wc.n
        E("vector", lambda e: e.memset(
            vaug.rearrange("p (c k) -> p c k", k=65)[:, :, 64:65], 1.0),
          waits=[(s_act, ROUTER_ACT)], inc=s_dve)
        VONES = s_dve.n

        NIT = NPAIR * 16
        m_qk = [0] * NIT
        m_exp = [0] * NIT
        m_av = [0] * NIT
        m_avsc = [0] * NIT
        m_attnT = [0] * NIT
        m_afe = [0] * NIT
        m_projdone = [0] * NPAIR
        m_qkdone = [0] * NPAIR
        m_projevac = [0] * NPAIR

        def emit_panels(p, gate):
            for j in range(3):
                for s2 in range(2):
                    E("gpsimd", dma(
                        pnl[:, j * 2048 + s2 * 1024: j * 2048 + (s2 + 1) * 1024],
                        wqkv_d[j * 16 + p][:, s2 * 1024:(s2 + 1) * 1024]),
                      waits=(gate if (j == 0 and s2 == 0) else ()), inc=s_wp, amt=16)

        def emit_crows(p, gate):
            h0, h1 = 2 * p, 2 * p + 1
            for kb_, h in ((kaug[0], h0), (kaug[1], h1)):
                E("gpsimd", dma(kb_[64:65, :], chi[h:h + 1, :]),
                  waits=gate, inc=s_cr, amt=16)
                gate = ()
                E("gpsimd", dma(kb_[65:66, :], cmid[h:h + 1, :]), inc=s_cr, amt=16)
                E("gpsimd", dma(kb_[66:67, :], clo[h:h + 1, :]), inc=s_cr, amt=16)
            E("gpsimd", dma(qaug[0][67:68, :], negm[h0:h0 + 1, :]), inc=s_cr, amt=16)
            E("gpsimd", dma(qaug[1][67:68, :], negm[h1:h1 + 1, :]), inc=s_cr, amt=16)

        emit_panels(0, [(s_pe, LN_PE_LAST), (s_dve, LN_XN_DVE),
                        (s_act, LN_ACT_DONE)])
        emit_crows(0, T2)

        for p in range(NPAIR):
            first_w = [(s_wp, 96 * (p + 1)), (s_wc, ONES_ALL)]
            if p > 0:
                first_w += [(s_dve, m_afe[p * 16 - 1])]
            for kc in range(4):
                r = rot_i % 4
                for c in range(ND):
                    E("tensor", lambda e, c=c, kc=kc, r=r: e.matmul(
                        rot512[r], pnl[:, 2048 + c * 128: 2048 + (c + 1) * 128],
                        xn[:, c * NKV + kc * 512: c * NKV + (kc + 1) * 512],
                        start=(c == 0), stop=(c == ND - 1)),
                      waits=((first_w + [rotg[r]]) if c == 0 else ()),
                      inc=(s_pe if c == ND - 1 else None))
                first_w = []
                pn = s_pe.n
                E("vector", lambda e, r=r, kc=kc: e.tensor_copy(
                    kaug[0][0:64, kc * 512:(kc + 1) * 512], rot512[r][0:64, :]),
                  waits=[(s_pe, pn)])
                E("vector", lambda e, r=r, kc=kc: e.tensor_copy(
                    kaug[1][0:64, kc * 512:(kc + 1) * 512], rot512[r][64:128, :]),
                  inc=s_dve)
                rotg[r] = (s_dve, s_dve.n)
                rot_i += 1
            for kt in range(16):
                r = rot_i % 4
                ps = rot512[r][:, 0:128]
                for c in range(ND):
                    E("tensor", lambda e, c=c, kt=kt, ps=ps: e.matmul(
                        ps, xn[:, c * NKV + kt * 128: c * NKV + (kt + 1) * 128],
                        pnl[:, 4096 + c * 128: 4096 + (c + 1) * 128],
                        start=(c == 0), stop=(c == ND - 1)),
                      waits=([rotg[r], (s_dve, VONES)] if c == 0 else ()),
                      inc=(s_pe if c == ND - 1 else None))
                pn = s_pe.n
                E("vector", lambda e, ps=ps, kt=kt: e.tensor_copy(
                    vaug.rearrange("p (c k) -> p c k", k=65)[:, 2 * kt:2 * kt + 2, 0:64],
                    ps.rearrange("p (c k) -> p c k", k=64)),
                  waits=[(s_pe, pn)], inc=s_dve)
                rotg[r] = (s_dve, s_dve.n)
                rot_i += 1
            for qc in range(2):
                r = rot_i % 4
                for c in range(ND):
                    E("tensor", lambda e, c=c, qc=qc, r=r: e.matmul(
                        rot512[r], pnl[:, c * 128:(c + 1) * 128],
                        xn[:, c * NKV + NQ + qc * 512: c * NKV + NQ + (qc + 1) * 512],
                        start=(c == 0), stop=(c == ND - 1)),
                      waits=([rotg[r]] if c == 0 else ()),
                      inc=(s_pe if c == ND - 1 else None))
                pn = s_pe.n
                E("vector", lambda e, r=r, qc=qc: e.tensor_scalar(
                    qaug[0][0:64, qc * 512:(qc + 1) * 512], rot512[r][0:64, :],
                    0.125, None, op0=OP.mult),
                  waits=[(s_pe, pn)])
                E("vector", lambda e, r=r, qc=qc: e.tensor_scalar(
                    qaug[1][0:64, qc * 512:(qc + 1) * 512], rot512[r][64:128, :],
                    0.125, None, op0=OP.mult),
                  inc=s_dve)
                rotg[r] = (s_dve, s_dve.n)
                rot_i += 1
            m_projdone[p] = s_pe.n
            m_projevac[p] = s_dve.n
            if p + 1 < NPAIR and p != 7:
                emit_panels(p + 1, [(s_pe, m_projdone[p])])

            QTORD = [(h, qt) for h in range(2) for qt in (7, 0, 6, 1, 5, 2, 4, 3)]

            def emit_qk(g):
                h, qt = QTORD[g % 16]
                ka, qa = kaug[h], qaug[h]
                par = g % 2
                T = SA if par == 0 else SB
                live = 2 * (qt + 1) * 128
                w0 = [(s_cr, 128 * (g // 16 + 1)), (s_dve, m_projevac[g // 16])]
                if g >= 3:
                    w0 += [(s_dve, m_afe[g - 3])]
                if g >= 2:
                    w0 += [(s_dve, m_avsc[g - 2])]
                for kt in range(qt + 1):
                    E("tensor", lambda e, kt=kt, ka=ka, qa=qa, qt=qt, T=T: e.matmul(
                        T[:, kt * 128:(kt + 1) * 128], ka[:, kt * 128:(kt + 1) * 128],
                        qa[:, qt * 128:(qt + 1) * 128], start=True, stop=(kt != qt)),
                      waits=(w0 if kt == 0 else ()))
                    w0 = []
                E("tensor", lambda e, qt=qt, T=T: e.matmul(
                    T[:, qt * 128:(qt + 1) * 128], identb[:], odiag[:],
                    start=False, stop=True))
                for j in range(qt + 1):
                    E("tensor", lambda e, j=j, ka=ka, qa=qa, qt=qt, T=T: e.matmul(
                        T[:, (qt + 1 + j) * 128: (qt + 2 + j) * 128],
                        ka[:, NQ + j * 128: NQ + (j + 1) * 128],
                        qa[:, qt * 128:(qt + 1) * 128], start=True, stop=(j != qt)))
                E("tensor", lambda e, qt=qt, T=T: e.matmul(
                    T[:, (2 * qt + 1) * 128: (2 * qt + 2) * 128], identb[:], trium[:],
                    start=False, stop=True), inc=s_pe)
                m_qk[g] = s_pe.n
                par2 = g % 2
                ew = [(s_pe, m_qk[g])]
                if g >= 2:
                    ew += [(s_pe, m_av[g - 2])]
                if dbg and g == gtap + 2:
                    ew += [(s_dbg2, 16)]
                E("scalar", lambda e, par=par2, live=live, T=T: e.activation(
                    pt[par][:, 0:live], T[:, 0:live], AF.Exp),
                  waits=ew, inc=s_act)
                m_exp[g] = s_act.n
                if dbg and g == gtap:
                    E("gpsimd", dma(dpt_d[:], pt[par2][:, 0:1152]),
                      waits=[(s_act, m_exp[g])], inc=s_dbg2, amt=16)
                    E("gpsimd", dma(dvau_d[:], vaug[:]),
                      waits=[(s_pe, m_qk[g])], inc=s_dbg2, amt=16)

            def emit_av(g):
                h, qt = QTORD[g % 16]
                par = g % 2
                T = SA if par == 0 else SB
                pav = T[:, 0:65]
                aw = [(s_act, m_exp[g])]
                if g >= 2:
                    aw += [(s_dve, m_avsc[g - 2])]
                nch = 2 * (qt + 1)
                for jj in range(nch):
                    vidx = jj if jj <= qt else 8 + (jj - (qt + 1))
                    E("tensor", lambda e, jj=jj, par=par, h=h, pav=pav, nch=nch,
                      vidx=vidx: e.matmul(
                        pav, pt[par][:, jj * 128:(jj + 1) * 128],
                        vaug[:, vidx * 130 + h * 65: vidx * 130 + h * 65 + 65],
                        start=(jj == 0), stop=(jj == nch - 1)),
                      waits=(aw if jj == 0 else ()),
                      inc=(s_pe if jj == nch - 1 else None))
                m_av[g] = s_pe.n
                lv = linv[:, par:par + 1]
                E("vector", lambda e, pav=pav, lv=lv: e.reciprocal(lv, pav[:, 64:65]),
                  waits=[(s_pe, m_av[g])])
                E("vector", lambda e, pav=pav, lv=lv, par=par: e.tensor_scalar(
                    atm[par][:], pav[:, 0:64], lv, None, op0=OP.mult),
                  waits=([(s_dbg3, 16)] if (dbg and g == gtap + 2) else []), inc=s_dve)
                m_avsc[g] = s_dve.n
                if dbg and g == gtap:
                    E("gpsimd", dma(datm_d[:], atm[par][:]),
                      waits=[(s_dve, m_avsc[g])], inc=s_dbg3, amt=16)

            def emit_tp(g):
                h, qt = QTORD[g % 16]
                par = g % 2
                Tother = SB if par == 0 else SA
                ptp = Tother[:, 512:640]
                tw = [(s_dve, m_avsc[g])]
                if g >= 2:
                    tw += [(s_dve, m_afe[g - 2])]
                E("tensor", lambda e, ptp=ptp, par=par: e.transpose(
                    ptp[0:64, :], atm[par][:], identf[:]), waits=tw, inc=s_pe)
                m_attnT[g] = s_pe.n
                p_, qt_ = g // 16, qt
                E("vector", lambda e, ptp=ptp, h=h, qt=qt_, p=p_: e.tensor_copy(
                    afm[h * 64:(h + 1) * 64,
                        (p % 8) * NQ + qt * 128:(p % 8) * NQ + (qt + 1) * 128],
                    ptp[0:64, :]),
                  waits=[(s_pe, m_attnT[g])], inc=s_dve)
                m_afe[g] = s_dve.n

            for i in range(16):
                g = p * 16 + i
                emit_qk(g)
                if i >= 1:
                    emit_av(g - 1)
                if i >= 2:
                    emit_tp(g - 2)
            emit_av(p * 16 + 15)
            emit_tp(p * 16 + 14)
            emit_tp(p * 16 + 15)
            m_qkdone[p] = s_pe.n
            if p + 1 < NPAIR:
                emit_crows(p + 1, [(s_pe, m_qkdone[p])])

            if p in (7, 15):
                half = p // 8
                wou_base = s_wou.n
                m_woev = [[0] * 8 for _ in range(4)]
                wo_tgt = [0] * 32
                for dq in range(4):
                    for pc2 in range(8):
                        n = dq * 8 + pc2
                        slot = n % 12
                        dchunk = half * 8 + pc2
                        if n == 0:
                            gate = [(s_pe, m_projdone[p])]
                        elif n < 12:
                            gate = ()
                        else:
                            gate = [(s_wou, wou_base + n - 12 + 1)]
                        E("gpsimd", dma(pnl[:, slot * 512:(slot + 1) * 512],
                                        wo_d[dchunk * 128:(dchunk + 1) * 128,
                                             dq * 512:(dq + 1) * 512]),
                          waits=gate, inc=s_wo[slot], amt=16)
                        wo_tgt[n] = s_wo[slot].n
                    for qt in range(8):
                        r = rot_i % 4
                        ps = rot512[r]
                        for pc2 in range(8):
                            slot = (dq * 8 + pc2) % 12
                            w = []
                            if pc2 == 0:
                                w = [rotg[r]]
                                if dq == 0 and qt == 0:
                                    w += [(s_dve, m_afe[p * 16 + 15]),
                                          (s_dve, m_avsc[p * 16 + 15])]
                            if qt == 0:
                                w = list(w) + [(s_wo[slot], wo_tgt[dq * 8 + pc2])]
                            lastmm = (pc2 == 7)
                            if qt == 7:
                                inc_s = s_wou
                            else:
                                inc_s = s_pe if lastmm else None
                            E("tensor", lambda e, pc2=pc2, qt=qt, dq=dq, ps=ps, slot=slot: e.matmul(
                                ps, afm[:, pc2 * NQ + qt * 128: pc2 * NQ + (qt + 1) * 128],
                                pnl[:, slot * 512:(slot + 1) * 512],
                                start=(pc2 == 0), stop=(pc2 == 7)),
                              waits=w,
                              inc=(inc_s if (qt == 7 or lastmm) else None))
                        if qt == 7:
                            sem, val = s_wou, s_wou.n
                        else:
                            sem, val = s_pe, s_pe.n
                        E("vector", lambda e, qt=qt, dq=dq, ps=ps: e.tensor_tensor(
                            acc[:, qt * D + dq * 512: qt * D + (dq + 1) * 512], ps,
                            acc[:, qt * D + dq * 512: qt * D + (dq + 1) * 512],
                            op=OP.add),
                          waits=[(sem, val)], inc=s_dve)
                        m_woev[dq][qt] = s_dve.n
                        rotg[r] = (s_dve, s_dve.n)
                        rot_i += 1
                WO_PE = s_pe.n
                WO_WOU = s_wou.n
                if p == 7:
                    emit_panels(8, [(s_wou, WO_WOU)])
        ATT_PE_DONE = s_pe.n
        ATT_DVE_DONE = s_dve.n
        ATT_ACT_DONE = s_act.n
        if dbg:
            E("gpsimd", dma(dafm_d[:], afm[:]), waits=[(s_dve, m_afe[NIT - 1])],
              inc=s_dbg, amt=16)

    B_FREE = [(s_pe, ATT_PE_DONE), (s_dve, ATT_DVE_DONE), (s_act, ATT_ACT_DONE),
              (s_wou, s_wou.n)]
    if dbg:
        B_FREE = B_FREE + [(s_dbg, s_dbg.n)]

    sAB.close()
    with contextlib.ExitStack() as sC:
        w13 = [sC.enter_context(nc.sbuf_tensor(f"w13_{i}", [128, 2 * 2048], BF16))
               for i in range(2)]
        w2s = [[sC.enter_context(nc.sbuf_tensor(f"w2s_{i}_{pb}", [128, 512], BF16))
                for pb in range(2)] for i in range(NI)]
        hb = sC.enter_context(nc.sbuf_tensor("hb", [128, NI * NQ], BF16))
        silu = [sC.enter_context(nc.sbuf_tensor("silu0", [128, NQ], F32))] * 2

        NEXP = NE + 2
        m_h1 = [[0] * NI for _ in range(NEXP)]
        m_h3 = [[0] * NI for _ in range(NEXP)]
        m_silu = [[0] * NI for _ in range(NEXP)]
        m_mult = [[0] * NI for _ in range(NEXP)]

        def w2src(e_, i, q):
            if e_ < NE:
                return w2_d[e_, i * 128:(i + 1) * 128, q * 512:(q + 1) * 512]
            o = (e_ - NE) * INTER
            return ws2_d[o + i * 128: o + (i + 1) * 128, q * 512:(q + 1) * 512]

        def emit_w13(n, gate):
            par = n % 2
            for s2 in range(4):
                E("gpsimd", dma(w13[par][:, s2 * 1024:(s2 + 1) * 1024],
                                w13h_d[n][:, s2 * 1024:(s2 + 1) * 1024]),
                  waits=(gate if s2 == 0 else ()), inc=s_w13[par], amt=16)

        emit_w13(0, list(B_FREE))
        emit_w13(1, [])
        n13 = 2

        for e_ in range(NEXP):
            for i in range(NI):
                n = e_ * NI + i
                par = n % 2
                hw = [(s_w13[par], (n // 2 + 1) * 64)]
                if n > 0:
                    pe_, pi = divmod(n - 1, NI)
                    hw += [(s_act, m_silu[pe_][pi])]
                else:
                    hw += [(s_act, ATT_ACT_DONE)]
                for tc in range(2):
                    for c in range(ND):
                        E("tensor", lambda e, c=c, tc=tc, par=par: e.matmul(
                            SA[:, tc * 512:(tc + 1) * 512],
                            w13[par][:, c * 128:(c + 1) * 128],
                            xn[:, c * NKV + NQ + tc * 512: c * NKV + NQ + (tc + 1) * 512],
                            start=(c == 0), stop=(c == ND - 1)),
                          waits=(hw if (tc == 0 and c == 0) else ()),
                          inc=(s_pe if (tc == 1 and c == ND - 1) else None))
                m_h1[e_][i] = s_pe.n
                hw3 = []
                if n > 0:
                    pe_, pi = divmod(n - 1, NI)
                    hw3 = [(s_dve, m_mult[pe_][pi])]
                for tc in range(2):
                    for c in range(ND):
                        E("tensor", lambda e, c=c, tc=tc, par=par: e.matmul(
                            SB[:, tc * 512:(tc + 1) * 512],
                            w13[par][:, 2048 + c * 128: 2048 + (c + 1) * 128],
                            xn[:, c * NKV + NQ + tc * 512: c * NKV + NQ + (tc + 1) * 512],
                            start=(c == 0), stop=(c == ND - 1)),
                          waits=(hw3 if (tc == 0 and c == 0) else ()),
                          inc=(s_pe if (tc == 1 and c == ND - 1) else None))
                m_h3[e_][i] = s_pe.n
                while n13 < NEXP * NI and n13 % NI >= 2 and (n13 - 2) <= e_ * NI + i:
                    ge_, gi_ = divmod(n13 - 2, NI)
                    emit_w13(n13, [(s_pe, m_h3[ge_][gi_])])
                    n13 += 1
                sw = [(s_pe, m_h1[e_][i])]
                if n >= 1:
                    pe2, pi2 = divmod(n - 1, NI)
                    sw += [(s_dve, m_mult[pe2][pi2])]
                E("scalar", lambda e, par=par: e.activation(silu[par][:], SA[:, 0:1024],
                                                            AF.Silu),
                  waits=sw, inc=s_act)
                m_silu[e_][i] = s_act.n
                E("vector", lambda e, i=i, par=par: e.tensor_tensor(
                    hb[:, i * NQ:(i + 1) * NQ], SB[:, 0:1024], silu[par][:],
                    op=OP.mult),
                  waits=[(s_act, m_silu[e_][i]), (s_pe, m_h3[e_][i])], inc=s_dve)
                m_mult[e_][i] = s_dve.n

            w2_tgt = [0] * NI
            for q in range(4):
                seq = e_ * 4 + q
                pb = seq % 2
                for i in range(NI):
                    if seq == 0:
                        gate = B_FREE if i == 0 else ()
                    elif seq == 1:
                        gate = ()
                    else:
                        gate = [(s_w2u, (seq - 2) * NI + i + 1)]
                    E("gpsimd", dma(w2s[i][pb][:], w2src(e_, i, q)), waits=gate,
                      inc=s_w2s[i][pb], amt=16)
                    w2_tgt[i] = s_w2s[i][pb].n
                for tt in range(8):
                    r = rot_i % 4
                    ps = rot512[r]
                    for i in range(NI):
                        w = []
                        if i == 0:
                            w = [rotg[r], (s_dve, m_mult[e_][NI - 1])]
                        if tt == 0:
                            w = list(w) + [(s_w2s[i][pb], w2_tgt[i])]
                        if qt_inc := (tt == 7):
                            inc_s = s_w2u
                        else:
                            inc_s = s_pe if i == NI - 1 else None
                        E("tensor", lambda e, i=i, tt=tt, ps=ps, pb=pb: e.matmul(
                            ps, hb[:, i * NQ + tt * 128: i * NQ + (tt + 1) * 128],
                            w2s[i][pb][:], start=(i == 0), stop=(i == NI - 1)),
                          waits=w,
                          inc=(inc_s if (tt == 7 or i == NI - 1) else None))
                    if tt == 7:
                        sem, val = s_w2u, s_w2u.n
                    else:
                        sem, val = s_pe, s_pe.n
                    scal = comb[:, tt * 16 + e_: tt * 16 + e_ + 1] if e_ < NE else 1.0
                    E("vector", lambda e, tt=tt, q=q, ps=ps, scal=scal: e.scalar_tensor_tensor(
                        acc[:, tt * D + q * 512: tt * D + (q + 1) * 512], ps, scal,
                        acc[:, tt * D + q * 512: tt * D + (q + 1) * 512],
                        op0=OP.mult, op1=OP.add),
                      waits=[(sem, val)], inc=s_dve)
                    rotg[r] = (s_dve, s_dve.n)
                    rot_i += 1
            while n13 < NEXP * NI and n13 % NI < 2:
                ge_, gi_ = divmod(n13 - 2, NI)
                emit_w13(n13, [(s_pe, m_h3[ge_][gi_])])
                n13 += 1
        MOE_DVE_DONE = s_dve.n

    if dbg:
        E("gpsimd", dma(dacc_d[:], acc[:]), waits=[(s_dve, MOE_DVE_DONE)],
          inc=s_dbg, amt=16)
        E("gpsimd", None, waits=[(s_dbg, s_dbg.n), (s_dbg2, s_dbg2.n), (s_dbg3, s_dbg3.n)])

    for tt in range(8):
        E("sync", dma(out_d[tt * 128:(tt + 1) * 128, :], acc[:, tt * D:(tt + 1) * D]),
          waits=([(s_dve, MOE_DVE_DONE)] if tt == 0 else ()), inc=s_out, amt=16)
    E("sync", None, waits=[(s_out, s_out.n)])

    kb.replay()
    nc.finalize()
    return nc


_CACHE = {}


def _host_prep(inputs):
    f = np.float32

    def panelize(w):
        wT = np.asarray(w, f).T
        return wT.reshape(16, 128, 16, 128).transpose(2, 1, 0, 3).reshape(16, 128, 2048)

    def blk13(w, ws):
        a = np.asarray(w, f).transpose(0, 2, 1).reshape(NE, 16, 128, NI, 128) \
            .transpose(0, 3, 2, 1, 4).reshape(NE * NI, 128, 2048)
        b = np.asarray(ws, f).T.reshape(16, 128, 2 * NI, 128) \
            .transpose(2, 1, 0, 3).reshape(2 * NI, 128, 2048)
        return np.concatenate([a, b], axis=0)

    wqkv = np.ascontiguousarray(np.concatenate(
        [panelize(inputs["L0_attn_Wq"]), panelize(inputs["L0_attn_Wk"]),
         panelize(inputs["L0_attn_Wv"])], axis=0))
    w13h = np.ascontiguousarray(np.concatenate(
        [blk13(inputs["L0_ffn_W1"], inputs["L0_ffn_Ws1"]),
         blk13(inputs["L0_ffn_W3"], inputs["L0_ffn_Ws3"])], axis=2))
    shared = dict(
        wqkv_p=wqkv,
        w13h=w13h,
        woT=np.ascontiguousarray(inputs["L0_attn_Wo"].T, dtype=f),
        wfT=np.ascontiguousarray(inputs["L0_attn_Wf"].T, dtype=f),
        bf_col=np.ascontiguousarray(np.asarray(inputs["L0_attn_bf"]).reshape(H, 1), dtype=f),
        wgT=np.ascontiguousarray(inputs["L0_ffn_Wg"].T, dtype=f),
        w2T=np.ascontiguousarray(np.asarray(inputs["L0_ffn_W2"]).transpose(0, 2, 1), dtype=f),
        ws2T=np.ascontiguousarray(inputs["L0_ffn_Ws2"].T, dtype=f),
        trium=np.ascontiguousarray(np.tril(np.full((128, 128), BNEG, f), -1).astype(_BF)),
        identb=np.ascontiguousarray(np.eye(128, dtype=_BF)),
        identf=np.ascontiguousarray(np.eye(128, dtype=f)),
        ones_rows=np.ones((3, NKV), dtype=_BF),
    )
    idx = np.arange(2048).reshape(16, 128)
    tp = {}
    for s in (0, 1):
        other = idx[(1 - s)::2].reshape(-1)
        own = idx[s::2].reshape(-1)
        tp[s] = np.concatenate([other, own])
    trit = {s: np.ascontiguousarray(
                (tp[s][:, None] <= tp[s][None, :]).astype(f)
                .reshape(16, 128, 16, 128).transpose(2, 1, 0, 3).reshape(16, 128, 2048))
            for s in (0, 1)}
    odiag = {0: np.full((128, 128), BNEG, dtype=_BF),
             1: np.zeros((128, 128), dtype=_BF)}

    x = np.asarray(inputs["L0_attn_x"])
    in_maps = []
    for c in range(8):
        b, s = divmod(c, 2)
        m = dict(shared)
        m["x_all"] = np.ascontiguousarray(x[b][tp[s]], dtype=f)
        m["trit"] = trit[s]
        m["odiag"] = odiag[s]
        in_maps.append(m)
    return in_maps, tp


def kernel(**inputs):
    if "nc" not in _CACHE:
        _CACHE["nc"] = build_program()
    nc = _CACHE["nc"]
    in_maps, tp = _host_prep(inputs)
    res = run_bass_kernel_spmd(nc, in_maps, core_ids=list(range(8)))
    out = np.empty((4, 2048, 2048), np.float32)
    for c in range(8):
        b, s = divmod(c, 2)
        out[b][tp[s][1024:]] = res.results[c]["out"]
    return out


# version 22 (final)
# speedup vs baseline: 1.1571x; 1.0044x over previous
import contextlib
import numpy as np
import ml_dtypes
import concourse.bass as bass
import concourse.mybir as mybir
from concourse.alu_op_type import AluOpType
from concourse.bass_utils import run_bass_kernel_spmd

F32 = mybir.dt.float32
I32 = mybir.dt.int32
BF16 = mybir.dt.bfloat16
AF = mybir.ActivationFunctionType
AX = mybir.AxisListType
OP = AluOpType

BNEG = -30000.0
BBND = 50.0

D = 2048
ND = 16
H = 32
NPAIR = 16
NQ = 1024
NKV = 2048
INTER = 1408
NI = 11
NE = 16
SH_INTER = 2816

_BF = ml_dtypes.bfloat16


class Cnt:
    def __init__(self, nc, name):
        self.h = nc.alloc_semaphore(name)
        self.n = 0


class KB:

    def __init__(self, simcheck=False):
        self.nc = bass.Bass("TRN2", target_bir_lowering=False,
                            detect_race_conditions=simcheck)
        self.events = {k: [] for k in ("sync", "gpsimd", "tensor", "vector", "scalar")}
        self.simcheck = simcheck
        self.selfsem = {}

    def sem(self, name):
        return Cnt(self.nc, name)

    def ev(self, eng, fn, waits=(), inc=None, amt=1):
        waits = list(waits)
        if eng in self.selfsem and fn is not None:
            ss = self.selfsem[eng]
            waits.append((ss, ss.n))
            if inc is None:
                inc = ss
        w = tuple((s.h, int(v)) for (s, v) in waits if s is not None and v > 0)
        ih = None
        if inc is not None:
            inc.n += amt
            ih = (inc.h, amt)
        self.events[eng].append((w, fn, ih))

    def replay(self):
        nc = self.nc
        with nc.Block() as blk:
            for name, deco in (("sync", blk.sync), ("gpsimd", blk.gpsimd),
                               ("tensor", blk.tensor), ("vector", blk.vector),
                               ("scalar", blk.scalar)):
                evs = self.events[name]

                def make(evs):
                    def body(eng):
                        for w, fn, ih in evs:
                            for h, v in w:
                                eng.wait_ge(h, v)
                            inst = fn(eng) if fn is not None else None
                            if ih is not None:
                                inst.then_inc(*ih)
                    return body

                deco(make(evs))


def build_program(simcheck=False, dbg=False, gtap=0):
    kb = KB(simcheck)
    nc = kb.nc
    E = kb.ev

    x_d = nc.dram_tensor("x_all", [NKV, D], F32, kind="ExternalInput")
    wqkv_d = nc.dram_tensor("wqkv_p", [48, 128, 2048], F32, kind="ExternalInput")
    wo_d = nc.dram_tensor("woT", [D, D], F32, kind="ExternalInput")
    wf_d = nc.dram_tensor("wfT", [D, H], F32, kind="ExternalInput")
    bf_d = nc.dram_tensor("bf_col", [H, 1], F32, kind="ExternalInput")
    wg_d = nc.dram_tensor("wgT", [D, NE], F32, kind="ExternalInput")
    w13h_d = nc.dram_tensor("w13h", [(NE + 2) * NI, 128, 4096], F32, kind="ExternalInput")
    w2_d = nc.dram_tensor("w2T", [NE, INTER, D], F32, kind="ExternalInput")
    ws2_d = nc.dram_tensor("ws2T", [SH_INTER, D], F32, kind="ExternalInput")
    trit_d = nc.dram_tensor("trit", [16, 128, 2048], F32, kind="ExternalInput")
    odiag_d = nc.dram_tensor("odiag", [128, 128], BF16, kind="ExternalInput")
    trium_d = nc.dram_tensor("trium", [128, 128], BF16, kind="ExternalInput")
    identb_d = nc.dram_tensor("identb", [128, 128], BF16, kind="ExternalInput")
    identf_d = nc.dram_tensor("identf", [128, 128], F32, kind="ExternalInput")
    ones_d = nc.dram_tensor("ones_rows", [3, NKV], BF16, kind="ExternalInput")
    out_d = nc.dram_tensor("out", [NQ, D], F32, kind="ExternalOutput")
    if dbg:
        dxn_d = nc.dram_tensor("dxn", [128, ND * NKV], BF16, kind="ExternalOutput")
        dnegc_d = nc.dram_tensor("dnegc", [H, NKV], F32, kind="ExternalOutput")
        dcbf_d = nc.dram_tensor("dcbf", [3 * H, NKV], BF16, kind="ExternalOutput")
        dnegm_d = nc.dram_tensor("dnegm", [H, NQ], BF16, kind="ExternalOutput")
        dcomb_d = nc.dram_tensor("dcomb", [128, 128], F32, kind="ExternalOutput")
        dpt_d = nc.dram_tensor("dpt", [128, 1152], BF16, kind="ExternalOutput")
        datm_d = nc.dram_tensor("datm", [128, 64], F32, kind="ExternalOutput")
        dafm_d = nc.dram_tensor("dafm", [128, 8 * NQ], BF16, kind="ExternalOutput")
        dacc_d = nc.dram_tensor("dacc", [128, 8 * D], F32, kind="ExternalOutput")
        dvau_d = nc.dram_tensor("dvau", [128, 16 * 130], BF16, kind="ExternalOutput")
        s_dbg = kb.sem("s_dbg")
        s_dbg2 = kb.sem("s_dbg2")
        s_dbg3 = kb.sem("s_dbg3")

    SA = nc.alloc_psum_tensor("SA", [128, 2048], F32)
    SB = nc.alloc_psum_tensor("SB", [128, 2048], F32)
    rot512 = [SA[:, 1024:1536], SA[:, 1536:2048], SB[:, 1024:1536], SB[:, 1536:2048]]
    rotg = [(None, 0)] * 4

    acc = nc.alloc_sbuf_tensor("acc", [128, 8 * D], F32)
    xn = nc.alloc_sbuf_tensor("xn_fm", [128, ND * NKV], BF16)
    comb = nc.alloc_sbuf_tensor("comb", [128, 8 * 16], F32)
    trium = nc.alloc_sbuf_tensor("trium_s", [128, 128], BF16)
    identb = nc.alloc_sbuf_tensor("identb_s", [128, 128], BF16)
    identf = nc.alloc_sbuf_tensor("identf_s", [128, 128], F32)
    wg_sb = nc.alloc_sbuf_tensor("wg_sb", [128, ND * NE], BF16)
    wf_sb = nc.alloc_sbuf_tensor("wf_sb", [128, ND * H], BF16)
    bf_sb = nc.alloc_sbuf_tensor("bf_sb", [H, 1], F32)
    odiag = nc.alloc_sbuf_tensor("odiag_s", [128, 128], BF16)
    cols = nc.alloc_sbuf_tensor("cols", [128, 16], F32)
    colsq = nc.alloc_sbuf_tensor("colsq", [128, 16], F32)
    neg4 = nc.alloc_sbuf_tensor("neg4", [128, 16], F32)
    epst = nc.alloc_sbuf_tensor("epst", [128, 1], F32)
    nc.const_aps.register(F32, 1e-5, epst[:])
    sAB = contextlib.ExitStack()
    chi = sAB.enter_context(nc.sbuf_tensor("chi", [H, NKV], BF16))
    cmid = sAB.enter_context(nc.sbuf_tensor("cmid", [H, NKV], BF16))
    clo = sAB.enter_context(nc.sbuf_tensor("clo", [H, NKV], BF16))
    negm = sAB.enter_context(nc.sbuf_tensor("negm", [H, NQ], BF16))
    negc = sAB.enter_context(nc.sbuf_tensor("negc", [H, NKV], F32))

    s_pe = kb.sem("s_pe")
    s_dve = kb.sem("s_dve")
    s_act = kb.sem("s_act")
    kb.selfsem = {"vector": s_dve, "scalar": s_act}
    s_xa = kb.sem("s_xa")
    s_xt = [kb.sem(f"s_xt{i}") for i in range(2)]
    s_wc = kb.sem("s_wc")
    s_xc = kb.sem("s_xc")
    s_tr = [kb.sem(f"s_tr{i}") for i in range(2)]
    s_wp = kb.sem("s_wp")
    s_wo = [kb.sem(f"s_wo{i}") for i in range(12)]
    s_wou = kb.sem("s_wou")
    s_cr = kb.sem("s_cr")
    s_w13 = [kb.sem(f"s_w13_{i}") for i in range(2)]
    s_w2s = [[kb.sem(f"s_w2s_{i}_{pb}") for pb in range(2)] for i in range(NI)]
    s_w2u = kb.sem("s_w2u")
    s_out = kb.sem("s_out")

    def dma(dst, src):
        return lambda eng, d=dst, s=src: eng.dma_start(d, s)

    E("vector", lambda e: e.memset(neg4[:], -4.0))
    E("vector", lambda e: e.memset(epst[:], 1e-5), inc=s_dve)
    EPS_SET = s_dve.n
    E("gpsimd", dma(trium[:], trium_d[:]), inc=s_wc, amt=16)
    E("gpsimd", dma(odiag[:], odiag_d[:]), inc=s_wc, amt=16)
    E("gpsimd", dma(identb[:], identb_d[:]), inc=s_wc, amt=16)
    E("gpsimd", dma(wg_sb.rearrange("q (c g) -> q c g", g=NE),
                    wg_d.rearrange("(c q) g -> q c g", q=128)), inc=s_wc, amt=16)
    E("gpsimd", dma(wf_sb.rearrange("q (c h) -> q c h", h=H),
                    wf_d.rearrange("(c q) h -> q c h", q=128)), inc=s_wc, amt=16)
    WC_INIT = s_wc.n
    E("sync", dma(identf[:], identf_d[:]), inc=s_xc, amt=16)
    E("sync", dma(bf_sb[:], bf_d[:]), inc=s_xc, amt=16)
    XC_ALL = s_xc.n
    for tt in range(8):
        E("sync", dma(acc[:, tt * D:(tt + 1) * D], x_d[(8 + tt) * 128:(9 + tt) * 128, :]),
          inc=s_xa, amt=16)
    XA_ALL = s_xa.n

    with contextlib.ExitStack() as sA:
        tritp = [sA.enter_context(nc.sbuf_tensor(f"tritp{i}", [128, ND * 128], F32))
                 for i in range(2)]
        xn_tm = sA.enter_context(nc.sbuf_tensor("xn_tm", [128, D], F32))
        u_fm = sA.enter_context(nc.sbuf_tensor("u_fm", [H, NKV], F32))
        wa = sA.enter_context(nc.sbuf_tensor("wa", [H, NKV], F32))
        wb = sA.enter_context(nc.sbuf_tensor("wb", [H, NKV], F32))
        nlt = sA.enter_context(nc.sbuf_tensor("nlt", [128, ND * H], F32))
        sct = [sA.enter_context(nc.sbuf_tensor(f"sct{i}", [128, H], F32))
               for i in range(2)]
        rsc = sA.enter_context(nc.sbuf_tensor("rsc", [128, 176], F32))
        lt4 = sA.enter_context(nc.sbuf_tensor("lt4", [128, 4], I32))
        lt16 = sA.enter_context(nc.sbuf_tensor("lt16", [128, 16], I32))

        m_sq = [0] * 16
        m_xn = [0] * 16
        m_tp = [0] * 16
        m_var = [0] * 16
        m_rn = [0] * 16
        rot_i = 0
        xnsrc = [None] * 16
        rstd16 = [None] * 16
        mean16 = [None] * 16

        def ln_stats(tt):
            par = tt % 2
            if tt < 8:
                w = []
                if tt >= 2:
                    w = [(s_act, m_sq[tt - 2]), (s_dve, m_xn[tt - 2])]
                E("sync", dma(tritp[par][:], x_d[tt * 128:(tt + 1) * 128, :]),
                  waits=w, inc=s_xt[par], amt=16)
                src = tritp[par][:]
                ldw = [(s_xt[par], s_xt[par].n)]
            else:
                src = acc[:, (tt - 8) * D:(tt - 7) * D]
                ldw = [(s_xa, XA_ALL)]
            xnsrc[tt] = src
            sumc = cols[:, 8 * par + 0: 8 * par + 1]
            meanc = cols[:, 8 * par + 2: 8 * par + 3]
            msqc = cols[:, 8 * par + 3: 8 * par + 4]
            varc = cols[:, 8 * par + 4: 8 * par + 5]
            lnvc = cols[:, 8 * par + 5: 8 * par + 6]
            rstdc = cols[:, 8 * par + 6: 8 * par + 7]
            mean16[tt] = meanc
            rstd16[tt] = rstdc
            sq4 = [colsq[:, 8 * par + j: 8 * par + j + 1] for j in range(4)]
            s01 = colsq[:, 8 * par + 4: 8 * par + 5]
            s23 = colsq[:, 8 * par + 5: 8 * par + 6]
            sqs = colsq[:, 8 * par + 6: 8 * par + 7]

            aw = list(ldw)
            if tt >= 2:
                aw += [(s_dve, m_var[tt - 2])]
            for j in range(4):
                E("scalar", lambda e, s=src, j=j, a=sq4[j]: e.activation(
                    nlt[:, 0:512], s[:, j * 512:(j + 1) * 512], AF.Square, accum_out=a),
                  waits=(aw if j == 0 else ()), inc=(s_act if j == 3 else None))
            m_sq[tt] = s_act.n
            E("vector", lambda e, s=src, o=sumc: e.reduce_sum(o, s, axis=AX.X), waits=ldw)
            E("vector", lambda e, i=sumc, o=meanc: e.tensor_scalar(o, i, 1.0 / D, None, op0=OP.mult))
            E("vector", lambda e, i=meanc, o=msqc: e.tensor_tensor(o, i, i, op=OP.mult))
            E("vector", lambda e, a=sq4[0], b=sq4[1], o=s01: e.tensor_tensor(o, a, b, op=OP.add),
              waits=[(s_act, m_sq[tt])])
            E("vector", lambda e, a=sq4[2], b=sq4[3], o=s23: e.tensor_tensor(o, a, b, op=OP.add))
            E("vector", lambda e, a=s01, b=s23, o=sqs: e.tensor_tensor(o, a, b, op=OP.add))
            E("vector", lambda e, i=sqs, m=msqc, o=varc: e.scalar_tensor_tensor(
                o, i, 1.0 / D, m, op0=OP.mult, op1=OP.subtract), inc=s_dve)
            m_var[tt] = s_dve.n
            E("scalar", lambda e, i=varc, o=lnvc: e.activation(o, i, AF.Ln, bias=1e-5),
              waits=[(s_dve, m_var[tt]), (s_dve, EPS_SET)])
            E("scalar", lambda e, i=lnvc, o=rstdc: e.activation(o, i, AF.Exp, scale=-0.5),
              inc=s_act)
            m_rn[tt] = s_act.n

        def ln_xn(tt):
            xw = [(s_act, m_rn[tt])]
            if tt >= 1:
                xw += [(s_pe, m_tp[tt - 1])]
            E("vector", lambda e, s=xnsrc[tt], m=mean16[tt], r=rstd16[tt]: e.tensor_scalar(
                xn_tm[:], s, m, r, op0=OP.subtract, op1=OP.mult),
              waits=xw, inc=s_dve)
            m_xn[tt] = s_dve.n

        def ln_tp(tt):
            nonlocal rot_i
            xnr = xn.rearrange("p (c k) -> p c k", k=NKV)
            for gblk in range(4):
                r = rot_i % 4
                for j in range(4):
                    c = 4 * gblk + j
                    E("tensor", lambda e, c=c, r=r, j=j: e.transpose(
                        rot512[r][:, j * 128:(j + 1) * 128],
                        xn_tm[:, c * 128:(c + 1) * 128], identf[:]),
                      waits=([(s_dve, m_xn[tt]), rotg[r], (s_xc, XC_ALL)]
                             if j == 0 else ()),
                      inc=(s_pe if j == 3 else None))
                pn = s_pe.n
                E("scalar", lambda e, gblk=gblk, r=r, tt=tt: e.copy(
                    xnr[:, 4 * gblk:4 * gblk + 4, tt * 128:(tt + 1) * 128],
                    rot512[r][:, 0:512].rearrange("p (c k) -> p c k", k=128)),
                  waits=[(s_pe, pn)], inc=s_act)
                rotg[r] = (s_act, s_act.n)
                rot_i += 1
            m_tp[tt] = s_pe.n

        for tt in range(16):
            ln_stats(tt)
            if tt >= 1:
                ln_tp(tt - 1)
            ln_xn(tt)
        ln_tp(15)
        LN_ACT_DONE = s_act.n
        LN_PE_LAST = s_pe.n
        LN_XN_DVE = m_xn[15]

        for kc in range(4):
            r = rot_i % 4
            for c in range(ND):
                E("tensor", lambda e, c=c, kc=kc, r=r: e.matmul(
                    rot512[r][0:H, :], wf_sb[:, c * H:(c + 1) * H],
                    xn[:, c * NKV + kc * 512: c * NKV + (kc + 1) * 512],
                    start=(c == 0), stop=(c == ND - 1)),
                  waits=([(s_act, LN_ACT_DONE), (s_wc, WC_INIT), rotg[r]] if c == 0 else ()),
                  inc=(s_pe if c == ND - 1 else None))
            pn = s_pe.n
            E("vector", lambda e, r=r, kc=kc: e.tensor_scalar(
                u_fm[:, kc * 512:(kc + 1) * 512], rot512[r][0:H, :], bf_sb[:],
                None, op0=OP.add),
              waits=[(s_pe, pn), (s_xc, XC_ALL)], inc=s_dve)
            rotg[r] = (s_dve, s_dve.n)
            rot_i += 1
        un = s_dve.n
        E("scalar", lambda e: e.activation(wa[:], u_fm[:], AF.Abs), waits=[(s_dve, un)])
        E("scalar", lambda e: e.activation(wb[:], wa[:], AF.Exp, scale=-1.0))
        E("scalar", lambda e: e.activation(wa[:], wb[:], AF.Ln, bias=1.0), inc=s_act)
        an = s_act.n
        E("vector", lambda e: e.tensor_scalar(wb[:], u_fm[:], 0.0, None, op0=OP.min),
          waits=[(s_act, an)])
        E("vector", lambda e: e.tensor_tensor(u_fm[:], wa[:], wb[:], op=OP.subtract),
          inc=s_dve)
        nlogf_n = s_dve.n

        for t2 in range(ND):
            r = rot_i % 4
            E("tensor", lambda e, t2=t2, r=r: e.transpose(
                rot512[r][0:128, 0:H], u_fm[0:H, t2 * 128:(t2 + 1) * 128],
                identf[0:H, 0:H]),
              waits=[(s_dve, nlogf_n), rotg[r]], inc=s_pe)
            pn = s_pe.n
            E("scalar", lambda e, t2=t2, r=r: e.copy(nlt[:, t2 * H:(t2 + 1) * H],
                                                     rot512[r][0:128, 0:H]),
              waits=[(s_pe, pn)], inc=s_act)
            rotg[r] = (s_act, s_act.n)
            rot_i += 1
        NLT_DONE = s_act.n

        m_cum = [0] * ND
        for it in range(ND):
            par2 = it % 2
            if it == 0:
                gate = [(s_act, m_sq[14]), (s_dve, m_xn[14])]
            elif it == 1:
                gate = [(s_act, m_sq[15]), (s_dve, m_xn[15])]
            else:
                gate = [(s_pe, m_cum[it - 2])]
            E("sync", dma(tritp[par2][:, 0:1024], trit_d[it][:, 0:1024]), waits=gate,
              inc=s_tr[par2], amt=16)
            E("sync", dma(tritp[par2][:, 1024:2048], trit_d[it][:, 1024:2048]),
              inc=s_tr[par2], amt=16)
            trw = (s_tr[par2], s_tr[par2].n)
            r = rot_i % 4
            for jc in range(ND):
                E("tensor", lambda e, jc=jc, r=r, par2=par2: e.matmul(
                    rot512[r][:, 0:H], tritp[par2][:, jc * 128:(jc + 1) * 128],
                    nlt[:, jc * H:(jc + 1) * H],
                    start=(jc == 0), stop=(jc == ND - 1)),
                  waits=([trw, (s_act, NLT_DONE), rotg[r]] if jc == 0 else ()),
                  inc=(s_pe if jc == ND - 1 else None))
            m_cum[it] = s_pe.n
            par = it % 2
            E("vector", lambda e, r=r, par=par: e.tensor_copy(
                sct[par][:], rot512[r][:, 0:H]),
              waits=[(s_pe, m_cum[it])], inc=s_dve)
            rotg[r] = (s_dve, s_dve.n)
            dn = s_dve.n
            rot_i += 1
            r2 = rot_i % 4
            E("tensor", lambda e, par=par, r2=r2: e.transpose(
                rot512[r2][0:H, 0:128], sct[par][:], identf[:]),
              waits=[(s_dve, dn), rotg[r2]], inc=s_pe)
            pn = s_pe.n
            E("scalar", lambda e, r2=r2, it=it: e.copy(negc[:, it * 128:(it + 1) * 128],
                                                       rot512[r2][0:H, 0:128]),
              waits=[(s_pe, pn)], inc=s_act)
            rotg[r2] = (s_act, s_act.n)
            rot_i += 1
        NEGC_DONE = s_act.n
        PHASE_A_CUM = s_pe.n

        E("vector", lambda e: e.tensor_copy(chi[:], negc[:]), waits=[(s_act, NEGC_DONE)])
        E("vector", lambda e: e.tensor_tensor(wa[:], negc[:], chi[:], op=OP.subtract))
        E("vector", lambda e: e.tensor_copy(cmid[:], wa[:]))
        E("vector", lambda e: e.tensor_tensor(wb[:], wa[:], cmid[:], op=OP.subtract))
        E("vector", lambda e: e.tensor_copy(clo[:], wb[:]))
        E("vector", lambda e: e.tensor_scalar(negm[:], negc[:, NQ:NKV], -1.0, -BBND,
                                              op0=OP.mult, op1=OP.add), inc=s_dve)
        C_DONE = s_dve.n

        scp = [rsc[:, 0:16], rsc[:, 96:112], rsc[:, 112:128], rsc[:, 128:144]]
        c1p = [rsc[:, 144 + i:145 + i] for i in range(4)]
        c2p = [rsc[:, 148 + i:149 + i] for i in range(4)]
        gs = rsc[:, 16:20]
        pen = rsc[:, 20:24]
        msk = rsc[:, 24:40]
        cur = rsc[:, 40:56]
        c0 = rsc[:, 56:57]
        cc1 = rsc[:, 59:60]
        gs2 = rsc[:, 60:64]
        ltm = rsc[:, 64:80]
        cur2 = rsc[:, 80:96]
        m_comb = [0] * 8
        for tq in range(8):
            sc = scp[tq % 4]
            c1 = c1p[tq % 4]
            c2 = c2p[tq % 4]
            r = rot_i % 4
            ps = rot512[r][:, 0:16]
            for c in range(ND):
                E("tensor", lambda e, c=c, tq=tq, ps=ps: e.matmul(
                    ps, xn[:, c * NKV + NQ + tq * 128: c * NKV + NQ + (tq + 1) * 128],
                    wg_sb[:, c * NE:(c + 1) * NE],
                    start=(c == 0), stop=(c == ND - 1)),
                  waits=([(s_act, LN_ACT_DONE), (s_wc, WC_INIT), rotg[r]] if c == 0 else ()),
                  inc=(s_pe if c == ND - 1 else None))
            rn = s_pe.n
            ew = [(s_pe, rn)]
            if tq >= 4:
                ew += [(s_dve, m_comb[tq - 4])]
            E("scalar", lambda e, ps=ps, sc=sc, c1=c1: e.activation(
                sc, ps, AF.Exp, accum_out=c1), waits=ew, inc=s_act)
            en = s_act.n
            rotg[r] = (s_act, en)
            rot_i += 1
            E("vector", lambda e, c1=c1, c2=c2: e.reciprocal(c2, c1), waits=[(s_act, en)])
            E("vector", lambda e, sc=sc, c2=c2: e.tensor_scalar(sc, sc, c2, None, op0=OP.mult))
            E("vector", lambda e, sc=sc: e.tensor_reduce(
                gs, sc.rearrange("p (g e) -> p g e", e=4), axis=AX.X, op=OP.max))
            E("vector", lambda e: e.reduce_max(c0, gs, axis=AX.X))
            E("vector", lambda e: e.tensor_scalar(lt4[:], gs, c0, None, op0=OP.is_lt))
            E("vector", lambda e: e.select(gs2, lt4[:], gs, neg4[:, 0:4], add_drain=True))
            E("vector", lambda e: e.reduce_max(cc1, gs2, axis=AX.X))
            E("vector", lambda e: e.tensor_scalar(pen, gs, cc1, None, op0=OP.subtract))
            E("vector", lambda e: e.tensor_scalar(pen, pen, 0.0, 30000.0,
                                                  op0=OP.min, op1=OP.mult))
            for g in range(4):
                E("vector", lambda e, g=g, sc=sc: e.tensor_scalar(
                    msk[:, 4 * g:4 * g + 4], sc[:, 4 * g:4 * g + 4], pen[:, g:g + 1],
                    None, op0=OP.add))
            E("vector", lambda e: e.tensor_copy(cur, msk))
            bufs = [cur, cur2]
            for rnd in range(3):
                a, b = bufs[rnd % 2], bufs[(rnd + 1) % 2]
                E("vector", lambda e, a=a: e.reduce_max(c0, a, axis=AX.X))
                E("vector", lambda e, a=a: e.tensor_scalar(lt16[:], a, c0, None, op0=OP.is_lt))
                E("vector", lambda e, a=a, b=b: e.select(b, lt16[:], a, neg4[:], add_drain=True))
            E("vector", lambda e: e.reduce_max(c0, cur2, axis=AX.X))
            E("vector", lambda e: e.tensor_scalar(cur, msk, c0, None, op0=OP.is_ge))
            E("vector", lambda e, tq=tq, sc=sc: e.tensor_tensor(
                comb[:, tq * 16:(tq + 1) * 16], sc, cur, op=OP.mult), inc=s_dve)
            m_comb[tq] = s_dve.n
        ROUTER_DONE = s_dve.n
        ROUTER_ACT = s_act.n
        PHASE_A_PE = s_pe.n
        if dbg:
            E("gpsimd", dma(dxn_d[:], xn[:]), waits=[(s_act, LN_ACT_DONE)], inc=s_dbg, amt=16)
            E("gpsimd", dma(dnegc_d[:], negc[:]), waits=[(s_act, NEGC_DONE)], inc=s_dbg, amt=16)
            E("gpsimd", dma(dcbf_d[0:H, :], chi[:]), waits=[(s_dve, C_DONE)], inc=s_dbg, amt=16)
            E("gpsimd", dma(dcbf_d[H:2 * H, :], cmid[:]), inc=s_dbg, amt=16)
            E("gpsimd", dma(dcbf_d[2 * H:3 * H, :], clo[:]), inc=s_dbg, amt=16)
            E("gpsimd", dma(dnegm_d[:], negm[:]), waits=[(s_dve, C_DONE)], inc=s_dbg, amt=16)
            E("gpsimd", dma(dcomb_d[:], comb[:]), waits=[(s_dve, ROUTER_DONE)], inc=s_dbg, amt=16)

    A_FREE = [(s_pe, PHASE_A_PE), (s_dve, ROUTER_DONE), (s_act, ROUTER_ACT)]

    with contextlib.ExitStack() as sB:
        pnl = sB.enter_context(nc.sbuf_tensor("pnl", [128, 3 * 2048], BF16))
        kaug = [sB.enter_context(nc.sbuf_tensor(f"kaug{i}", [68, NKV], BF16))
                for i in range(2)]
        vaug = sB.enter_context(nc.sbuf_tensor("vaug", [128, 16 * 130], BF16))
        qaug = [sB.enter_context(nc.sbuf_tensor(f"qaug{i}", [68, NQ], BF16))
                for i in range(2)]
        pt = [sB.enter_context(nc.sbuf_tensor(f"pt{i}", [128, 2048], BF16))
              for i in range(2)]
        atm = [sB.enter_context(nc.sbuf_tensor(f"atm{i}", [128, 64], F32))
               for i in range(2)]
        afm = sB.enter_context(nc.sbuf_tensor("afm", [128, 8 * NQ], BF16))
        linv = sB.enter_context(nc.sbuf_tensor("linv", [128, 2], F32))

        T2 = [(s_pe, PHASE_A_CUM), (s_dve, C_DONE), (s_act, NEGC_DONE)]
        E("gpsimd", dma(kaug[0][67:68, :], ones_d[0:1, :]), waits=T2, inc=s_wc, amt=16)
        E("gpsimd", dma(kaug[1][67:68, :], ones_d[0:1, :]), inc=s_wc, amt=16)
        E("gpsimd", dma(qaug[0][64:67, :], ones_d[0:3, 0:NQ]), inc=s_wc, amt=16)
        E("gpsimd", dma(qaug[1][64:67, :], ones_d[0:3, 0:NQ]), inc=s_wc, amt=16)
        ONES_ALL = s_wc.n
        E("vector", lambda e: e.memset(
            vaug.rearrange("p (c k) -> p c k", k=65)[:, :, 64:65], 1.0),
          waits=[(s_act, ROUTER_ACT)], inc=s_dve)
        VONES = s_dve.n

        NIT = NPAIR * 16
        m_qk = [0] * NIT
        m_exp = [0] * NIT
        m_av = [0] * NIT
        m_avsc = [0] * NIT
        m_attnT = [0] * NIT
        m_afe = [0] * NIT
        m_projdone = [0] * NPAIR
        m_qkdone = [0] * NPAIR
        m_projevac = [0] * NPAIR

        def emit_panels(p, gate):
            for j in range(3):
                for s2 in range(2):
                    E("gpsimd", dma(
                        pnl[:, j * 2048 + s2 * 1024: j * 2048 + (s2 + 1) * 1024],
                        wqkv_d[j * 16 + p][:, s2 * 1024:(s2 + 1) * 1024]),
                      waits=(gate if (j == 0 and s2 == 0) else ()), inc=s_wp, amt=16)

        def emit_crows(p, gate):
            h0, h1 = 2 * p, 2 * p + 1
            for kb_, h in ((kaug[0], h0), (kaug[1], h1)):
                E("gpsimd", dma(kb_[64:65, :], chi[h:h + 1, :]),
                  waits=gate, inc=s_cr, amt=16)
                gate = ()
                E("gpsimd", dma(kb_[65:66, :], cmid[h:h + 1, :]), inc=s_cr, amt=16)
                E("gpsimd", dma(kb_[66:67, :], clo[h:h + 1, :]), inc=s_cr, amt=16)
            E("gpsimd", dma(qaug[0][67:68, :], negm[h0:h0 + 1, :]), inc=s_cr, amt=16)
            E("gpsimd", dma(qaug[1][67:68, :], negm[h1:h1 + 1, :]), inc=s_cr, amt=16)

        emit_panels(0, [(s_pe, LN_PE_LAST), (s_dve, LN_XN_DVE),
                        (s_act, LN_ACT_DONE)])
        emit_crows(0, T2)

        for p in range(NPAIR):
            first_w = [(s_wp, 96 * (p + 1)), (s_wc, ONES_ALL)]
            if p > 0:
                first_w += [(s_dve, m_afe[p * 16 - 1])]
            for kc in range(4):
                r = rot_i % 4
                for c in range(ND):
                    E("tensor", lambda e, c=c, kc=kc, r=r: e.matmul(
                        rot512[r], pnl[:, 2048 + c * 128: 2048 + (c + 1) * 128],
                        xn[:, c * NKV + kc * 512: c * NKV + (kc + 1) * 512],
                        start=(c == 0), stop=(c == ND - 1)),
                      waits=((first_w + [rotg[r]]) if c == 0 else ()),
                      inc=(s_pe if c == ND - 1 else None))
                first_w = []
                pn = s_pe.n
                E("vector", lambda e, r=r, kc=kc: e.tensor_copy(
                    kaug[0][0:64, kc * 512:(kc + 1) * 512], rot512[r][0:64, :]),
                  waits=[(s_pe, pn)])
                E("vector", lambda e, r=r, kc=kc: e.tensor_copy(
                    kaug[1][0:64, kc * 512:(kc + 1) * 512], rot512[r][64:128, :]),
                  inc=s_dve)
                rotg[r] = (s_dve, s_dve.n)
                rot_i += 1
            for kt in range(16):
                r = rot_i % 4
                ps = rot512[r][:, 0:128]
                for c in range(ND):
                    E("tensor", lambda e, c=c, kt=kt, ps=ps: e.matmul(
                        ps, xn[:, c * NKV + kt * 128: c * NKV + (kt + 1) * 128],
                        pnl[:, 4096 + c * 128: 4096 + (c + 1) * 128],
                        start=(c == 0), stop=(c == ND - 1)),
                      waits=([rotg[r], (s_dve, VONES)] if c == 0 else ()),
                      inc=(s_pe if c == ND - 1 else None))
                pn = s_pe.n
                E("vector", lambda e, ps=ps, kt=kt: e.tensor_copy(
                    vaug.rearrange("p (c k) -> p c k", k=65)[:, 2 * kt:2 * kt + 2, 0:64],
                    ps.rearrange("p (c k) -> p c k", k=64)),
                  waits=[(s_pe, pn)], inc=s_dve)
                rotg[r] = (s_dve, s_dve.n)
                rot_i += 1
            for qc in range(2):
                r = rot_i % 4
                for c in range(ND):
                    E("tensor", lambda e, c=c, qc=qc, r=r: e.matmul(
                        rot512[r], pnl[:, c * 128:(c + 1) * 128],
                        xn[:, c * NKV + NQ + qc * 512: c * NKV + NQ + (qc + 1) * 512],
                        start=(c == 0), stop=(c == ND - 1)),
                      waits=([rotg[r]] if c == 0 else ()),
                      inc=(s_pe if c == ND - 1 else None))
                pn = s_pe.n
                E("vector", lambda e, r=r, qc=qc: e.tensor_scalar(
                    qaug[0][0:64, qc * 512:(qc + 1) * 512], rot512[r][0:64, :],
                    0.125, None, op0=OP.mult),
                  waits=[(s_pe, pn)])
                E("vector", lambda e, r=r, qc=qc: e.tensor_scalar(
                    qaug[1][0:64, qc * 512:(qc + 1) * 512], rot512[r][64:128, :],
                    0.125, None, op0=OP.mult),
                  inc=s_dve)
                rotg[r] = (s_dve, s_dve.n)
                rot_i += 1
            m_projdone[p] = s_pe.n
            m_projevac[p] = s_dve.n
            if p + 1 < NPAIR and p != 7:
                emit_panels(p + 1, [(s_pe, m_projdone[p])])

            QTORD = [(h, qt) for h in range(2) for qt in (7, 0, 6, 1, 5, 2, 4, 3)]

            def emit_qk(g):
                h, qt = QTORD[g % 16]
                ka, qa = kaug[h], qaug[h]
                par = g % 2
                T = SA if par == 0 else SB
                live = 2 * (qt + 1) * 128
                w0 = [(s_cr, 128 * (g // 16 + 1)), (s_dve, m_projevac[g // 16])]
                if g >= 3:
                    w0 += [(s_dve, m_afe[g - 3])]
                for j in range(qt + 1):
                    E("tensor", lambda e, j=j, ka=ka, qa=qa, qt=qt, T=T: e.matmul(
                        T[:, (qt + 1 + j) * 128: (qt + 2 + j) * 128],
                        ka[:, NQ + j * 128: NQ + (j + 1) * 128],
                        qa[:, qt * 128:(qt + 1) * 128], start=True, stop=(j != qt)),
                      waits=(w0 if j == 0 else ()))
                    w0 = []
                E("tensor", lambda e, qt=qt, T=T: e.matmul(
                    T[:, (2 * qt + 1) * 128: (2 * qt + 2) * 128], identb[:], trium[:],
                    start=False, stop=True))
                wp = [(s_dve, m_avsc[g - 2])] if g >= 2 else []
                if qt > 0:
                    E("tensor", lambda e, qt=qt, ka=ka, qa=qa, T=T: e.matmul(
                        T[:, qt * 128:(qt + 1) * 128], ka[:, qt * 128:(qt + 1) * 128],
                        qa[:, qt * 128:(qt + 1) * 128], start=True, stop=False))
                    E("tensor", lambda e, qt=qt, T=T: e.matmul(
                        T[:, qt * 128:(qt + 1) * 128], identb[:], odiag[:],
                        start=False, stop=True))
                    for kt in range(qt - 1, 0, -1):
                        E("tensor", lambda e, kt=kt, ka=ka, qa=qa, qt=qt, T=T: e.matmul(
                            T[:, kt * 128:(kt + 1) * 128], ka[:, kt * 128:(kt + 1) * 128],
                            qa[:, qt * 128:(qt + 1) * 128], start=True, stop=True))
                    E("tensor", lambda e, ka=ka, qa=qa, qt=qt, T=T: e.matmul(
                        T[:, 0:128], ka[:, 0:128],
                        qa[:, qt * 128:(qt + 1) * 128], start=True, stop=True),
                      waits=wp, inc=s_pe)
                else:
                    E("tensor", lambda e, ka=ka, qa=qa, qt=qt, T=T: e.matmul(
                        T[:, 0:128], ka[:, 0:128],
                        qa[:, qt * 128:(qt + 1) * 128], start=True, stop=False),
                      waits=wp)
                    E("tensor", lambda e, T=T: e.matmul(
                        T[:, 0:128], identb[:], odiag[:],
                        start=False, stop=True), inc=s_pe)
                m_qk[g] = s_pe.n
                par2 = g % 2
                ew = [(s_pe, m_qk[g])]
                if g >= 2:
                    ew += [(s_pe, m_av[g - 2])]
                if dbg and g == gtap + 2:
                    ew += [(s_dbg2, 16)]
                E("scalar", lambda e, par=par2, live=live, T=T: e.activation(
                    pt[par][:, 0:live], T[:, 0:live], AF.Exp),
                  waits=ew, inc=s_act)
                m_exp[g] = s_act.n
                if dbg and g == gtap:
                    E("gpsimd", dma(dpt_d[:], pt[par2][:, 0:1152]),
                      waits=[(s_act, m_exp[g])], inc=s_dbg2, amt=16)
                    E("gpsimd", dma(dvau_d[:], vaug[:]),
                      waits=[(s_pe, m_qk[g])], inc=s_dbg2, amt=16)

            def emit_av(g):
                h, qt = QTORD[g % 16]
                par = g % 2
                T = SA if par == 0 else SB
                pav = T[:, 0:65]
                aw = [(s_act, m_exp[g])]
                if g >= 2:
                    aw += [(s_dve, m_avsc[g - 2])]
                nch = 2 * (qt + 1)
                for jj in range(nch):
                    vidx = jj if jj <= qt else 8 + (jj - (qt + 1))
                    E("tensor", lambda e, jj=jj, par=par, h=h, pav=pav, nch=nch,
                      vidx=vidx: e.matmul(
                        pav, pt[par][:, jj * 128:(jj + 1) * 128],
                        vaug[:, vidx * 130 + h * 65: vidx * 130 + h * 65 + 65],
                        start=(jj == 0), stop=(jj == nch - 1)),
                      waits=(aw if jj == 0 else ()),
                      inc=(s_pe if jj == nch - 1 else None))
                m_av[g] = s_pe.n
                lv = linv[:, par:par + 1]
                E("vector", lambda e, pav=pav, lv=lv: e.reciprocal(lv, pav[:, 64:65]),
                  waits=[(s_pe, m_av[g])])
                E("vector", lambda e, pav=pav, lv=lv, par=par: e.tensor_scalar(
                    atm[par][:], pav[:, 0:64], lv, None, op0=OP.mult),
                  waits=([(s_dbg3, 16)] if (dbg and g == gtap + 2) else []), inc=s_dve)
                m_avsc[g] = s_dve.n
                if dbg and g == gtap:
                    E("gpsimd", dma(datm_d[:], atm[par][:]),
                      waits=[(s_dve, m_avsc[g])], inc=s_dbg3, amt=16)

            def emit_tp(g):
                h, qt = QTORD[g % 16]
                par = g % 2
                Tother = SB if par == 0 else SA
                ptp = Tother[:, 512:640]
                tw = [(s_dve, m_avsc[g])]
                if g >= 2:
                    tw += [(s_dve, m_afe[g - 2])]
                E("tensor", lambda e, ptp=ptp, par=par: e.transpose(
                    ptp[0:64, :], atm[par][:], identf[:]), waits=tw, inc=s_pe)
                m_attnT[g] = s_pe.n
                p_, qt_ = g // 16, qt
                E("vector", lambda e, ptp=ptp, h=h, qt=qt_, p=p_: e.tensor_copy(
                    afm[h * 64:(h + 1) * 64,
                        (p % 8) * NQ + qt * 128:(p % 8) * NQ + (qt + 1) * 128],
                    ptp[0:64, :]),
                  waits=[(s_pe, m_attnT[g])], inc=s_dve)
                m_afe[g] = s_dve.n

            for i in range(16):
                g = p * 16 + i
                emit_qk(g)
                if i >= 1:
                    emit_av(g - 1)
                if i >= 2:
                    emit_tp(g - 2)
            emit_av(p * 16 + 15)
            emit_tp(p * 16 + 14)
            emit_tp(p * 16 + 15)
            m_qkdone[p] = s_pe.n
            if p + 1 < NPAIR:
                emit_crows(p + 1, [(s_pe, m_qkdone[p])])

            if p in (7, 15):
                half = p // 8
                wou_base = s_wou.n
                m_woev = [[0] * 8 for _ in range(4)]
                wo_tgt = [0] * 32
                for dq in range(4):
                    for pc2 in range(8):
                        n = dq * 8 + pc2
                        slot = n % 12
                        dchunk = half * 8 + pc2
                        if n == 0:
                            gate = [(s_pe, m_projdone[p])]
                        elif n < 12:
                            gate = ()
                        else:
                            gate = [(s_wou, wou_base + n - 12 + 1)]
                        E("gpsimd", dma(pnl[:, slot * 512:(slot + 1) * 512],
                                        wo_d[dchunk * 128:(dchunk + 1) * 128,
                                             dq * 512:(dq + 1) * 512]),
                          waits=gate, inc=s_wo[slot], amt=16)
                        wo_tgt[n] = s_wo[slot].n
                    for qt in range(8):
                        r = rot_i % 4
                        ps = rot512[r]
                        for pc2 in range(8):
                            slot = (dq * 8 + pc2) % 12
                            w = []
                            if pc2 == 0:
                                w = [rotg[r]]
                                if dq == 0 and qt == 0:
                                    w += [(s_dve, m_afe[p * 16 + 15]),
                                          (s_dve, m_avsc[p * 16 + 15])]
                            if qt == 0:
                                w = list(w) + [(s_wo[slot], wo_tgt[dq * 8 + pc2])]
                            lastmm = (pc2 == 7)
                            if qt == 7:
                                inc_s = s_wou
                            else:
                                inc_s = s_pe if lastmm else None
                            E("tensor", lambda e, pc2=pc2, qt=qt, dq=dq, ps=ps, slot=slot: e.matmul(
                                ps, afm[:, pc2 * NQ + qt * 128: pc2 * NQ + (qt + 1) * 128],
                                pnl[:, slot * 512:(slot + 1) * 512],
                                start=(pc2 == 0), stop=(pc2 == 7)),
                              waits=w,
                              inc=(inc_s if (qt == 7 or lastmm) else None))
                        if qt == 7:
                            sem, val = s_wou, s_wou.n
                        else:
                            sem, val = s_pe, s_pe.n
                        E("vector", lambda e, qt=qt, dq=dq, ps=ps: e.tensor_tensor(
                            acc[:, qt * D + dq * 512: qt * D + (dq + 1) * 512], ps,
                            acc[:, qt * D + dq * 512: qt * D + (dq + 1) * 512],
                            op=OP.add),
                          waits=[(sem, val)], inc=s_dve)
                        m_woev[dq][qt] = s_dve.n
                        rotg[r] = (s_dve, s_dve.n)
                        rot_i += 1
                WO_PE = s_pe.n
                WO_WOU = s_wou.n
                if p == 7:
                    emit_panels(8, [(s_wou, WO_WOU)])
        ATT_PE_DONE = s_pe.n
        ATT_DVE_DONE = s_dve.n
        ATT_ACT_DONE = s_act.n
        if dbg:
            E("gpsimd", dma(dafm_d[:], afm[:]), waits=[(s_dve, m_afe[NIT - 1])],
              inc=s_dbg, amt=16)

    B_FREE = [(s_pe, ATT_PE_DONE), (s_dve, ATT_DVE_DONE), (s_act, ATT_ACT_DONE),
              (s_wou, s_wou.n)]
    if dbg:
        B_FREE = B_FREE + [(s_dbg, s_dbg.n)]

    sAB.close()
    with contextlib.ExitStack() as sC:
        w13 = [sC.enter_context(nc.sbuf_tensor(f"w13_{i}", [128, 2 * 2048], BF16))
               for i in range(2)]
        w2s = [[sC.enter_context(nc.sbuf_tensor(f"w2s_{i}_{pb}", [128, 512], BF16))
                for pb in range(2)] for i in range(NI)]
        hb = sC.enter_context(nc.sbuf_tensor("hb", [128, NI * NQ], BF16))
        silu = [sC.enter_context(nc.sbuf_tensor("silu0", [128, NQ], F32))] * 2

        NEXP = NE + 2
        m_h1 = [[0] * NI for _ in range(NEXP)]
        m_h3 = [[0] * NI for _ in range(NEXP)]
        m_silu = [[0] * NI for _ in range(NEXP)]
        m_mult = [[0] * NI for _ in range(NEXP)]

        def w2src(e_, i, q):
            if e_ < NE:
                return w2_d[e_, i * 128:(i + 1) * 128, q * 512:(q + 1) * 512]
            o = (e_ - NE) * INTER
            return ws2_d[o + i * 128: o + (i + 1) * 128, q * 512:(q + 1) * 512]

        def emit_w13(n, gate):
            par = n % 2
            for s2 in range(4):
                E("gpsimd", dma(w13[par][:, s2 * 1024:(s2 + 1) * 1024],
                                w13h_d[n][:, s2 * 1024:(s2 + 1) * 1024]),
                  waits=(gate if s2 == 0 else ()), inc=s_w13[par], amt=16)

        emit_w13(0, list(B_FREE))
        emit_w13(1, [])
        n13 = 2

        for e_ in range(NEXP):
            for i in range(NI):
                n = e_ * NI + i
                par = n % 2
                hw = [(s_w13[par], (n // 2 + 1) * 64)]
                if n > 0:
                    pe_, pi = divmod(n - 1, NI)
                    hw += [(s_act, m_silu[pe_][pi])]
                else:
                    hw += [(s_act, ATT_ACT_DONE)]
                for tc in range(2):
                    for c in range(ND):
                        E("tensor", lambda e, c=c, tc=tc, par=par: e.matmul(
                            SA[:, tc * 512:(tc + 1) * 512],
                            w13[par][:, c * 128:(c + 1) * 128],
                            xn[:, c * NKV + NQ + tc * 512: c * NKV + NQ + (tc + 1) * 512],
                            start=(c == 0), stop=(c == ND - 1)),
                          waits=(hw if (tc == 0 and c == 0) else ()),
                          inc=(s_pe if (tc == 1 and c == ND - 1) else None))
                m_h1[e_][i] = s_pe.n
                hw3 = []
                if n > 0:
                    pe_, pi = divmod(n - 1, NI)
                    hw3 = [(s_dve, m_mult[pe_][pi])]
                for tc in range(2):
                    for c in range(ND):
                        E("tensor", lambda e, c=c, tc=tc, par=par: e.matmul(
                            SB[:, tc * 512:(tc + 1) * 512],
                            w13[par][:, 2048 + c * 128: 2048 + (c + 1) * 128],
                            xn[:, c * NKV + NQ + tc * 512: c * NKV + NQ + (tc + 1) * 512],
                            start=(c == 0), stop=(c == ND - 1)),
                          waits=(hw3 if (tc == 0 and c == 0) else ()),
                          inc=(s_pe if (tc == 1 and c == ND - 1) else None))
                m_h3[e_][i] = s_pe.n
                while n13 < NEXP * NI and n13 % NI >= 2 and (n13 - 2) <= e_ * NI + i:
                    ge_, gi_ = divmod(n13 - 2, NI)
                    emit_w13(n13, [(s_pe, m_h3[ge_][gi_])])
                    n13 += 1
                sw = [(s_pe, m_h1[e_][i])]
                if n >= 1:
                    pe2, pi2 = divmod(n - 1, NI)
                    sw += [(s_dve, m_mult[pe2][pi2])]
                E("scalar", lambda e, par=par: e.activation(silu[par][:], SA[:, 0:1024],
                                                            AF.Silu),
                  waits=sw, inc=s_act)
                m_silu[e_][i] = s_act.n
                E("vector", lambda e, i=i, par=par: e.tensor_tensor(
                    hb[:, i * NQ:(i + 1) * NQ], SB[:, 0:1024], silu[par][:],
                    op=OP.mult),
                  waits=[(s_act, m_silu[e_][i]), (s_pe, m_h3[e_][i])], inc=s_dve)
                m_mult[e_][i] = s_dve.n

            w2_tgt = [0] * NI
            for q in range(4):
                seq = e_ * 4 + q
                pb = seq % 2
                for i in range(NI):
                    if seq == 0:
                        gate = B_FREE if i == 0 else ()
                    elif seq == 1:
                        gate = ()
                    else:
                        gate = [(s_w2u, (seq - 2) * NI + i + 1)]
                    E("gpsimd", dma(w2s[i][pb][:], w2src(e_, i, q)), waits=gate,
                      inc=s_w2s[i][pb], amt=16)
                    w2_tgt[i] = s_w2s[i][pb].n
                for tt in range(8):
                    r = rot_i % 4
                    ps = rot512[r]
                    for i in range(NI):
                        w = []
                        if i == 0:
                            w = [rotg[r], (s_dve, m_mult[e_][NI - 1])]
                        if tt == 0:
                            w = list(w) + [(s_w2s[i][pb], w2_tgt[i])]
                        if qt_inc := (tt == 7):
                            inc_s = s_w2u
                        else:
                            inc_s = s_pe if i == NI - 1 else None
                        E("tensor", lambda e, i=i, tt=tt, ps=ps, pb=pb: e.matmul(
                            ps, hb[:, i * NQ + tt * 128: i * NQ + (tt + 1) * 128],
                            w2s[i][pb][:], start=(i == 0), stop=(i == NI - 1)),
                          waits=w,
                          inc=(inc_s if (tt == 7 or i == NI - 1) else None))
                    if tt == 7:
                        sem, val = s_w2u, s_w2u.n
                    else:
                        sem, val = s_pe, s_pe.n
                    scal = comb[:, tt * 16 + e_: tt * 16 + e_ + 1] if e_ < NE else 1.0
                    E("vector", lambda e, tt=tt, q=q, ps=ps, scal=scal: e.scalar_tensor_tensor(
                        acc[:, tt * D + q * 512: tt * D + (q + 1) * 512], ps, scal,
                        acc[:, tt * D + q * 512: tt * D + (q + 1) * 512],
                        op0=OP.mult, op1=OP.add),
                      waits=[(sem, val)], inc=s_dve)
                    rotg[r] = (s_dve, s_dve.n)
                    rot_i += 1
            while n13 < NEXP * NI and n13 % NI < 2:
                ge_, gi_ = divmod(n13 - 2, NI)
                emit_w13(n13, [(s_pe, m_h3[ge_][gi_])])
                n13 += 1
        MOE_DVE_DONE = s_dve.n

    if dbg:
        E("gpsimd", dma(dacc_d[:], acc[:]), waits=[(s_dve, MOE_DVE_DONE)],
          inc=s_dbg, amt=16)
        E("gpsimd", None, waits=[(s_dbg, s_dbg.n), (s_dbg2, s_dbg2.n), (s_dbg3, s_dbg3.n)])

    for tt in range(8):
        E("sync", dma(out_d[tt * 128:(tt + 1) * 128, :], acc[:, tt * D:(tt + 1) * D]),
          waits=([(s_dve, MOE_DVE_DONE)] if tt == 0 else ()), inc=s_out, amt=16)
    E("sync", None, waits=[(s_out, s_out.n)])

    kb.replay()
    nc.finalize()
    return nc


_CACHE = {}


def _host_prep(inputs):
    f = np.float32

    def panelize(w):
        wT = np.asarray(w, f).T
        return wT.reshape(16, 128, 16, 128).transpose(2, 1, 0, 3).reshape(16, 128, 2048)

    def blk13(w, ws):
        a = np.asarray(w, f).transpose(0, 2, 1).reshape(NE, 16, 128, NI, 128) \
            .transpose(0, 3, 2, 1, 4).reshape(NE * NI, 128, 2048)
        b = np.asarray(ws, f).T.reshape(16, 128, 2 * NI, 128) \
            .transpose(2, 1, 0, 3).reshape(2 * NI, 128, 2048)
        return np.concatenate([a, b], axis=0)

    wqkv = np.ascontiguousarray(np.concatenate(
        [panelize(inputs["L0_attn_Wq"]), panelize(inputs["L0_attn_Wk"]),
         panelize(inputs["L0_attn_Wv"])], axis=0))
    w13h = np.ascontiguousarray(np.concatenate(
        [blk13(inputs["L0_ffn_W1"], inputs["L0_ffn_Ws1"]),
         blk13(inputs["L0_ffn_W3"], inputs["L0_ffn_Ws3"])], axis=2))
    shared = dict(
        wqkv_p=wqkv,
        w13h=w13h,
        woT=np.ascontiguousarray(inputs["L0_attn_Wo"].T, dtype=f),
        wfT=np.ascontiguousarray(inputs["L0_attn_Wf"].T, dtype=f),
        bf_col=np.ascontiguousarray(np.asarray(inputs["L0_attn_bf"]).reshape(H, 1), dtype=f),
        wgT=np.ascontiguousarray(inputs["L0_ffn_Wg"].T, dtype=f),
        w2T=np.ascontiguousarray(np.asarray(inputs["L0_ffn_W2"]).transpose(0, 2, 1), dtype=f),
        ws2T=np.ascontiguousarray(inputs["L0_ffn_Ws2"].T, dtype=f),
        trium=np.ascontiguousarray(np.tril(np.full((128, 128), BNEG, f), -1).astype(_BF)),
        identb=np.ascontiguousarray(np.eye(128, dtype=_BF)),
        identf=np.ascontiguousarray(np.eye(128, dtype=f)),
        ones_rows=np.ones((3, NKV), dtype=_BF),
    )
    idx = np.arange(2048).reshape(16, 128)
    tp = {}
    for s in (0, 1):
        other = idx[(1 - s)::2].reshape(-1)
        own = idx[s::2].reshape(-1)
        tp[s] = np.concatenate([other, own])
    trit = {s: np.ascontiguousarray(
                (tp[s][:, None] <= tp[s][None, :]).astype(f)
                .reshape(16, 128, 16, 128).transpose(2, 1, 0, 3).reshape(16, 128, 2048))
            for s in (0, 1)}
    odiag = {0: np.full((128, 128), BNEG, dtype=_BF),
             1: np.zeros((128, 128), dtype=_BF)}

    x = np.asarray(inputs["L0_attn_x"])
    in_maps = []
    for c in range(8):
        b, s = divmod(c, 2)
        m = dict(shared)
        m["x_all"] = np.ascontiguousarray(x[b][tp[s]], dtype=f)
        m["trit"] = trit[s]
        m["odiag"] = odiag[s]
        in_maps.append(m)
    return in_maps, tp


def kernel(**inputs):
    if "nc" not in _CACHE:
        _CACHE["nc"] = build_program()
    nc = _CACHE["nc"]
    in_maps, tp = _host_prep(inputs)
    res = run_bass_kernel_spmd(nc, in_maps, core_ids=list(range(8)))
    out = np.empty((4, 2048, 2048), np.float32)
    for c in range(8):
        b, s = divmod(c, 2)
        out[b][tp[s][1024:]] = res.results[c]["out"]
    return out
